# version 13
# baseline (speedup 1.0000x reference)
import contextlib
import numpy as np
import concourse.bass as bass
import concourse.mybir as mybir
from concourse.bass_utils import run_bass_kernel_spmd

F32 = mybir.dt.float32
BF16 = mybir.dt.bfloat16
FP16 = mybir.dt.float16
AF = mybir.ActivationFunctionType
OP = mybir.AluOpType
AX = mybir.AxisListType

D = 2048
H = 8
DV = 256
CH = 128
TOK = 1024
NCH = TOK // CH
GA = 512
CPG = GA // CH
NG = TOK // GA
I8 = 8192
EPS = 1e-5

C23 = float(np.float32(1.5 * 2 ** 23))
S29 = float(2.0 ** -29)
B29 = float(2.0 ** 29)
LN16 = float(np.log(16.0))
CW2 = float(1.0 / np.sqrt(2048.0))
CW8 = float(1.0 / np.sqrt(8192.0))

ENGS = ("sync", "scalar", "vector", "tensor", "gpsimd")
DEBUG_TAPS = []
PHASE_LIMIT = "full"


def hadamard(n):
    h = np.array([[1.0]], np.float32)
    while h.shape[0] < n:
        h = np.block([[h, h], [h, -h]])
    return np.ascontiguousarray(h, np.float32)


def tile_cols(w):
    di, no = w.shape
    t = w.reshape(di // 128, 128, no // 128, 128).transpose(2, 1, 0, 3).reshape(no, di)
    return np.ascontiguousarray(t)


class P:
    def __init__(self, nc):
        self.nc = nc
        self.q = {e: [] for e in ENGS}
        self.sems = {}
        self.cnt = {}
        self._names = []

    def sem(self, name):
        if name not in self.cnt:
            self._names.append(name)
            self.cnt[name] = 0
        return name

    def op(self, eng, fn, inc=None):
        incs = list(inc or [])
        for s, n in incs:
            self.sem(s)
            self.cnt[s] += n

        def run(h, fn=fn, incs=incs):
            instr = fn(h)
            for s, n in incs:
                instr = instr.then_inc(self.sems[s], n)

        self.q[eng].append(run)

    def wait(self, eng, s, val=None):
        self.sem(s)
        v = self.cnt[s] if val is None else val
        if v <= 0:
            return
        self.q[eng].append(lambda h, s=s, v=v: h.wait_ge(self.sems[s], v))

    def val(self, s):
        return self.cnt[self.sem(s)]

    def barrier(self, name, engines=ENGS):
        s = self.sem(name)
        snap = [(n, v) for n, v in self.cnt.items() if v > 0 and n != s]
        for e in engines:
            for n, v in snap:
                self.wait(e, n, v)
            self.op(e, lambda h: h.nop(), [(s, 1)])
        tot = self.cnt[s]
        for e in engines:
            self.wait(e, s, tot)

    def emit(self, stack):
        for n in self._names:
            self.sems[n] = stack.enter_context(self.nc.semaphore(n))
        block = stack.enter_context(self.nc.Block())
        for e in ENGS:
            ops = self.q[e]

            def body(h, ops=ops):
                for f in ops:
                    f(h)

            getattr(block, e)(body)


def build():
    nc = bass.Bass("TRN2", target_bir_lowering=False, debug=False)
    p = P(nc)
    din = lambda name, shape: nc.dram_tensor(name, shape, F32, kind="ExternalInput").ap()

    x_own = din("x_own", [TOK, D])
    x_pre = din("x_pre", [TOK, D])
    wq = din("wq", [1024, D])
    wk = din("wk", [1024, D])
    wkg1 = din("wkg1", [128, 16 * 16])
    wkg2 = din("wkg2", [16, 1024])
    bkg2 = din("bkg2", [128, 8])
    wv = din("wv", [D, D])
    wg = din("wg", [D, D])
    bg = din("bg", [128, 16])
    wo = din("wo", [4 * 128, 16 * 512])
    wgt = din("wgt", [I8, D])
    wut = din("wut", [I8, D])
    wdt = din("wdt", [I8, D])
    h128 = din("h128", [128, 128])
    idf = din("idf", [128, 128])
    maskt = din("maskt", [128, 128])
    onesf = din("onesf", [1, 128])
    onecol = din("onecol", [128, 1])

    y_out = nc.dram_tensor("y", [TOK, D], F32, kind="ExternalOutput").ap()
    tern_g = nc.dram_tensor("tern_g", [64 * 128, D], BF16, kind="Internal").ap()
    tern_u = nc.dram_tensor("tern_u", [64 * 128, D], BF16, kind="Internal").ap()
    tern_d = nc.dram_tensor("tern_d", [64 * 128, D], BF16, kind="Internal").ap()

    top = contextlib.ExitStack()
    RA = lambda name, shape, dt: top.enter_context(nc.sbuf_tensor(name, shape, dt))

    x_res = RA("x_res", [128, NCH, D], BF16)
    S_f = RA("S_f", [128, H * DV], F32)
    S_bf = RA("S_bf", [128, H * DV], BF16)
    tmpS = RA("tmpS", [128, DV], F32)
    onesf_t = RA("onesf_t", [1, 128], F32)
    onecol_t = RA("onecol_t", [128, 1], F32)
    onecol_b = RA("onecol_b", [128, 1], BF16)
    bkgn_c = RA("bkgn_c", [128, 8], F32)
    bg_c = RA("bg_c", [128, 16], F32)
    yA_c = RA("yA_c", [128, 1], F32)
    yNY_c = RA("yNY_c", [128, 1], F32)
    col2_c = RA("col2_c", [128, 1], F32)
    s_sc = RA("s_sc", [1, 4], F32)
    s_stat = RA("s_stat", [1, 4], F32)
    shalf = RA("shalf", [128, 3], F32)
    nshalf = RA("nshalf", [128, 3], F32)
    trash = RA("trash", [128, D], BF16)
    cst_t = RA("cst_t", [128, 4], F32)

    ps = [top.enter_context(nc.psum_tensor(f"ps{i}", [128, 512], F32)) for i in range(7)]
    psT = top.enter_context(nc.psum_tensor("psT", [128, 1024], BF16))

    for s in ("PE", "VE", "AC", "PO", "TR", "LD", "ST"):
        p.sem(s)

    def dma(out, in_, sem="LD"):
        p.op("gpsimd", lambda h, o=out, i=in_: h.dma_start(out=o, in_=i), [(sem, 16)])

    def dmat(out, in_):
        p.op("sync", lambda h, o=out, i=in_: h.dma_start(out=o, in_=i, transpose=True), [("TR", 16)])

    def ve(fn, inc=1):
        p.op("vector", fn, [("VE", inc)])

    def ac(fn, inc=1):
        p.op("scalar", fn, [("AC", inc)])

    def pe(fn, inc=1):
        p.op("tensor", fn, [("PE", inc)])

    bank_guard = {}

    def take_bank(i):
        g = bank_guard.pop(i, None)
        if g:
            for s, v in g:
                p.wait("tensor", s, v)
        return ps[i]

    def done_bank(i, sems):
        bank_guard[i] = [(s, p.val(s)) for s in sems]

    dma(x_res[:], x_own.rearrange("(c p) f -> p c f", p=128))
    dma(onesf_t[:], onesf)
    dma(onecol_t[:], onecol)
    dma(onecol_b[:], onecol)
    dma(bkgn_c[:], bkg2)
    dma(bg_c[:], bg)
    p.wait("scalar", "LD")
    p.wait("vector", "LD")
    p.wait("tensor", "LD")

    _cvals = [EPS, 1.0, C23, -LN16]
    for _i, _v in enumerate(_cvals):
        ve(lambda h, i=_i, v=_v: h.memset(cst_t[:, i:i + 1], v))
        nc.const_aps.register(F32, float(np.float32(_v)), cst_t[:, _i:_i + 1])
        nc.const_aps.register(F32, float(_v), cst_t[:, _i:_i + 1])
    p.wait("scalar", "VE")
    ve(lambda h: h.tensor_scalar(out=bkgn_c[:], in0=bkgn_c[:], scalar1=-1.0, scalar2=None,
                                 op0=OP.mult, op1=OP.bypass))
    ac(lambda h: h.activation(out=yA_c[:], in_=onecol_t[:], func=AF.Copy, scale=S29, bias=C23))
    p.wait("vector", "AC")
    ve(lambda h: h.tensor_scalar(out=yNY_c[:], in0=yA_c[:], scalar1=-C23, scalar2=B29,
                                 op0=OP.add, op1=OP.mult))
    p.wait("vector", "VE")
    ve(lambda h: h.tensor_scalar(out=yA_c[:], in0=yNY_c[:], scalar1=-1.0, scalar2=1.0,
                                 op0=OP.mult, op1=OP.add))
    p.wait("vector", "VE")
    ve(lambda h: h.tensor_copy(out=yNY_c[:], in_=yA_c[:]))
    p.wait("vector", "VE")
    ve(lambda h: h.tensor_scalar(out=col2_c[:], in0=yNY_c[:], scalar1=C23, scalar2=None,
                                 op0=OP.mult, op1=OP.bypass))
    ve(lambda h: h.memset(S_f[:], 0.0))
    ve(lambda h: h.memset(S_bf[:], 0.0))
    p.wait("tensor", "VE")
    p.wait("gpsimd", "VE")

    def ln_stats(src, stats, n_ch, dmodel=D):
        for e in ("vector", "scalar"):
            for s in ("AC", "VE", "TR"):
                p.wait(e, s)
        for ci in range(n_ch):
            ve(lambda h, ci=ci: h.tensor_reduce(out=stats["s1"][:, ci:ci + 1],
                                                in_=src(ci), axis=AX.X, op=OP.add))
        for ci in range(n_ch):
            if ci > 0:
                p.wait("scalar", "AC")
            ac(lambda h, ci=ci: h.activation(out=trash[:, 0:dmodel], in_=src(ci),
                                             func=AF.Square,
                                             accum_out=stats["s2"][:, ci:ci + 1]))
        p.wait("vector", "AC")
        p.wait("vector", "VE")
        ve(lambda h: h.tensor_scalar(out=stats["m"][:, 0:n_ch], in0=stats["s1"][:, 0:n_ch],
                                     scalar1=1.0 / dmodel, scalar2=None,
                                     op0=OP.mult, op1=OP.bypass))
        p.wait("vector", "VE")
        ve(lambda h: h.tensor_tensor(out=stats["ms"][:, 0:n_ch], in0=stats["m"][:, 0:n_ch],
                                     in1=stats["m"][:, 0:n_ch], op=OP.mult))
        p.wait("vector", "VE")
        ve(lambda h: h.tensor_scalar(out=stats["v"][:, 0:n_ch], in0=stats["s2"][:, 0:n_ch],
                                     scalar1=1.0 / dmodel, scalar2=None,
                                     op0=OP.mult, op1=OP.bypass))
        p.wait("vector", "VE")
        ve(lambda h: h.tensor_tensor(out=stats["v"][:, 0:n_ch], in0=stats["v"][:, 0:n_ch],
                                     in1=stats["ms"][:, 0:n_ch], op=OP.subtract))
        p.wait("scalar", "VE")
        ac(lambda h: h.activation(out=stats["r"][:, 0:n_ch], in_=stats["v"][:, 0:n_ch],
                                  func=AF.Ln, bias=EPS))
        p.wait("scalar", "AC")
        ac(lambda h: h.activation(out=stats["r"][:, 0:n_ch], in_=stats["r"][:, 0:n_ch],
                                  func=AF.Exp, scale=-0.5))

    if PHASE_LIMIT == "init":
        p.wait("gpsimd", "VE")
        dma(y_out.rearrange("(c p) f -> p c f", p=128), x_res[:], sem="ST")
        p.wait("gpsimd", "ST", p.val("ST"))
        p.emit(top)
        top.close()
        return nc

    attn_stack = contextlib.ExitStack()
    AA = lambda name, shape, dt: attn_stack.enter_context(nc.sbuf_tensor(name, shape, dt))
    idb_t = AA("idb_t", [128, 128], BF16)
    mask_t = AA("mask_t", [128, 128], BF16)
    wkg1_t = AA("wkg1_t", [128, 16, 16], BF16)
    wkg2_t = AA("wkg2_t", [16, 1024], BF16)
    dma(idb_t[:], idf)
    dma(mask_t[:], maskt)
    dma(wkg1_t[:], wkg1.rearrange("p (j c) -> p j c", j=16))
    dma(wkg2_t[:], wkg2)
    p.wait("tensor", "LD")
    p.wait("vector", "LD")
    k_fm = AA("k_fm", [128, H, GA], BF16)
    gkln = AA("gkln", [128, H, GA], BF16)
    gkc_c = AA("gkc_c", [128, H, CH], F32)
    pool4 = AA("pool4", [128, 2, 512], BF16)
    e1_c = AA("e1_c", [128, H, CH], BF16)
    e3_c = AA("e3_c", [128, H, CH], BF16)
    dl_c = AA("dl_c", [128, CPG * H], F32)
    v_tm = AA("v_tm", [128, CPG, D], BF16)
    qd_t = AA("qd_t", [128, H, CH], BF16)
    ke3_t = AA("ke3_t", [128, H, CH], BF16)
    ke3m = AA("ke3m", [128, H, CH], BF16)
    amT = AA("amT", [128, H, CH], BF16)
    o_bf = AA("o_bf", [128, D], BF16)
    o_fm = AA("o_fm", [128, D], BF16)
    st4 = {n: AA("st4_" + n, [128, CPG], F32) for n in ("s1", "s2", "m", "ms", "v", "r")}
    st8 = {n: AA("st8_" + n, [128, H], F32) for n in ("s1", "s2", "m", "ms", "v", "r")}
    xlnT = AA("xlnT", [128, 16, GA], BF16)
    xln_b = AA("xln_b", [128, 2, D], BF16)
    wbA = [AA(f"wbA{i}", [128, 16, 128], BF16) for i in range(3)]
    t1_sb = AA("t1_sb", [16, GA], BF16)
    sw_b = [AA(f"sw{i}", [128, 2048], BF16) for i in range(2)]
    o_bfB = AA("o_bfB", [128, D], BF16)
    scol = AA("scol", [128, 192], F32)
    sred = AA("sred", [128, 4], F32)
    own_stack = contextlib.ExitStack()
    OA = lambda name, shape, dt: own_stack.enter_context(nc.sbuf_tensor(name, shape, dt))

    for i in range(2):
        p.sem(f"SWL{i}")
    SUNIT = 128 * 2048
    s_srcs = [wgt.rearrange("a b -> (a b)"), wut.rearrange("a b -> (a b)"),
              wdt.rearrange("a b -> (a b)")]
    NPM = (D * I8) // SUNIT
    s_plan = [(m, off) for m in range(3) for off in range(0, D * I8, SUNIT)]
    s_ld = [0]
    s_ab = [0]
    swl_v = [None] * len(s_plan)
    ac_val = [None] * len(s_plan)

    def s_finalize(m):
        p.wait("vector", "AC")
        ve(lambda h, m=m: h.tensor_reduce(out=sred[:, m:m + 1],
                                          in_=scol[:, m * NPM:(m + 1) * NPM],
                                          axis=AX.X, op=OP.add))
        p.wait("tensor", "VE")
        b = take_bank(6)
        pe(lambda h, m=m, b=b: h.matmul(out=b[0:1, 0:1], lhsT=sred[:, m:m + 1],
                                        rhs=onecol_t[:], start=True, stop=True))
        p.wait("vector", "PE")
        ve(lambda h, m=m, b=b: h.tensor_scalar(out=s_sc[0:1, m:m + 1], in0=b[0:1, 0:1],
                                               scalar1=1.0 / (D * I8), scalar2=EPS,
                                               op0=OP.mult, op1=OP.max))
        done_bank(6, ["VE"])
        p.wait("tensor", "VE")
        b = take_bank(6)
        pe(lambda h, m=m, b=b: h.matmul(out=b[0:128, 0:1], lhsT=onesf_t[:],
                                        rhs=s_sc[0:1, m:m + 1], start=True, stop=True))
        p.wait("vector", "PE")
        ve(lambda h, m=m, b=b: h.tensor_scalar(out=shalf[:, m:m + 1], in0=b[0:128, 0:1],
                                               scalar1=0.5, scalar2=None,
                                               op0=OP.mult, op1=OP.bypass))
        ve(lambda h, m=m, b=b: h.tensor_scalar(out=nshalf[:, m:m + 1], in0=b[0:128, 0:1],
                                               scalar1=-0.5, scalar2=None,
                                               op0=OP.mult, op1=OP.bypass))
        done_bank(6, ["VE"])

    def s_emit_load(i):
        m, off = s_plan[i]
        slot = i % 2
        if i >= 2:
            p.wait("gpsimd", "AC", ac_val[i - 2])
        dma(sw_b[slot][:], s_srcs[m][off:off + SUNIT].rearrange("(p n) -> p n", p=128),
            sem=f"SWL{slot}")
        swl_v[i] = p.val(f"SWL{slot}")

    def s_emit_abs(i):
        m, off = s_plan[i]
        slot = i % 2
        u = i - m * NPM
        p.wait("scalar", f"SWL{slot}", swl_v[i])
        ac(lambda h, sl=slot, c=m * NPM + u: h.activation(
            out=sw_b[sl][:], in_=sw_b[sl][:], func=AF.Abs,
            accum_out=scol[:, c:c + 1]))
        ac_val[i] = p.val("AC")
        if u == NPM - 1:
            s_finalize(m)

    def s_step(k):
        for _ in range(k):
            n = len(s_plan)
            if s_ld[0] < n:
                s_emit_load(s_ld[0])
                s_ld[0] += 1
            lag = 1 if s_ld[0] < n else 0
            if s_ab[0] < s_ld[0] - lag:
                s_emit_abs(s_ab[0])
                s_ab[0] += 1

    for i in range(3):
        p.sem(f"WBL{i}")
    wb_n = [0]
    wb_free = [None, None, None]

    def load_wcol(src_col):
        n = wb_n[0]; wb_n[0] += 1
        slot = n % 3
        if wb_free[slot] is not None:
            p.wait("gpsimd", wb_free[slot][0], wb_free[slot][1])
        dma(wbA[slot][:], src_col.rearrange("p (j c) -> p j c", j=16), sem=f"WBL{slot}")
        return slot, p.val(f"WBL{slot}")

    def project(w_src, n_oc, evac):
        pend = [load_wcol(w_src[i * 128:(i + 1) * 128, :])
                for i in range(min(3, n_oc))]
        for oc in range(n_oc):
            slot, wblv = pend.pop(0)
            p.wait("tensor", f"WBL{slot}", wblv)
            bi = oc % 4
            bank = take_bank(bi)
            for j in range(16):
                pe(lambda h, sl=slot, j=j, b=bank, st=(j == 0): h.matmul(
                    out=b[:, 0:512], lhsT=wbA[sl][:, j, :], rhs=xlnT[:, j, :],
                    start=st, stop=(j == 15)))
            wb_free[slot] = ("PE", p.val("PE"))
            if oc + 3 < n_oc:
                pend.append(load_wcol(w_src[(oc + 3) * 128:(oc + 4) * 128, :]))
            evac(oc, bi, bank)
            s_step(1)

    def attn_group(gi, own, q_fm=None, gscr=None, gsilu=None, P_t=None, attn_o=None,
                   xpre_b=None, upto="full"):
        p.barrier(f"AG{int(own)}_{gi}")
        if own:
            src = lambda ci: x_res[:, gi * CPG + ci, :]
        else:
            for half in range(2):
                dma(xpre_b[:, half, :, :],
                    x_pre.rearrange("(c p) f -> p c f", p=128)
                    [:, gi * CPG + half * 2:gi * CPG + half * 2 + 2, :])
            p.wait("vector", "LD")
            p.wait("scalar", "LD")
            src = lambda ci: xpre_b[:, ci // 2, ci % 2, :]

        ln_stats(src, st4, CPG)
        p.wait("vector", "AC")
        for ci in range(CPG):
            sl = ci % 2
            if ci >= 2:
                p.wait("vector", "TR", p.val("TR"))
            ve(lambda h, ci=ci, sl=sl: h.tensor_scalar(
                out=xln_b[:, sl, :], in0=src(ci),
                scalar1=st4["m"][:, ci:ci + 1], scalar2=st4["r"][:, ci:ci + 1],
                op0=OP.subtract, op1=OP.mult))
            p.wait("sync", "VE")
            dmat(xlnT[:, :, ci * CH:(ci + 1) * CH], xln_b[:, sl, :])
        p.wait("tensor", "TR")

        b4 = take_bank(4)
        for j in range(16):
            pe(lambda h, j=j, b=b4, st=(j == 0): h.matmul(
                out=b[0:16, 0:512], lhsT=wkg1_t[:, j, :], rhs=xlnT[:, j, :],
                start=st, stop=(j == 15)))
        p.wait("vector", "PE")
        ve(lambda h, b=b4: h.tensor_copy(out=t1_sb[:], in_=b[0:16, 0:512]))
        done_bank(4, ["VE"])
        p.wait("tensor", "VE")
        for hh in range(H):
            bi = hh % 2
            bank = take_bank(bi)
            pe(lambda h, hh=hh, b=bank: h.matmul(
                out=b[:, 0:512], lhsT=wkg2_t[0:16, hh * 128:(hh + 1) * 128],
                rhs=t1_sb[:], start=True, stop=True))
            sl = hh % 2
            p.wait("scalar", "PE")
            if hh >= 2:
                p.wait("scalar", "AC", p.val("AC"))
            ac(lambda h, hh=hh, b=bank, sl=sl: h.activation(
                out=pool4[:, sl, :], in_=b[:, 0:512], func=AF.Exp, scale=-1.0,
                bias=bkgn_c[:, hh:hh + 1]))
            p.wait("scalar", "AC")
            ac(lambda h, hh=hh, sl=sl: h.activation(
                out=gkln[:, hh, :], in_=pool4[:, sl, :], func=AF.Ln, bias=1.0))
            done_bank(bi, ["AC"])

        def evac_k(oc, bi, bank):
            p.wait("scalar", "PE")
            ac(lambda h, oc=oc, b=bank: h.activation(out=k_fm[:, oc, :], in_=b[:, 0:512],
                                                     func=AF.Copy))
            done_bank(bi, ["AC"])

        project(wk, 8, evac_k)

        def evac_v(oc, bi, bank):
            sl = oc % 2
            p.wait("vector", "PE")
            if oc < 2:
                p.wait("vector", "AC")
            if oc >= 2:
                p.wait("vector", "TR", p.val("TR"))
            ve(lambda h, b=bank, sl=sl: h.tensor_copy(out=pool4[:, sl, :], in_=b[:, 0:512]))
            done_bank(bi, ["VE"])
            p.wait("sync", "VE")
            dmat(v_tm[:, :, oc * 128:(oc + 1) * 128], pool4[:, sl, :])

        project(wv, 16, evac_v)

        if own:
            def evac_q(oc, bi, bank):
                p.wait("scalar", "PE")
                ac(lambda h, oc=oc, b=bank: h.activation(out=q_fm[:, oc, :], in_=b[:, 0:512],
                                                         func=AF.Copy))
                done_bank(bi, ["AC"])

            project(wq, 8, evac_q)

            def evac_g(oc, bi, bank):
                sl = oc % 2
                p.wait("scalar", "PE")
                if oc >= 2:
                    p.wait("scalar", "VE", p.val("VE"))
                p.wait("vector", "PE")
                ve(lambda h, oc=oc, b=bank, sl=sl: h.tensor_scalar(
                    out=gscr[:, sl, :], in0=b[:, 0:512],
                    scalar1=bg_c[:, oc:oc + 1], scalar2=None, op0=OP.add, op1=OP.bypass))
                p.wait("scalar", "VE")
                ac(lambda h, oc=oc, b=bank, sl=sl: h.activation(
                    out=gscr[:, 2 + sl, :], in_=b[:, 0:512], func=AF.Sigmoid,
                    bias=bg_c[:, oc:oc + 1]))
                done_bank(bi, ["AC", "VE"])
                p.wait("vector", "AC")
                p.wait("vector", "VE")
                ve(lambda h, oc=oc, sl=sl: h.tensor_tensor(
                    out=gsilu[:, oc, :], in0=gscr[:, sl, :], in1=gscr[:, 2 + sl, :],
                    op=OP.mult))

            project(wg, 16, evac_g)
        if upto == "proj":
            return

        p.wait("tensor", "TR")
        o_bf2 = [o_bf, o_bfB]
        prep_ve = [None] * CPG
        prep_ac = [None] * CPG
        body_pe = [None] * CPG
        body_ac = [None] * CPG
        tail_tr = [None] * CPG
        tail_ve = [None] * CPG

        def chunk_prep(ci):
            if ci > 0:
                p.wait("vector", "AC", prep_ac[ci - 1])
            else:
                p.wait("vector", "AC")
                p.wait("vector", "PE")
            for hh in range(H):
                ve(lambda h, hh=hh, ci=ci: h.tensor_tensor_scan(
                    out=gkc_c[:, hh, :], data0=gkln[:, hh, ci * CH:(ci + 1) * CH],
                    data1=gkln[:, hh, ci * CH:(ci + 1) * CH], initial=0.0,
                    op0=OP.add, op1=OP.bypass))
            p.wait("scalar", "VE")
            ac(lambda h: h.activation(out=e1_c[:].rearrange("p h c -> p (h c)"),
                                      in_=gkc_c[:].rearrange("p h c -> p (h c)"),
                                      func=AF.Exp, scale=-1.0 / 16.0))
            ac(lambda h: h.activation(out=e3_c[:].rearrange("p h c -> p (h c)"),
                                      in_=gkc_c[:].rearrange("p h c -> p (h c)"),
                                      func=AF.Exp, scale=1.0 / 16.0, bias=-LN16))
            ac(lambda h, ci=ci: h.activation(
                out=dl_c[:, ci * H:(ci + 1) * H],
                in_=gkc_c[:, :, 127:128].rearrange("p h c -> p (h c)"),
                func=AF.Exp, scale=-1.0 / 16.0))
            prep_ac[ci] = p.val("AC")
            p.wait("vector", "AC", prep_ac[ci])
            if ci > 0:
                p.wait("vector", "PE", body_pe[ci - 1])
            if own:
                ve(lambda h, ci=ci: h.tensor_tensor(
                    out=qd_t[:], in0=q_fm[:, :, ci * CH:(ci + 1) * CH],
                    in1=e1_c[:], op=OP.mult))
            ve(lambda h, ci=ci: h.tensor_tensor(
                out=ke3_t[:], in0=k_fm[:, :, ci * CH:(ci + 1) * CH],
                in1=e3_c[:], op=OP.mult))
            prep_ve[ci] = p.val("VE")

        def chunk_body(ci):
            ob = o_bf2[ci % 2]
            p.wait("tensor", "VE", prep_ve[ci])
            cp_ve = [None] * H
            for hh in range(H):
                half = hh % 2
                if hh >= 1:
                    p.wait("tensor", "VE", cp_ve[hh - 1])
                pe(lambda h, hh=hh, half=half: h.transpose(
                    out=psT[:, half * 128:(half + 1) * 128], in_=ke3_t[:, hh, :],
                    identity=idb_t[:]))
                p.wait("vector", "PE")
                ve(lambda h, hh=hh, half=half: h.tensor_copy(
                    out=ke3m[:, hh, :], in_=psT[:, half * 128:(half + 1) * 128]))
                cp_ve[hh] = p.val("VE")
            for hh in range(H):
                bi = 4 + (hh % 2)
                if own:
                    bank = take_bank(bi)
                    pe(lambda h, hh=hh, b=bank: h.matmul(
                        out=b[:, 0:128], lhsT=ke3_t[:, hh, :], rhs=qd_t[:, hh, :],
                        start=True, stop=True))
                    p.wait("vector", "PE")
                    ve(lambda h, hh=hh, b=bank: h.tensor_tensor(
                        out=amT[:, hh, :], in0=b[:, 0:128], in1=mask_t[:], op=OP.mult))
                    amt_ve = p.val("VE")
                    done_bank(bi, ["VE"])
                    bank = take_bank(bi)
                    p.wait("tensor", "VE", amt_ve)
                    pe(lambda h, hh=hh, ci=ci, b=bank: h.matmul(
                        out=b[:, 0:256], lhsT=amT[:, hh, :],
                        rhs=v_tm[:, ci, hh * 256:(hh + 1) * 256], start=True, stop=False))
                    pe(lambda h, hh=hh, b=bank: h.matmul(
                        out=b[:, 0:256], lhsT=qd_t[:, hh, :],
                        rhs=S_bf[:, hh * 256:(hh + 1) * 256], start=False, stop=True))
                    p.wait("scalar", "PE")
                    if hh == 0 and ci >= 2:
                        p.wait("scalar", "TR", tail_tr[ci - 2])
                        p.wait("scalar", "VE", tail_ve[ci - 2])
                    ac(lambda h, hh=hh, b=bank, ob=ob: h.activation(
                        out=ob[:, hh * 256:(hh + 1) * 256], in_=b[:, 0:256],
                        func=AF.Copy))
                    done_bank(bi, ["AC"])
                bank = take_bank(bi)
                pe(lambda h, hh=hh, ci=ci, b=bank: h.matmul(
                    out=b[:, 0:256], lhsT=ke3m[:, hh, :],
                    rhs=v_tm[:, ci, hh * 256:(hh + 1) * 256], start=True, stop=True))
                p.wait("vector", "PE")
                ve(lambda h, hh=hh, b=bank: h.tensor_tensor(
                    out=tmpS[:], in0=S_f[:, hh * 256:(hh + 1) * 256], in1=b[:, 0:256],
                    op=OP.add))
                done_bank(bi, ["VE"])
                p.wait("vector", "VE")
                ve(lambda h, hh=hh, ci=ci: h.tensor_scalar(
                    out=S_f[:, hh * 256:(hh + 1) * 256], in0=tmpS[:],
                    scalar1=dl_c[:, ci * H + hh:ci * H + hh + 1], scalar2=None,
                    op0=OP.mult, op1=OP.bypass))
                p.wait("vector", "VE")
                ve(lambda h, hh=hh: h.tensor_copy(
                    out=S_bf[:, hh * 256:(hh + 1) * 256],
                    in_=S_f[:, hh * 256:(hh + 1) * 256]))
            body_pe[ci] = p.val("PE")
            body_ac[ci] = p.val("AC")

        def chunk_tail(ci):
            ob = o_bf2[ci % 2]
            p.wait("vector", "AC", body_ac[ci])
            ve(lambda h, ob=ob: h.tensor_reduce(
                out=st8["s1"][:], in_=ob[:].rearrange("p (h w) -> p h w", h=H),
                axis=AX.X, op=OP.add))
            for hh in range(H):
                p.wait("scalar", "AC")
                ac(lambda h, hh=hh, ob=ob: h.activation(
                    out=trash[:, 0:256], in_=ob[:, hh * 256:(hh + 1) * 256],
                    func=AF.Square, accum_out=st8["s2"][:, hh:hh + 1]))
            p.wait("vector", "AC")
            p.wait("vector", "VE")
            ve(lambda h: h.tensor_scalar(out=st8["m"][:], in0=st8["s1"][:],
                                         scalar1=1.0 / DV, scalar2=None,
                                         op0=OP.mult, op1=OP.bypass))
            p.wait("vector", "VE")
            ve(lambda h: h.tensor_tensor(out=st8["ms"][:], in0=st8["m"][:],
                                         in1=st8["m"][:], op=OP.mult))
            p.wait("vector", "VE")
            ve(lambda h: h.tensor_scalar(out=st8["v"][:], in0=st8["s2"][:],
                                         scalar1=1.0 / DV, scalar2=None,
                                         op0=OP.mult, op1=OP.bypass))
            p.wait("vector", "VE")
            ve(lambda h: h.tensor_tensor(out=st8["v"][:], in0=st8["v"][:],
                                         in1=st8["ms"][:], op=OP.subtract))
            p.wait("scalar", "VE")
            ac(lambda h: h.activation(out=st8["r"][:], in_=st8["v"][:],
                                      func=AF.Ln, bias=EPS))
            p.wait("scalar", "AC")
            ac(lambda h: h.activation(out=st8["r"][:], in_=st8["r"][:],
                                      func=AF.Exp, scale=-0.5))
            p.wait("vector", "AC")
            for hh in range(H):
                ve(lambda h, hh=hh, ob=ob: h.tensor_scalar(
                    out=ob[:, hh * 256:(hh + 1) * 256],
                    in0=ob[:, hh * 256:(hh + 1) * 256],
                    scalar1=st8["m"][:, hh:hh + 1], scalar2=st8["r"][:, hh:hh + 1],
                    op0=OP.subtract, op1=OP.mult))
            p.wait("sync", "VE")
            p.wait("sync", "TR", p.val("TR"))
            dmat(o_fm[:].rearrange("p (j r) -> p j r", j=16), ob[:])
            tail_tr[ci] = p.val("TR")
            p.wait("vector", "TR")
            ve(lambda h, ci=ci: h.tensor_tensor(
                out=P_t[:, :, ci * CH:(ci + 1) * CH],
                in0=o_fm[:].rearrange("p (j r) -> p j r", j=16),
                in1=gsilu[:, :, ci * CH:(ci + 1) * CH], op=OP.mult))
            tail_ve[ci] = p.val("VE")

        chunk_prep(0)
        for ci in range(CPG):
            chunk_body(ci)
            if ci + 1 < CPG:
                chunk_prep(ci + 1)
            if own:
                chunk_tail(ci)
            s_step(2)

        if not own or upto == "chunks":
            return

        p.sem("WOL0"); p.sem("WOL1")
        p.wait("tensor", "VE")
        wo_bufs = [xlnT, gsilu]
        wo_pe = [None] * 4
        wov = [None] * 4

        def wo_load(b):
            buf = wo_bufs[b % 2]
            if b < 2:
                p.wait("gpsimd", "PE", p.val("PE"))
                p.wait("gpsimd", "TR", p.val("TR"))
                if b == 1:
                    p.wait("gpsimd", "VE", p.val("VE"))
            else:
                p.wait("gpsimd", "PE", wo_pe[b - 2])
            dma(buf[:], wo[b * 128:(b + 1) * 128, :].rearrange("p (j c) -> p j c", j=16),
                sem=f"WOL{b % 2}")
            wov[b] = p.val(f"WOL{b % 2}")

        wo_load(0)
        wo_load(1)
        for b in range(4):
            buf = wo_bufs[b % 2]
            p.wait("tensor", f"WOL{b % 2}", wov[b])
            for ci in range(CPG):
                bi = ci % 4
                bank = take_bank(bi)
                for j in range(16):
                    pe(lambda h, j=j, ci=ci, b2=bank, bf=buf, st=(j == 0): h.matmul(
                        out=b2[:, 0:512], lhsT=P_t[:, j, ci * CH:(ci + 1) * CH],
                        rhs=bf[:, j, :], start=st, stop=(j == 15)))
                p.wait("scalar", "PE")
                ac(lambda h, ci=ci, b=b, b2=bank: h.activation(
                    out=attn_o[:, ci, b * 512:(b + 1) * 512], in_=b2[:, 0:512],
                    func=AF.Copy))
                done_bank(bi, ["AC"])
            wo_pe[b] = p.val("PE")
            if b + 2 < 4:
                wo_load(b + 2)
            s_step(2)
        p.wait("vector", "AC")
        ln_stats(lambda ci: attn_o[:, ci, :], st4, CPG)
        p.wait("vector", "AC")
        for ci in range(CPG):
            p.wait("vector", "VE")
            ve(lambda h, ci=ci: h.tensor_scalar(
                out=xln_b[:, ci % 2, :], in0=attn_o[:, ci, :],
                scalar1=st4["m"][:, ci:ci + 1], scalar2=st4["r"][:, ci:ci + 1],
                op0=OP.subtract, op1=OP.mult))
            p.wait("vector", "VE")
            ve(lambda h, ci=ci, gi=gi: h.tensor_tensor(
                out=x_res[:, gi * CPG + ci, :], in0=x_res[:, gi * CPG + ci, :],
                in1=xln_b[:, ci % 2, :], op=OP.add))

    with contextlib.ExitStack() as pre_stack:
        xpre_b = pre_stack.enter_context(nc.sbuf_tensor("xpre_b", [128, 2, 2, D], BF16))
        for gi in range(NG):
            attn_group(gi, own=False, xpre_b=xpre_b)
        p.barrier("PREDONE")
    if PHASE_LIMIT == "prefix":
        p.wait("gpsimd", "VE")
        dma(y_out.rearrange("(c p) f -> p c f", p=128), x_res[:], sem="ST")
        p.wait("gpsimd", "ST", p.val("ST"))
        own_stack.close()
        attn_stack.close()
        p.emit(top)
        top.close()
        return nc
    q_fm = OA("q_fm", [128, H, GA], BF16)
    gscr = OA("gscr", [128, 4, 512], BF16)
    gsilu = OA("gsilu", [128, 16, GA], BF16)
    P_t = OA("P_t", [128, 16, GA], BF16)
    attn_o = v_tm
    _upto = {"ownproj": "proj", "ownchunks": "chunks"}.get(PHASE_LIMIT, "full")
    for gi in range(1 if PHASE_LIMIT in ("own1", "ownproj", "ownchunks") else NG):
        attn_group(gi, own=True, q_fm=q_fm, gscr=gscr, gsilu=gsilu, P_t=P_t, attn_o=attn_o,
                   upto=_upto)
    if PHASE_LIMIT in ("own1", "ownproj", "ownchunks"):
        p.wait("gpsimd", "VE")
        dma(y_out.rearrange("(c p) f -> p c f", p=128), x_res[:], sem="ST")
        p.wait("gpsimd", "ST", p.val("ST"))
        own_stack.close()
        attn_stack.close()
        p.emit(top)
        top.close()
        return nc
    while s_ab[0] < len(s_plan):
        s_step(4)
    own_stack.close()
    attn_stack.close()
    p.barrier("ATTNDONE")
    if "xattn" in DEBUG_TAPS:
        tap_xattn = nc.dram_tensor("tap_xattn", [TOK, D], F32, kind="ExternalOutput").ap()
        dma(tap_xattn.rearrange("(c p) f -> p c f", p=128), x_res[:], sem="ST")
    if PHASE_LIMIT == "attn":
        p.wait("gpsimd", "VE")
        dma(y_out.rearrange("(c p) f -> p c f", p=128), x_res[:], sem="ST")
        p.wait("gpsimd", "ST", p.val("ST"))
        p.emit(top)
        top.close()
        return nc

    ffn_stack = contextlib.ExitStack()
    FA = lambda name, shape, dt: ffn_stack.enter_context(nc.sbuf_tensor(name, shape, dt))
    h128_t = FA("h128_t", [128, 128], BF16)
    idf_t = FA("idf_t", [128, 128], F32)
    dma(h128_t[:], h128)
    dma(idf_t[:], idf)
    p.wait("tensor", "LD")
    p.wait("vector", "LD")
    P2 = FA("P2", [128, 64, 512], FP16)
    bsp = FA("bsp", [128, 8, 512], FP16)
    rows = FA("rows", [1, 2, 512], F32)
    scale_bc = FA("scale_bc", [128, 512], FP16)
    bc_g = FA("bc_g", [128, 512], FP16)
    bc_u = FA("bc_u", [128, 512], FP16)
    wfb = [FA(f"wfb{i}", [128, 16, 128], F32) for i in range(3)]
    tern_s = [FA(f"tern_s{i}", [128, 16, 128], BF16) for i in range(2)]
    aq_s = FA("aq_s", [128, 2048], BF16)
    aq_s2 = FA("aq_s2", [128, 2048], BF16)
    aq2 = [aq_s, aq_s2]
    dfm = [FA(f"dfm{i}", [128, 512], FP16) for i in range(2)]
    tm_buf = FA("tm_buf", [128, CPG, D], FP16)
    y16 = tm_buf[:].rearrange("p c (j f) -> p (c j) f", j=4)
    fst = {n: FA("fst_" + n, [128, CPG], F32) for n in ("s1", "s2", "m", "ms", "v", "r")}
    amax4 = fst["ms"]
    g_scr = [FA(f"g_scr{i}", [128, 512], FP16) for i in range(2)]
    gs_scr = [FA("gs_scr0", [128, 512], FP16)] * 2
    up_scr = [FA(f"up_scr{i}", [128, 512], FP16) for i in range(2)]
    sg_scr = [FA("sg_scr0", [128, 512], FP16)] * 2

    for i in range(3):
        p.sem(f"WFL{i}")
    for i in range(2):
        p.sem(f"STC{i}")
    p.sem("PO")

    def w_quant(ws, ts, aslot, m, wflv, aq_guard, tern_guards):
        p.wait("scalar", f"WFL{ws}", wflv)
        for sm, vv in tern_guards:
            p.wait("scalar", sm, vv)
        if aq_guard is not None:
            p.wait("scalar", "VE", aq_guard)
        ac(lambda h, ws=ws, ts=ts, m=m: h.activation(
            out=tern_s[ts][:].rearrange("p j c -> p (j c)"),
            in_=wfb[ws][:].rearrange("p j c -> p (j c)"),
            func=AF.Sign, bias=nshalf[:, m:m + 1]))
        ac(lambda h, ws=ws, a=aslot, m=m: h.activation(
            out=aq2[a][:, 0:2048],
            in_=wfb[ws][:].rearrange("p j c -> p (j c)"),
            func=AF.Sign, bias=shalf[:, m:m + 1]))
        acv = p.val("AC")
        p.wait("vector", "AC", acv)
        ve(lambda h, ts=ts, a=aslot: h.tensor_tensor(
            out=tern_s[ts][:].rearrange("p j c -> p (j c)"),
            in0=tern_s[ts][:].rearrange("p j c -> p (j c)"),
            in1=aq2[a][:, 0:2048], op=OP.add))
        return acv, p.val("VE")

    def fwht_stage2(tiles, nbits, wave=8):
        p.wait("vector", "VE")
        p.wait("vector", "PE")
        p.wait("vector", "AC")
        p.wait("gpsimd", "VE")
        p.wait("gpsimd", "PE")
        p.wait("gpsimd", "AC")
        n = len(tiles) - 8
        idx = list(range(n))
        spares = list(range(n, n + 8))
        for bit in range(nbits):
            pairs = [(a, a | (1 << bit)) for a in range(n) if not (a & (1 << bit))]
            for w0 in range(0, len(pairs), wave):
                wv = pairs[w0:w0 + wave]
                npool = 2 if len(wv) >= 8 else 0
                for wi, (a, b) in enumerate(wv):
                    fn = lambda h, a=idx[a], b=idx[b], s=spares[wi]: h.tensor_tensor(
                        out=tiles[s], in0=tiles[a], in1=tiles[b], op=OP.add)
                    if wi < npool:
                        p.op("gpsimd", fn, [("PO", 1)])
                    else:
                        ve(fn)
                p.wait("vector", "VE")
                p.wait("gpsimd", "PO")
                for wi, (a, b) in enumerate(wv):
                    fn = lambda h, a=idx[a], b=idx[b]: h.tensor_tensor(
                        out=tiles[b], in0=tiles[a], in1=tiles[b], op=OP.subtract)
                    if wi < npool:
                        p.op("gpsimd", fn, [("PO", 1)])
                    else:
                        ve(fn)
                p.wait("vector", "VE")
                p.wait("vector", "PO")
                p.wait("gpsimd", "PO")
                p.wait("gpsimd", "VE")
                for wi, (a, b) in enumerate(wv):
                    freed = idx[a]
                    idx[a] = spares[wi]
                    spares[wi] = freed
        return idx, spares

    def absmax_scale(tiles, idx, spares, n, cnorm, rowslot):
        p.wait("vector", "PO")
        nm = 8
        for i in range(nm):
            ve(lambda h, a=idx[i], s=spares[i]: h.tensor_scalar(
                out=tiles[s], in0=tiles[a], scalar1=0.0, scalar2=None,
                op0=OP.min, op1=OP.bypass))
        p.wait("vector", "VE")
        for i in range(nm):
            ve(lambda h, a=idx[i], s=spares[i]: h.scalar_tensor_tensor(
                out=tiles[s], in0=tiles[a], scalar=0.0, in1=tiles[s],
                op0=OP.max, op1=OP.subtract))
        p.wait("vector", "VE")
        pos = nm
        while pos < n:
            for i in range(nm):
                if pos + i < n:
                    ve(lambda h, s=spares[i], a=idx[pos + i]: h.scalar_tensor_tensor(
                        out=tiles[s], in0=tiles[a], scalar=-1.0, in1=tiles[s],
                        op0=OP.mult, op1=OP.max))
            p.wait("vector", "VE")
            for i in range(nm):
                if pos + i < n:
                    ve(lambda h, s=spares[i], a=idx[pos + i]: h.scalar_tensor_tensor(
                        out=tiles[s], in0=tiles[a], scalar=1.0, in1=tiles[s],
                        op0=OP.mult, op1=OP.max))
            p.wait("vector", "VE")
            pos += nm
        m = nm
        while m > 1:
            for i in range(m // 2):
                ve(lambda h, s=spares[i], a=spares[i + m // 2]: h.tensor_tensor(
                    out=tiles[s], in0=tiles[s], in1=tiles[a], op=OP.max))
            p.wait("vector", "VE")
            m //= 2
        p.wait("sync", "VE")
        p.wait("sync", "TR", p.val("TR"))
        mt_v = tiles[spares[1]].rearrange("p (j r) -> p j r", j=4)
        dmat(mt_v, tiles[spares[0]])
        p.wait("vector", "TR")
        ve(lambda h, mt_v=mt_v: h.tensor_reduce(out=amax4[:], in_=mt_v, axis=AX.X, op=OP.max))
        for c in range(CPG):
            b = take_bank(6)
            p.wait("tensor", "VE")
            pe(lambda h, c=c, b=b: h.transpose(out=b[0:1, 0:128], in_=amax4[:, c:c + 1],
                                               identity=idf_t[:]))
            p.wait("vector", "PE")
            ve(lambda h, c=c, b=b, rs=rowslot: h.tensor_scalar(
                out=rows[0:1, rs, c * 128:(c + 1) * 128], in0=b[0:1, 0:128],
                scalar1=cnorm, scalar2=EPS, op0=OP.mult, op1=OP.max))
            done_bank(6, ["VE"])
        p.wait("vector", "VE")
        ve(lambda h, rs=rowslot: h.reciprocal(out=srow_rec[0:1, :], in_=rows[0:1, rs, :]))
        p.wait("vector", "VE")
        ve(lambda h, cn=cnorm: h.tensor_scalar(out=srow_rec[0:1, :], in0=srow_rec[0:1, :],
                                               scalar1=127.0 * cn, scalar2=None,
                                               op0=OP.mult, op1=OP.bypass))
        p.wait("tensor", "VE")
        b = take_bank(6)
        pe(lambda h, b=b: h.matmul(out=b[:, 0:512], lhsT=onesf_t[:], rhs=srow_rec[0:1, :],
                                   start=True, stop=True))
        p.wait("vector", "PE")
        ve(lambda h, b=b: h.tensor_copy(out=scale_bc[:], in_=b[:, 0:512]))
        done_bank(6, ["VE"])

    srow_rec = rows[:, 1, :]
    qsc = {
        "t": [FA(f"qt{i}", [128, 512], F32) for i in range(2)],
        "p": [FA(f"qp{i}", [128, 512], BF16) for i in range(2)],
        "a": [FA(f"qa{i}", [128, 512], BF16) for i in range(2)],
        "v": [FA(f"qv{i}", [128, 512], BF16) for i in range(2)],
    }

    def quant_tiles(tiles, idx, n):
        for w0 in range(0, n, 2):
            wv = list(range(w0, min(w0 + 2, n)))
            p.wait("vector", "VE")
            p.wait("vector", "AC")
            for wi, i in enumerate(wv):
                ve(lambda h, a=idx[i], wi=wi: h.tensor_tensor(
                    out=qsc["t"][wi][:], in0=tiles[a], in1=scale_bc[:], op=OP.mult))
            p.wait("scalar", "VE")
            for wi, i in enumerate(wv):
                ac(lambda h, wi=wi: h.activation(out=qsc["a"][wi][:], in_=qsc["t"][wi][:],
                                                 func=AF.Copy, scale=S29, bias=C23))
            p.wait("scalar", "AC")
            for wi, i in enumerate(wv):
                ac(lambda h, wi=wi: h.activation(out=qsc["t"][wi][:], in_=qsc["t"][wi][:],
                                                 func=AF.Copy, bias=C23))
            p.wait("vector", "AC")
            for wi, i in enumerate(wv):
                ve(lambda h, wi=wi: h.tensor_scalar(
                    out=qsc["v"][wi][:], in0=qsc["a"][wi][:], scalar1=-C23, scalar2=B29,
                    op0=OP.add, op1=OP.mult))
                ve(lambda h, wi=wi: h.tensor_scalar(
                    out=qsc["p"][wi][:], in0=qsc["t"][wi][:], scalar1=yNY_c[:],
                    scalar2=col2_c[:], op0=OP.mult, op1=OP.subtract))
            p.wait("vector", "VE")
            for wi, i in enumerate(wv):
                ve(lambda h, a=idx[i], wi=wi: h.tensor_tensor(
                    out=tiles[a], in0=qsc["p"][wi][:], in1=qsc["v"][wi][:], op=OP.add))
        p.wait("vector", "VE")

    def ffn_group(gi, first):
        p.barrier(f"FG{gi}")
        cbase = gi * CPG
        src = lambda ci: x_res[:, cbase + ci, :]
        ln_stats(src, fst, CPG)
        p.wait("vector", "AC")
        for ci in range(CPG):
            p.wait("vector", "VE")
            ve(lambda h, ci=ci: h.tensor_scalar(
                out=tm_buf[:, ci, :], in0=src(ci),
                scalar1=fst["m"][:, ci:ci + 1], scalar2=fst["r"][:, ci:ci + 1],
                op0=OP.subtract, op1=OP.mult))
            p.wait("sync", "VE")
            dmat(xlnT_f[:, :, ci * CH:(ci + 1) * CH], tm_buf[:, ci, :])
        p.wait("tensor", "TR")
        p.wait("scalar", "TR")
        for j in range(16):
            bi = j % 4
            bank = take_bank(bi)
            pe(lambda h, j=j, b=bank: h.matmul(out=b[:, 0:512], lhsT=h128_t[:],
                                               rhs=xlnT_f[:, j, :], start=True, stop=True))
            p.wait("scalar", "PE")
            ac(lambda h, j=j, b=bank: h.activation(out=y16[:, j, :], in_=b[:, 0:512],
                                                   func=AF.Copy))
            done_bank(bi, ["AC"])
        tiles2 = [y16[:, j, :] for j in range(16)] + [bsp[:, i, :] for i in range(8)]
        idx2, sp2 = fwht_stage2(tiles2, 4)
        absmax_scale(tiles2, idx2, sp2, 16, CW2, 0)
        quant_tiles(tiles2, idx2, 16)
        p.wait("tensor", "VE")

        p.wait("vector", "VE")
        ve(lambda h: h.tensor_scalar(out=rows[0:1, 1, :], in0=rows[0:1, 0, :],
                                     scalar1=s_sc[0:1, 0:1], scalar2=0.5 / 127.0,
                                     op0=OP.mult, op1=OP.mult))
        p.wait("tensor", "VE")
        b = take_bank(6)
        pe(lambda h, b=b: h.matmul(out=b[:, 0:512], lhsT=onesf_t[:], rhs=rows[0:1, 1, :],
                                   start=True, stop=True))
        p.wait("vector", "PE")
        ve(lambda h, b=b: h.tensor_copy(out=bc_g[:], in_=b[:, 0:512]))
        done_bank(6, ["VE"])
        p.wait("vector", "VE")
        ve(lambda h: h.tensor_scalar(out=rows[0:1, 1, :], in0=rows[0:1, 0, :],
                                     scalar1=s_sc[0:1, 1:2], scalar2=0.5 / 127.0,
                                     op0=OP.mult, op1=OP.mult))
        p.wait("tensor", "VE")
        b = take_bank(6)
        pe(lambda h, b=b: h.matmul(out=b[:, 0:512], lhsT=onesf_t[:], rhs=rows[0:1, 1, :],
                                   start=True, stop=True))
        p.wait("vector", "PE")
        ve(lambda h, b=b: h.tensor_copy(out=bc_u[:], in_=b[:, 0:512]))
        done_bank(6, ["VE"])
        p.wait("vector", "VE")

        NU = 128
        u_src = lambda n: (wgt if n % 2 == 0 else wut)
        u_cache = lambda n: (tern_g if n % 2 == 0 else tern_u)
        q_ve = [None] * NU
        q_ac = [None] * NU
        pe_aft = [None] * NU
        wfl_v = [None] * NU
        stc_v = [None] * NU
        gu_banks = {}

        def gu_load(n):
            if first:
                ws = n % 3
                if n >= 3:
                    p.wait("sync", "AC", q_ac[n - 3])
                src = u_src(n)[(n // 2) * 128:(n // 2 + 1) * 128, :]
                p.op("sync", lambda h, s=src, sl=ws: h.dma_start(
                    out=wfb[sl][:].rearrange("p j c -> p (j c)"), in_=s),
                    [(f"WFL{ws}", 16)])
                wfl_v[n] = p.val(f"WFL{ws}")
            else:
                ts = n % 2
                if n >= 2:
                    p.wait("sync", "PE", pe_aft[n - 2])
                src = u_cache(n)[(n // 2) * 128:(n // 2 + 1) * 128, :]
                p.op("sync", lambda h, s=src, sl=ts: h.dma_start(
                    out=tern_s[sl][:].rearrange("p j c -> p (j c)"), in_=s),
                    [(f"WFL{ts}", 16)])
                wfl_v[n] = p.val(f"WFL{ts}")

        def gu_quant(n):
            ts = n % 2
            guards = ([("PE", pe_aft[n - 2]), (f"STC{ts}", stc_v[n - 2])]
                      if n >= 2 else [])
            q_ac[n], q_ve[n] = w_quant(
                n % 3, ts, n % 2, n % 2, wfl_v[n],
                q_ve[n - 2] if n >= 2 else None, guards)

        def gu_store(n):
            ts = n % 2
            p.wait("gpsimd", "VE", q_ve[n])
            p.op("gpsimd", lambda h, sl=ts, n=n: h.dma_start(
                out=u_cache(n)[(n // 2) * 128:(n // 2 + 1) * 128, :],
                in_=tern_s[sl][:].rearrange("p j c -> p (j c)")), [(f"STC{ts}", 16)])
            stc_v[n] = p.val(f"STC{ts}")

        def gu_mm(n):
            ts = n % 2
            if first:
                p.wait("tensor", "VE", q_ve[n])
            else:
                p.wait("tensor", f"WFL{ts}", wfl_v[n])
            bi = n % 6
            bank = take_bank(bi)
            for j in range(16):
                pe(lambda h, sl=ts, j=j, b=bank, st=(j == 0): h.matmul(
                    out=b[:, 0:512], lhsT=tern_s[sl][:, j, :], rhs=tiles2[idx2[j]],
                    start=st, stop=(j == 15)))
            pe_aft[n] = p.val("PE")
            gu_banks[n] = (bi, bank)

        def gu_evac(oc):
            sl = oc % 2
            big, bg_ = gu_banks.pop(2 * oc)
            biu, bu_ = gu_banks.pop(2 * oc + 1)
            p.wait("vector", "PE", pe_aft[2 * oc])
            ve(lambda h, b=bg_, sl=sl: h.tensor_tensor(out=g_scr[sl][:], in0=b[:, 0:512],
                                                       in1=bc_g[:], op=OP.mult))
            done_bank(big, ["VE"])
            p.wait("scalar", "VE")
            ac(lambda h, sl=sl: h.activation(out=gs_scr[sl][:], in_=g_scr[sl][:],
                                             func=AF.Sigmoid))
            p.wait("vector", "PE", pe_aft[2 * oc + 1])
            ve(lambda h, b=bu_, sl=sl: h.tensor_tensor(out=up_scr[sl][:], in0=b[:, 0:512],
                                                       in1=bc_u[:], op=OP.mult))
            done_bank(biu, ["VE"])
            p.wait("vector", "AC")
            ve(lambda h, sl=sl: h.tensor_tensor(out=sg_scr[sl][:], in0=g_scr[sl][:],
                                                in1=gs_scr[sl][:], op=OP.mult))
            p.wait("vector", "VE")
            ve(lambda h, oc=oc, sl=sl: h.tensor_tensor(out=P2[:, oc, :], in0=sg_scr[sl][:],
                                                       in1=up_scr[sl][:], op=OP.mult))

        look = 3 if first else 2
        for i in range(look):
            gu_load(i)
        if first:
            gu_quant(0)
        for n in range(NU):
            if first and n + 1 < NU:
                gu_quant(n + 1)
            gu_mm(n)
            if first:
                gu_store(n)
            if n + look < NU:
                gu_load(n + look)
            if n % 2 == 1:
                gu_evac(n // 2)

        p.wait("tensor", "VE")
        for j in range(64):
            bi = j % 4
            bank = take_bank(bi)
            pe(lambda h, j=j, b=bank: h.matmul(out=b[:, 0:512], lhsT=h128_t[:],
                                               rhs=P2[:, j, :], start=True, stop=True))
            p.wait("scalar", "PE")
            ac(lambda h, j=j, b=bank: h.activation(out=P2[:, j, :], in_=b[:, 0:512],
                                                   func=AF.Copy))
            done_bank(bi, ["AC"])
        tiles8 = [P2[:, j, :] for j in range(64)] + [bsp[:, i, :] for i in range(8)]
        idx8, sp8 = fwht_stage2(tiles8, 6)
        absmax_scale(tiles8, idx8, sp8, 64, CW8, 0)
        quant_tiles(tiles8, idx8, 64)
        p.wait("tensor", "VE")

        p.wait("vector", "VE")
        ve(lambda h: h.tensor_scalar(out=rows[0:1, 1, :], in0=rows[0:1, 0, :],
                                     scalar1=s_sc[0:1, 2:3], scalar2=0.5 / 127.0,
                                     op0=OP.mult, op1=OP.mult))
        p.wait("tensor", "VE")
        b = take_bank(6)
        pe(lambda h, b=b: h.matmul(out=b[:, 0:512], lhsT=onesf_t[:], rhs=rows[0:1, 1, :],
                                   start=True, stop=True))
        p.wait("vector", "PE")
        ve(lambda h, b=b: h.tensor_copy(out=bc_g[:], in_=b[:, 0:512]))
        done_bank(6, ["VE"])
        p.wait("vector", "VE")
        ND = 64
        q_ve_d = [None] * ND
        q_ac_d = [None] * ND
        pe_d = [None] * ND
        wfl_d = [None] * ND
        stc_d = [None] * ND
        tr_d = [None] * 16
        dn_banks = {}
        last_gu = {0: 126, 1: 127, 2: 125}

        def dn_load(m):
            if first:
                ws = m % 3
                p.wait("scalar", "AC", q_ac_d[m - 3] if m >= 3 else q_ac[last_gu[ws]])
                src = wdt[m * 128:(m + 1) * 128, :]
                p.op("scalar", lambda h, s=src, sl=ws: h.dma_start(
                    out=wfb[sl][:].rearrange("p j c -> p (j c)"), in_=s),
                    [(f"WFL{ws}", 16)])
                wfl_d[m] = p.val(f"WFL{ws}")
            else:
                ts = m % 2
                p.wait("scalar", "PE", pe_d[m - 2] if m >= 2 else pe_aft[NU - 2 + m])
                src = tern_d[m * 128:(m + 1) * 128, :]
                p.op("scalar", lambda h, s=src, sl=ts: h.dma_start(
                    out=tern_s[sl][:].rearrange("p j c -> p (j c)"), in_=s),
                    [(f"WFL{ts}", 16)])
                wfl_d[m] = p.val(f"WFL{ts}")

        def dn_quant(m):
            ts = m % 2
            guards = [("PE", pe_d[m - 2] if m >= 2 else pe_aft[NU - 2 + m]),
                      (f"STC{ts}", stc_d[m - 2] if m >= 2 else stc_v[NU - 2 + m])]
            q_ac_d[m], q_ve_d[m] = w_quant(
                m % 3, ts, m % 2, 2, wfl_d[m],
                q_ve_d[m - 2] if m >= 2 else q_ve[NU - 2 + m], guards)

        def dn_store(m):
            slot = m % 2
            p.wait("gpsimd", "VE", q_ve_d[m])
            p.op("gpsimd", lambda h, sl=slot, m=m: h.dma_start(
                out=tern_d[m * 128:(m + 1) * 128, :],
                in_=tern_s[sl][:].rearrange("p j c -> p (j c)")), [(f"STC{slot}", 16)])
            stc_d[m] = p.val(f"STC{slot}")

        def dn_mm(m):
            slot = m % 2
            oc, qtr = m // 4, m % 4
            if first:
                p.wait("tensor", "VE", q_ve_d[m])
            else:
                p.wait("tensor", f"WFL{m % 2}", wfl_d[m])
            bi = oc % 6
            if qtr == 0:
                dn_banks[oc] = (bi, take_bank(bi))
            bank = dn_banks[oc][1]
            for j in range(16):
                pe(lambda h, sl=slot, j=j, jg=qtr * 16 + j, b=bank,
                   st=(qtr == 0 and j == 0), sp=(qtr == 3 and j == 15): h.matmul(
                    out=b[:, 0:512], lhsT=tern_s[sl][:, j, :], rhs=tiles8[idx8[jg]],
                    start=st, stop=sp))
            pe_d[m] = p.val("PE")

        def dn_evac(oc):
            sl = oc % 2
            bi, bank = dn_banks.pop(oc)
            p.wait("vector", "PE", pe_d[4 * oc + 3])
            if oc >= 2:
                p.wait("vector", f"TRD{sl}", tr_d[oc - 2])
            ve(lambda h, b=bank, sl=sl: h.tensor_tensor(out=dfm[sl][:], in0=b[:, 0:512],
                                                        in1=bc_g[:], op=OP.mult))
            done_bank(bi, ["VE"])
            p.wait("sync", "VE")
            p.op("sync", lambda h, oc=oc, sl=sl: h.dma_start(
                out=tm_buf[:, :, oc * 128:(oc + 1) * 128], in_=dfm[sl][:],
                transpose=True), [(f"TRD{sl}", 16)])
            tr_d[oc] = p.val(f"TRD{sl}")

        look_d = 3 if first else 2
        for i in range(look_d):
            dn_load(i)
        if first:
            dn_quant(0)
        for m in range(ND):
            if first and m + 1 < ND:
                dn_quant(m + 1)
            dn_mm(m)
            if first:
                dn_store(m)
            if m + look_d < ND:
                dn_load(m + look_d)
            if m % 4 == 3:
                dn_evac(m // 4)
        for e in ("vector", "scalar"):
            p.wait(e, "TR")
            p.wait(e, "TRD0")
            p.wait(e, "TRD1")
        ln_stats(lambda ci: tm_buf[:, ci, :], fst, CPG)
        p.wait("vector", "AC")
        for ci in range(CPG):
            p.wait("vector", "VE")
            ve(lambda h, ci=ci: h.tensor_scalar(
                out=aq_s[:, 0:D], in0=tm_buf[:, ci, :],
                scalar1=fst["m"][:, ci:ci + 1], scalar2=fst["r"][:, ci:ci + 1],
                op0=OP.subtract, op1=OP.mult))
            p.wait("vector", "VE")
            ve(lambda h, ci=ci, cb=cbase: h.tensor_tensor(
                out=x_res[:, cb + ci, :], in0=x_res[:, cb + ci, :],
                in1=aq_s[:, 0:D], op=OP.add))
        p.wait("gpsimd", "VE")
        dma(y_out.rearrange("(c p) f -> p c f", p=128)[:, cbase:cbase + CPG, :],
            x_res[:, cbase:cbase + CPG, :], sem="ST")

    xlnT_f = P2[:, 0:16, :]
    for gi in range(TOK // GA):
        ffn_group(gi, first=(gi == 0))

    p.wait("gpsimd", "ST", p.val("ST"))
    ffn_stack.close()

    p.emit(top)
    top.close()
    return nc


_NC_CACHE = {}


def stage_shared(inp):
    f = lambda a: np.ascontiguousarray(np.asarray(a, dtype=np.float32))
    jj, ii = np.meshgrid(np.arange(128), np.arange(128), indexing="ij")
    return dict(
        wq=tile_cols(f(inp["L0_attn_Wq"])),
        wk=tile_cols(f(inp["L0_attn_Wk"])),
        wkg1=np.ascontiguousarray(
            f(inp["L0_attn_Wkg1"]).reshape(16, 128, 16).transpose(1, 0, 2).reshape(128, 256)),
        wkg2=f(inp["L0_attn_Wkg2"]),
        bkg2=np.ascontiguousarray(f(inp["L0_attn_bkg2"]).reshape(8, 128).T),
        wv=tile_cols(f(inp["L0_attn_Wv"])),
        wg=tile_cols(f(inp["L0_attn_Wg"])),
        bg=np.ascontiguousarray(f(inp["L0_attn_bg"]).reshape(16, 128).T),
        wo=np.ascontiguousarray(
            f(inp["L0_attn_Wo"]).reshape(16, 128, 4, 512).transpose(2, 1, 0, 3)
            .reshape(512, 8192)),
        wgt=tile_cols(f(inp["L0_ffn_w_gate"]).T),
        wut=tile_cols(f(inp["L0_ffn_w_up"]).T),
        wdt=np.ascontiguousarray(
            f(inp["L0_ffn_w_down"]).T.reshape(4, 16, 128, 16, 128).transpose(3, 0, 2, 1, 4)
            .reshape(8192, 2048)),
        h128=hadamard(128),
        idf=np.eye(128, dtype=np.float32),
        maskt=(jj <= ii).astype(np.float32),
        onesf=np.ones((1, 128), np.float32),
        onecol=np.ones((128, 1), np.float32),
    )


def kernel(L0_attn_x, L0_attn_Wq, L0_attn_Wk, L0_attn_Wkg1, L0_attn_Wkg2, L0_attn_bkg2,
           L0_attn_Wv, L0_attn_Wg, L0_attn_bg, L0_attn_Wo,
           L0_ffn_w_gate, L0_ffn_w_up, L0_ffn_w_down):
    x = np.ascontiguousarray(np.asarray(L0_attn_x, dtype=np.float32))
    shared = stage_shared(dict(
        L0_attn_Wq=L0_attn_Wq, L0_attn_Wk=L0_attn_Wk, L0_attn_Wkg1=L0_attn_Wkg1,
        L0_attn_Wkg2=L0_attn_Wkg2, L0_attn_bkg2=L0_attn_bkg2, L0_attn_Wv=L0_attn_Wv,
        L0_attn_Wg=L0_attn_Wg, L0_attn_bg=L0_attn_bg, L0_attn_Wo=L0_attn_Wo,
        L0_ffn_w_gate=L0_ffn_w_gate, L0_ffn_w_up=L0_ffn_w_up,
        L0_ffn_w_down=L0_ffn_w_down))
    zeros_pre = np.zeros((TOK, D), np.float32)

    if "nc" not in _NC_CACHE:
        _NC_CACHE["nc"] = build()
    nc = _NC_CACHE["nc"]

    in_maps = []
    for c in range(8):
        seq, half = c // 2, c % 2
        m = dict(shared)
        m["x_own"] = np.ascontiguousarray(x[seq, half * TOK:(half + 1) * TOK, :])
        m["x_pre"] = np.ascontiguousarray(x[seq, 0:TOK, :]) if half == 1 else zeros_pre
        in_maps.append(m)

    res = run_bass_kernel_spmd(nc, in_maps, core_ids=list(range(8)))
    out = np.empty((4, 2048, D), np.float32)
    for c in range(8):
        seq, half = c // 2, c % 2
        out[seq, half * TOK:(half + 1) * TOK, :] = res.results[c]["y"]
    return out


# version 15
# speedup vs baseline: 1.0072x; 1.0072x over previous
import contextlib
import numpy as np
import concourse.bass as bass
import concourse.mybir as mybir
from concourse.bass_utils import run_bass_kernel_spmd

F32 = mybir.dt.float32
BF16 = mybir.dt.bfloat16
FP16 = mybir.dt.float16
AF = mybir.ActivationFunctionType
OP = mybir.AluOpType
AX = mybir.AxisListType

D = 2048
H = 8
DV = 256
CH = 128
TOK = 1024
NCH = TOK // CH
GA = 512
CPG = GA // CH
NG = TOK // GA
I8 = 8192
EPS = 1e-5

C23 = float(np.float32(1.5 * 2 ** 23))
S29 = float(2.0 ** -29)
B29 = float(2.0 ** 29)
LN16 = float(np.log(16.0))
CW2 = float(1.0 / np.sqrt(2048.0))
CW8 = float(1.0 / np.sqrt(8192.0))

ENGS = ("sync", "scalar", "vector", "tensor", "gpsimd")
DEBUG_TAPS = []
PHASE_LIMIT = "full"


def hadamard(n):
    h = np.array([[1.0]], np.float32)
    while h.shape[0] < n:
        h = np.block([[h, h], [h, -h]])
    return np.ascontiguousarray(h, np.float32)


def tile_cols(w):
    di, no = w.shape
    t = w.reshape(di // 128, 128, no // 128, 128).transpose(2, 1, 0, 3).reshape(no, di)
    return np.ascontiguousarray(t)


class P:
    def __init__(self, nc):
        self.nc = nc
        self.q = {e: [] for e in ENGS}
        self.sems = {}
        self.cnt = {}
        self._names = []

    def sem(self, name):
        if name not in self.cnt:
            self._names.append(name)
            self.cnt[name] = 0
        return name

    def op(self, eng, fn, inc=None):
        incs = list(inc or [])
        for s, n in incs:
            self.sem(s)
            self.cnt[s] += n

        def run(h, fn=fn, incs=incs):
            instr = fn(h)
            for s, n in incs:
                instr = instr.then_inc(self.sems[s], n)

        self.q[eng].append(run)

    def wait(self, eng, s, val=None):
        self.sem(s)
        v = self.cnt[s] if val is None else val
        if v <= 0:
            return
        self.q[eng].append(lambda h, s=s, v=v: h.wait_ge(self.sems[s], v))

    def val(self, s):
        return self.cnt[self.sem(s)]

    def barrier(self, name, engines=ENGS):
        s = self.sem(name)
        snap = [(n, v) for n, v in self.cnt.items() if v > 0 and n != s]
        for e in engines:
            for n, v in snap:
                self.wait(e, n, v)
            self.op(e, lambda h: h.nop(), [(s, 1)])
        tot = self.cnt[s]
        for e in engines:
            self.wait(e, s, tot)

    def emit(self, stack):
        for n in self._names:
            self.sems[n] = stack.enter_context(self.nc.semaphore(n))
        block = stack.enter_context(self.nc.Block())
        for e in ENGS:
            ops = self.q[e]

            def body(h, ops=ops):
                for f in ops:
                    f(h)

            getattr(block, e)(body)


def build():
    nc = bass.Bass("TRN2", target_bir_lowering=False, debug=False)
    p = P(nc)
    din = lambda name, shape: nc.dram_tensor(name, shape, F32, kind="ExternalInput").ap()

    x_own = din("x_own", [TOK, D])
    x_pre = din("x_pre", [TOK, D])
    wq = din("wq", [1024, D])
    wk = din("wk", [1024, D])
    wkg1 = din("wkg1", [128, 16 * 16])
    wkg2 = din("wkg2", [16, 1024])
    bkg2 = din("bkg2", [128, 8])
    wv = din("wv", [D, D])
    wg = din("wg", [D, D])
    bg = din("bg", [128, 16])
    wo = din("wo", [4 * 128, 16 * 512])
    wgt = din("wgt", [I8, D])
    wut = din("wut", [I8, D])
    wdt = din("wdt", [I8, D])
    h128 = din("h128", [128, 128])
    idf = din("idf", [128, 128])
    maskt = din("maskt", [128, 128])
    onesf = din("onesf", [1, 128])
    onecol = din("onecol", [128, 1])

    y_out = nc.dram_tensor("y", [TOK, D], F32, kind="ExternalOutput").ap()
    tern_g = nc.dram_tensor("tern_g", [64 * 128, D], BF16, kind="Internal").ap()
    tern_u = nc.dram_tensor("tern_u", [64 * 128, D], BF16, kind="Internal").ap()
    tern_d = nc.dram_tensor("tern_d", [64 * 128, D], BF16, kind="Internal").ap()

    top = contextlib.ExitStack()
    RA = lambda name, shape, dt: top.enter_context(nc.sbuf_tensor(name, shape, dt))

    x_res = RA("x_res", [128, NCH, D], BF16)
    S_f = RA("S_f", [128, H * DV], F32)
    S_bf = RA("S_bf", [128, H * DV], BF16)
    tmpS = RA("tmpS", [128, DV], F32)
    onesf_t = RA("onesf_t", [1, 128], F32)
    onecol_t = RA("onecol_t", [128, 1], F32)
    onecol_b = RA("onecol_b", [128, 1], BF16)
    bkgn_c = RA("bkgn_c", [128, 8], F32)
    bg_c = RA("bg_c", [128, 16], F32)
    yA_c = RA("yA_c", [128, 1], F32)
    yNY_c = RA("yNY_c", [128, 1], F32)
    col2_c = RA("col2_c", [128, 1], F32)
    s_sc = RA("s_sc", [1, 4], F32)
    s_stat = RA("s_stat", [1, 4], F32)
    shalf = RA("shalf", [128, 3], F32)
    nshalf = RA("nshalf", [128, 3], F32)
    trash = RA("trash", [128, D], BF16)
    cst_t = RA("cst_t", [128, 4], F32)

    ps = [top.enter_context(nc.psum_tensor(f"ps{i}", [128, 512], F32)) for i in range(7)]
    psT = top.enter_context(nc.psum_tensor("psT", [128, 1024], BF16))

    for s in ("PE", "VE", "AC", "PO", "TR", "LD", "ST"):
        p.sem(s)

    def dma(out, in_, sem="LD"):
        p.op("gpsimd", lambda h, o=out, i=in_: h.dma_start(out=o, in_=i), [(sem, 16)])

    def dmat(out, in_):
        p.op("sync", lambda h, o=out, i=in_: h.dma_start(out=o, in_=i, transpose=True), [("TR", 16)])

    def ve(fn, inc=1):
        p.op("vector", fn, [("VE", inc)])

    def ac(fn, inc=1):
        p.op("scalar", fn, [("AC", inc)])

    def pe(fn, inc=1):
        p.op("tensor", fn, [("PE", inc)])

    bank_guard = {}

    def take_bank(i):
        g = bank_guard.pop(i, None)
        if g:
            for s, v in g:
                p.wait("tensor", s, v)
        return ps[i]

    def done_bank(i, sems):
        bank_guard[i] = [(s, p.val(s)) for s in sems]

    dma(x_res[:], x_own.rearrange("(c p) f -> p c f", p=128))
    dma(onesf_t[:], onesf)
    dma(onecol_t[:], onecol)
    dma(onecol_b[:], onecol)
    dma(bkgn_c[:], bkg2)
    dma(bg_c[:], bg)
    p.wait("scalar", "LD")
    p.wait("vector", "LD")
    p.wait("tensor", "LD")

    _cvals = [EPS, 1.0, C23, -LN16]
    for _i, _v in enumerate(_cvals):
        ve(lambda h, i=_i, v=_v: h.memset(cst_t[:, i:i + 1], v))
        nc.const_aps.register(F32, float(np.float32(_v)), cst_t[:, _i:_i + 1])
        nc.const_aps.register(F32, float(_v), cst_t[:, _i:_i + 1])
    p.wait("scalar", "VE")
    ve(lambda h: h.tensor_scalar(out=bkgn_c[:], in0=bkgn_c[:], scalar1=-1.0, scalar2=None,
                                 op0=OP.mult, op1=OP.bypass))
    ac(lambda h: h.activation(out=yA_c[:], in_=onecol_t[:], func=AF.Copy, scale=S29, bias=C23))
    p.wait("vector", "AC")
    ve(lambda h: h.tensor_scalar(out=yNY_c[:], in0=yA_c[:], scalar1=-C23, scalar2=B29,
                                 op0=OP.add, op1=OP.mult))
    p.wait("vector", "VE")
    ve(lambda h: h.tensor_scalar(out=yA_c[:], in0=yNY_c[:], scalar1=-1.0, scalar2=1.0,
                                 op0=OP.mult, op1=OP.add))
    p.wait("vector", "VE")
    ve(lambda h: h.tensor_copy(out=yNY_c[:], in_=yA_c[:]))
    p.wait("vector", "VE")
    ve(lambda h: h.tensor_scalar(out=col2_c[:], in0=yNY_c[:], scalar1=C23, scalar2=None,
                                 op0=OP.mult, op1=OP.bypass))
    ve(lambda h: h.memset(S_f[:], 0.0))
    ve(lambda h: h.memset(S_bf[:], 0.0))
    p.wait("tensor", "VE")
    p.wait("gpsimd", "VE")

    def ln_stats(src, stats, n_ch, dmodel=D):
        for e in ("vector", "scalar"):
            for s in ("AC", "VE", "TR"):
                p.wait(e, s)
        for ci in range(n_ch):
            ve(lambda h, ci=ci: h.tensor_reduce(out=stats["s1"][:, ci:ci + 1],
                                                in_=src(ci), axis=AX.X, op=OP.add))
        for ci in range(n_ch):
            if ci > 0:
                p.wait("scalar", "AC")
            ac(lambda h, ci=ci: h.activation(out=trash[:, 0:dmodel], in_=src(ci),
                                             func=AF.Square,
                                             accum_out=stats["s2"][:, ci:ci + 1]))
        p.wait("vector", "AC")
        p.wait("vector", "VE")
        ve(lambda h: h.tensor_scalar(out=stats["m"][:, 0:n_ch], in0=stats["s1"][:, 0:n_ch],
                                     scalar1=1.0 / dmodel, scalar2=None,
                                     op0=OP.mult, op1=OP.bypass))
        p.wait("vector", "VE")
        ve(lambda h: h.tensor_tensor(out=stats["ms"][:, 0:n_ch], in0=stats["m"][:, 0:n_ch],
                                     in1=stats["m"][:, 0:n_ch], op=OP.mult))
        p.wait("vector", "VE")
        ve(lambda h: h.tensor_scalar(out=stats["v"][:, 0:n_ch], in0=stats["s2"][:, 0:n_ch],
                                     scalar1=1.0 / dmodel, scalar2=None,
                                     op0=OP.mult, op1=OP.bypass))
        p.wait("vector", "VE")
        ve(lambda h: h.tensor_tensor(out=stats["v"][:, 0:n_ch], in0=stats["v"][:, 0:n_ch],
                                     in1=stats["ms"][:, 0:n_ch], op=OP.subtract))
        p.wait("scalar", "VE")
        ac(lambda h: h.activation(out=stats["r"][:, 0:n_ch], in_=stats["v"][:, 0:n_ch],
                                  func=AF.Ln, bias=EPS))
        p.wait("scalar", "AC")
        ac(lambda h: h.activation(out=stats["r"][:, 0:n_ch], in_=stats["r"][:, 0:n_ch],
                                  func=AF.Exp, scale=-0.5))

    if PHASE_LIMIT == "init":
        p.wait("gpsimd", "VE")
        dma(y_out.rearrange("(c p) f -> p c f", p=128), x_res[:], sem="ST")
        p.wait("gpsimd", "ST", p.val("ST"))
        p.emit(top)
        top.close()
        return nc

    attn_stack = contextlib.ExitStack()
    AA = lambda name, shape, dt: attn_stack.enter_context(nc.sbuf_tensor(name, shape, dt))
    idb_t = AA("idb_t", [128, 128], BF16)
    mask_t = AA("mask_t", [128, 128], BF16)
    wkg1_t = AA("wkg1_t", [128, 16, 16], BF16)
    wkg2_t = AA("wkg2_t", [16, 1024], BF16)
    dma(idb_t[:], idf)
    dma(mask_t[:], maskt)
    dma(wkg1_t[:], wkg1.rearrange("p (j c) -> p j c", j=16))
    dma(wkg2_t[:], wkg2)
    p.wait("tensor", "LD")
    p.wait("vector", "LD")
    k_fm = AA("k_fm", [128, H, GA], BF16)
    gkln = AA("gkln", [128, H, GA], BF16)
    gkc_c = AA("gkc_c", [128, H, CH], F32)
    pool4 = AA("pool4", [128, 2, 512], BF16)
    e1_c = AA("e1_c", [128, H, CH], BF16)
    e3_c = AA("e3_c", [128, H, CH], BF16)
    dl_c = AA("dl_c", [128, CPG * H], F32)
    v_tm = AA("v_tm", [128, CPG, D], BF16)
    qd_t = AA("qd_t", [128, H, CH], BF16)
    ke3_t = AA("ke3_t", [128, H, CH], BF16)
    ke3m = AA("ke3m", [128, H, CH], BF16)
    amT = AA("amT", [128, H, CH], BF16)
    o_bf = AA("o_bf", [128, D], BF16)
    o_fm = AA("o_fm", [128, D], BF16)
    st4 = {n: AA("st4_" + n, [128, CPG], F32) for n in ("s1", "s2", "m", "ms", "v", "r")}
    st8 = {n: AA("st8_" + n, [128, H], F32) for n in ("s1", "s2", "m", "ms", "v", "r")}
    xlnT = AA("xlnT", [128, 16, GA], BF16)
    xln_b = AA("xln_b", [128, 2, D], BF16)
    wbA = [AA(f"wbA{i}", [128, 16, 128], BF16) for i in range(3)]
    t1_sb = AA("t1_sb", [16, GA], BF16)
    sw_b = [AA(f"sw{i}", [128, 2048], BF16) for i in range(2)]
    o_bfB = AA("o_bfB", [128, D], BF16)
    scol = AA("scol", [128, 192], F32)
    sred = AA("sred", [128, 4], F32)
    own_stack = contextlib.ExitStack()
    OA = lambda name, shape, dt: own_stack.enter_context(nc.sbuf_tensor(name, shape, dt))

    for i in range(2):
        p.sem(f"SWL{i}")
    SUNIT = 128 * 2048
    s_srcs = [wgt.rearrange("a b -> (a b)"), wut.rearrange("a b -> (a b)"),
              wdt.rearrange("a b -> (a b)")]
    NPM = (D * I8) // SUNIT
    s_plan = [(m, off) for m in range(3) for off in range(0, D * I8, SUNIT)]
    s_ld = [0]
    s_ab = [0]
    swl_v = [None] * len(s_plan)
    ac_val = [None] * len(s_plan)

    def s_finalize(m):
        p.wait("vector", "AC")
        ve(lambda h, m=m: h.tensor_reduce(out=sred[:, m:m + 1],
                                          in_=scol[:, m * NPM:(m + 1) * NPM],
                                          axis=AX.X, op=OP.add))
        p.wait("tensor", "VE")
        b = take_bank(6)
        pe(lambda h, m=m, b=b: h.matmul(out=b[0:1, 0:1], lhsT=sred[:, m:m + 1],
                                        rhs=onecol_t[:], start=True, stop=True))
        p.wait("vector", "PE")
        ve(lambda h, m=m, b=b: h.tensor_scalar(out=s_sc[0:1, m:m + 1], in0=b[0:1, 0:1],
                                               scalar1=1.0 / (D * I8), scalar2=EPS,
                                               op0=OP.mult, op1=OP.max))
        done_bank(6, ["VE"])
        p.wait("tensor", "VE")
        b = take_bank(6)
        pe(lambda h, m=m, b=b: h.matmul(out=b[0:128, 0:1], lhsT=onesf_t[:],
                                        rhs=s_sc[0:1, m:m + 1], start=True, stop=True))
        p.wait("vector", "PE")
        ve(lambda h, m=m, b=b: h.tensor_scalar(out=shalf[:, m:m + 1], in0=b[0:128, 0:1],
                                               scalar1=0.5, scalar2=None,
                                               op0=OP.mult, op1=OP.bypass))
        ve(lambda h, m=m, b=b: h.tensor_scalar(out=nshalf[:, m:m + 1], in0=b[0:128, 0:1],
                                               scalar1=-0.5, scalar2=None,
                                               op0=OP.mult, op1=OP.bypass))
        done_bank(6, ["VE"])

    def s_emit_load(i):
        m, off = s_plan[i]
        slot = i % 2
        if i >= 2:
            p.wait("gpsimd", "AC", ac_val[i - 2])
        dma(sw_b[slot][:], s_srcs[m][off:off + SUNIT].rearrange("(p n) -> p n", p=128),
            sem=f"SWL{slot}")
        swl_v[i] = p.val(f"SWL{slot}")

    def s_emit_abs(i):
        m, off = s_plan[i]
        slot = i % 2
        u = i - m * NPM
        p.wait("scalar", f"SWL{slot}", swl_v[i])
        ac(lambda h, sl=slot, c=m * NPM + u: h.activation(
            out=sw_b[sl][:], in_=sw_b[sl][:], func=AF.Abs,
            accum_out=scol[:, c:c + 1]))
        ac_val[i] = p.val("AC")
        if u == NPM - 1:
            s_finalize(m)

    def s_step(k):
        for _ in range(k):
            n = len(s_plan)
            if s_ld[0] < n:
                s_emit_load(s_ld[0])
                s_ld[0] += 1
            lag = 1 if s_ld[0] < n else 0
            if s_ab[0] < s_ld[0] - lag:
                s_emit_abs(s_ab[0])
                s_ab[0] += 1

    for i in range(3):
        p.sem(f"WBL{i}")
    wb_n = [0]
    wb_free = [None, None, None]

    def load_wcol(src_col):
        n = wb_n[0]; wb_n[0] += 1
        slot = n % 3
        if wb_free[slot] is not None:
            p.wait("gpsimd", wb_free[slot][0], wb_free[slot][1])
        dma(wbA[slot][:], src_col.rearrange("p (j c) -> p j c", j=16), sem=f"WBL{slot}")
        return slot, p.val(f"WBL{slot}")

    def project(w_src, n_oc, evac):
        pend = [load_wcol(w_src[i * 128:(i + 1) * 128, :])
                for i in range(min(3, n_oc))]
        for oc in range(n_oc):
            slot, wblv = pend.pop(0)
            p.wait("tensor", f"WBL{slot}", wblv)
            bi = oc % 4
            bank = take_bank(bi)
            for j in range(16):
                pe(lambda h, sl=slot, j=j, b=bank, st=(j == 0): h.matmul(
                    out=b[:, 0:512], lhsT=wbA[sl][:, j, :], rhs=xlnT[:, j, :],
                    start=st, stop=(j == 15)))
            wb_free[slot] = ("PE", p.val("PE"))
            if oc + 3 < n_oc:
                pend.append(load_wcol(w_src[(oc + 3) * 128:(oc + 4) * 128, :]))
            evac(oc, bi, bank)
            s_step(1)

    def attn_group(gi, own, q_fm=None, gscr=None, gsilu=None, P_t=None, attn_o=None,
                   xpre_b=None, upto="full"):
        p.barrier(f"AG{int(own)}_{gi}")
        if own:
            src = lambda ci: x_res[:, gi * CPG + ci, :]
        else:
            for half in range(2):
                dma(xpre_b[:, half, :, :],
                    x_pre.rearrange("(c p) f -> p c f", p=128)
                    [:, gi * CPG + half * 2:gi * CPG + half * 2 + 2, :])
            p.wait("vector", "LD")
            p.wait("scalar", "LD")
            src = lambda ci: xpre_b[:, ci // 2, ci % 2, :]

        ln_stats(src, st4, CPG)
        p.wait("vector", "AC")
        for ci in range(CPG):
            sl = ci % 2
            if ci >= 2:
                p.wait("vector", "TR", p.val("TR"))
            ve(lambda h, ci=ci, sl=sl: h.tensor_scalar(
                out=xln_b[:, sl, :], in0=src(ci),
                scalar1=st4["m"][:, ci:ci + 1], scalar2=st4["r"][:, ci:ci + 1],
                op0=OP.subtract, op1=OP.mult))
            p.wait("sync", "VE")
            dmat(xlnT[:, :, ci * CH:(ci + 1) * CH], xln_b[:, sl, :])
        p.wait("tensor", "TR")

        b4 = take_bank(4)
        for j in range(16):
            pe(lambda h, j=j, b=b4, st=(j == 0): h.matmul(
                out=b[0:16, 0:512], lhsT=wkg1_t[:, j, :], rhs=xlnT[:, j, :],
                start=st, stop=(j == 15)))
        p.wait("vector", "PE")
        ve(lambda h, b=b4: h.tensor_copy(out=t1_sb[:], in_=b[0:16, 0:512]))
        done_bank(4, ["VE"])
        p.wait("tensor", "VE")
        for hh in range(H):
            bi = hh % 2
            bank = take_bank(bi)
            pe(lambda h, hh=hh, b=bank: h.matmul(
                out=b[:, 0:512], lhsT=wkg2_t[0:16, hh * 128:(hh + 1) * 128],
                rhs=t1_sb[:], start=True, stop=True))
            sl = hh % 2
            p.wait("scalar", "PE")
            if hh >= 2:
                p.wait("scalar", "AC", p.val("AC"))
            ac(lambda h, hh=hh, b=bank, sl=sl: h.activation(
                out=pool4[:, sl, :], in_=b[:, 0:512], func=AF.Exp, scale=-1.0,
                bias=bkgn_c[:, hh:hh + 1]))
            p.wait("scalar", "AC")
            ac(lambda h, hh=hh, sl=sl: h.activation(
                out=gkln[:, hh, :], in_=pool4[:, sl, :], func=AF.Ln, bias=1.0))
            done_bank(bi, ["AC"])

        def evac_k(oc, bi, bank):
            p.wait("scalar", "PE")
            ac(lambda h, oc=oc, b=bank: h.activation(out=k_fm[:, oc, :], in_=b[:, 0:512],
                                                     func=AF.Copy))
            done_bank(bi, ["AC"])

        project(wk, 8, evac_k)

        def evac_v(oc, bi, bank):
            sl = oc % 2
            p.wait("vector", "PE")
            if oc < 2:
                p.wait("vector", "AC")
            if oc >= 2:
                p.wait("vector", "TR", p.val("TR"))
            ve(lambda h, b=bank, sl=sl: h.tensor_copy(out=pool4[:, sl, :], in_=b[:, 0:512]))
            done_bank(bi, ["VE"])
            p.wait("sync", "VE")
            dmat(v_tm[:, :, oc * 128:(oc + 1) * 128], pool4[:, sl, :])

        project(wv, 16, evac_v)

        if own:
            def evac_q(oc, bi, bank):
                p.wait("scalar", "PE")
                ac(lambda h, oc=oc, b=bank: h.activation(out=q_fm[:, oc, :], in_=b[:, 0:512],
                                                         func=AF.Copy))
                done_bank(bi, ["AC"])

            project(wq, 8, evac_q)

            def evac_g(oc, bi, bank):
                sl = oc % 2
                p.wait("scalar", "PE")
                if oc >= 2:
                    p.wait("scalar", "VE", p.val("VE"))
                p.wait("vector", "PE")
                ve(lambda h, oc=oc, b=bank, sl=sl: h.tensor_scalar(
                    out=gscr[:, sl, :], in0=b[:, 0:512],
                    scalar1=bg_c[:, oc:oc + 1], scalar2=None, op0=OP.add, op1=OP.bypass))
                p.wait("scalar", "VE")
                ac(lambda h, oc=oc, b=bank, sl=sl: h.activation(
                    out=gscr[:, 2 + sl, :], in_=b[:, 0:512], func=AF.Sigmoid,
                    bias=bg_c[:, oc:oc + 1]))
                done_bank(bi, ["AC", "VE"])
                p.wait("vector", "AC")
                p.wait("vector", "VE")
                ve(lambda h, oc=oc, sl=sl: h.tensor_tensor(
                    out=gsilu[:, oc, :], in0=gscr[:, sl, :], in1=gscr[:, 2 + sl, :],
                    op=OP.mult))

            project(wg, 16, evac_g)
        if upto == "proj":
            return

        p.wait("tensor", "TR")
        o_bf2 = [o_bf, o_bfB]
        prep_ve = [None] * CPG
        prep_ac = [None] * CPG
        body_pe = [None] * CPG
        body_ac = [None] * CPG
        tail_tr = [None] * CPG
        tail_ve = [None] * CPG

        def chunk_prep(ci):
            if ci > 0:
                p.wait("vector", "AC", prep_ac[ci - 1])
            else:
                p.wait("vector", "AC")
                p.wait("vector", "PE")
            for hh in range(H):
                ve(lambda h, hh=hh, ci=ci: h.tensor_tensor_scan(
                    out=gkc_c[:, hh, :], data0=gkln[:, hh, ci * CH:(ci + 1) * CH],
                    data1=gkln[:, hh, ci * CH:(ci + 1) * CH], initial=0.0,
                    op0=OP.add, op1=OP.bypass))
            p.wait("scalar", "VE")
            ac(lambda h: h.activation(out=e1_c[:].rearrange("p h c -> p (h c)"),
                                      in_=gkc_c[:].rearrange("p h c -> p (h c)"),
                                      func=AF.Exp, scale=-1.0 / 16.0))
            ac(lambda h: h.activation(out=e3_c[:].rearrange("p h c -> p (h c)"),
                                      in_=gkc_c[:].rearrange("p h c -> p (h c)"),
                                      func=AF.Exp, scale=1.0 / 16.0, bias=-LN16))
            ac(lambda h, ci=ci: h.activation(
                out=dl_c[:, ci * H:(ci + 1) * H],
                in_=gkc_c[:, :, 127:128].rearrange("p h c -> p (h c)"),
                func=AF.Exp, scale=-1.0 / 16.0))
            prep_ac[ci] = p.val("AC")
            p.wait("vector", "AC", prep_ac[ci])
            if ci > 0:
                p.wait("vector", "PE", body_pe[ci - 1])
            if own:
                ve(lambda h, ci=ci: h.tensor_tensor(
                    out=qd_t[:], in0=q_fm[:, :, ci * CH:(ci + 1) * CH],
                    in1=e1_c[:], op=OP.mult))
            ve(lambda h, ci=ci: h.tensor_tensor(
                out=ke3_t[:], in0=k_fm[:, :, ci * CH:(ci + 1) * CH],
                in1=e3_c[:], op=OP.mult))
            prep_ve[ci] = p.val("VE")

        def chunk_body(ci):
            ob = o_bf2[ci % 2]
            p.wait("sync", "VE", prep_ve[ci])
            if ci > 0:
                p.wait("sync", "PE", body_pe[ci - 1])
            else:
                p.wait("sync", "PE")
            for hh in range(H):
                p.op("sync", lambda h, hh=hh: h.dma_start(
                    out=ke3m[:, hh:hh + 1, :], in_=ke3_t[:, hh, :], transpose=True),
                    [("KTR", 16)])
            ktr = p.val("KTR")
            p.wait("tensor", "VE", prep_ve[ci])
            if own:
                for hh in range(H):
                    bi = 4 + (hh % 2)
                    bank = take_bank(bi)
                    pe(lambda h, hh=hh, b=bank: h.matmul(
                        out=b[:, 0:128], lhsT=ke3_t[:, hh, :], rhs=qd_t[:, hh, :],
                        start=True, stop=True))
                    p.wait("vector", "PE")
                    ve(lambda h, hh=hh, b=bank: h.tensor_tensor(
                        out=amT[:, hh, :], in0=b[:, 0:128], in1=mask_t[:], op=OP.mult))
                    amt_ve = p.val("VE")
                    done_bank(bi, ["VE"])
                    bank = take_bank(bi)
                    p.wait("tensor", "VE", amt_ve)
                    pe(lambda h, hh=hh, ci=ci, b=bank: h.matmul(
                        out=b[:, 0:256], lhsT=amT[:, hh, :],
                        rhs=v_tm[:, ci, hh * 256:(hh + 1) * 256], start=True, stop=False))
                    pe(lambda h, hh=hh, b=bank: h.matmul(
                        out=b[:, 0:256], lhsT=qd_t[:, hh, :],
                        rhs=S_bf[:, hh * 256:(hh + 1) * 256], start=False, stop=True))
                    p.wait("scalar", "PE")
                    if hh == 0 and ci >= 2:
                        p.wait("scalar", "TR", tail_tr[ci - 2])
                        p.wait("scalar", "VE", tail_ve[ci - 2])
                    ac(lambda h, hh=hh, b=bank, ob=ob: h.activation(
                        out=ob[:, hh * 256:(hh + 1) * 256], in_=b[:, 0:256],
                        func=AF.Copy))
                    done_bank(bi, ["AC"])
            p.wait("tensor", "KTR", ktr)
            for hh in range(H):
                bi = 4 + (hh % 2)
                bank = take_bank(bi)
                pe(lambda h, hh=hh, ci=ci, b=bank: h.matmul(
                    out=b[:, 0:256], lhsT=ke3m[:, hh, :],
                    rhs=v_tm[:, ci, hh * 256:(hh + 1) * 256], start=True, stop=True))
                p.wait("vector", "PE")
                p.wait("vector", "VE")
                ve(lambda h, hh=hh, b=bank: h.tensor_tensor(
                    out=tmpS[:], in0=S_f[:, hh * 256:(hh + 1) * 256], in1=b[:, 0:256],
                    op=OP.add))
                done_bank(bi, ["VE"])
                p.wait("vector", "VE")
                ve(lambda h, hh=hh, ci=ci: h.tensor_scalar(
                    out=S_f[:, hh * 256:(hh + 1) * 256], in0=tmpS[:],
                    scalar1=dl_c[:, ci * H + hh:ci * H + hh + 1], scalar2=None,
                    op0=OP.mult, op1=OP.bypass))
            p.wait("vector", "VE")
            ve(lambda h: h.tensor_copy(out=S_bf[:], in_=S_f[:]))
            body_pe[ci] = p.val("PE")
            body_ac[ci] = p.val("AC")

        def chunk_tail(ci):
            ob = o_bf2[ci % 2]
            p.wait("vector", "AC", body_ac[ci])
            ve(lambda h, ob=ob: h.tensor_reduce(
                out=st8["s1"][:], in_=ob[:].rearrange("p (h w) -> p h w", h=H),
                axis=AX.X, op=OP.add))
            for hh in range(H):
                p.wait("scalar", "AC")
                ac(lambda h, hh=hh, ob=ob: h.activation(
                    out=trash[:, 0:256], in_=ob[:, hh * 256:(hh + 1) * 256],
                    func=AF.Square, accum_out=st8["s2"][:, hh:hh + 1]))
            p.wait("vector", "AC")
            p.wait("vector", "VE")
            ve(lambda h: h.tensor_scalar(out=st8["m"][:], in0=st8["s1"][:],
                                         scalar1=1.0 / DV, scalar2=None,
                                         op0=OP.mult, op1=OP.bypass))
            p.wait("vector", "VE")
            ve(lambda h: h.tensor_tensor(out=st8["ms"][:], in0=st8["m"][:],
                                         in1=st8["m"][:], op=OP.mult))
            p.wait("vector", "VE")
            ve(lambda h: h.tensor_scalar(out=st8["v"][:], in0=st8["s2"][:],
                                         scalar1=1.0 / DV, scalar2=None,
                                         op0=OP.mult, op1=OP.bypass))
            p.wait("vector", "VE")
            ve(lambda h: h.tensor_tensor(out=st8["v"][:], in0=st8["v"][:],
                                         in1=st8["ms"][:], op=OP.subtract))
            p.wait("scalar", "VE")
            ac(lambda h: h.activation(out=st8["r"][:], in_=st8["v"][:],
                                      func=AF.Ln, bias=EPS))
            p.wait("scalar", "AC")
            ac(lambda h: h.activation(out=st8["r"][:], in_=st8["r"][:],
                                      func=AF.Exp, scale=-0.5))
            p.wait("vector", "AC")
            for hh in range(H):
                ve(lambda h, hh=hh, ob=ob: h.tensor_scalar(
                    out=ob[:, hh * 256:(hh + 1) * 256],
                    in0=ob[:, hh * 256:(hh + 1) * 256],
                    scalar1=st8["m"][:, hh:hh + 1], scalar2=st8["r"][:, hh:hh + 1],
                    op0=OP.subtract, op1=OP.mult))
            p.wait("sync", "VE")
            p.wait("sync", "TR", p.val("TR"))
            dmat(o_fm[:].rearrange("p (j r) -> p j r", j=16), ob[:])
            tail_tr[ci] = p.val("TR")
            p.wait("vector", "TR")
            ve(lambda h, ci=ci: h.tensor_tensor(
                out=P_t[:, :, ci * CH:(ci + 1) * CH],
                in0=o_fm[:].rearrange("p (j r) -> p j r", j=16),
                in1=gsilu[:, :, ci * CH:(ci + 1) * CH], op=OP.mult))
            tail_ve[ci] = p.val("VE")

        chunk_prep(0)
        for ci in range(CPG):
            chunk_body(ci)
            if ci + 1 < CPG:
                chunk_prep(ci + 1)
            if own:
                chunk_tail(ci)
            s_step(2)

        if not own or upto == "chunks":
            return

        p.sem("WOL0"); p.sem("WOL1")
        p.wait("tensor", "VE")
        wo_bufs = [xlnT, gsilu]
        wo_pe = [None] * 4
        wov = [None] * 4

        def wo_load(b):
            buf = wo_bufs[b % 2]
            if b < 2:
                p.wait("gpsimd", "PE", p.val("PE"))
                p.wait("gpsimd", "TR", p.val("TR"))
                if b == 1:
                    p.wait("gpsimd", "VE", p.val("VE"))
            else:
                p.wait("gpsimd", "PE", wo_pe[b - 2])
            dma(buf[:], wo[b * 128:(b + 1) * 128, :].rearrange("p (j c) -> p j c", j=16),
                sem=f"WOL{b % 2}")
            wov[b] = p.val(f"WOL{b % 2}")

        wo_load(0)
        wo_load(1)
        for b in range(4):
            buf = wo_bufs[b % 2]
            p.wait("tensor", f"WOL{b % 2}", wov[b])
            for ci in range(CPG):
                bi = ci % 4
                bank = take_bank(bi)
                for j in range(16):
                    pe(lambda h, j=j, ci=ci, b2=bank, bf=buf, st=(j == 0): h.matmul(
                        out=b2[:, 0:512], lhsT=P_t[:, j, ci * CH:(ci + 1) * CH],
                        rhs=bf[:, j, :], start=st, stop=(j == 15)))
                p.wait("scalar", "PE")
                ac(lambda h, ci=ci, b=b, b2=bank: h.activation(
                    out=attn_o[:, ci, b * 512:(b + 1) * 512], in_=b2[:, 0:512],
                    func=AF.Copy))
                done_bank(bi, ["AC"])
            wo_pe[b] = p.val("PE")
            if b + 2 < 4:
                wo_load(b + 2)
            s_step(2)
        p.wait("vector", "AC")
        ln_stats(lambda ci: attn_o[:, ci, :], st4, CPG)
        p.wait("vector", "AC")
        for ci in range(CPG):
            p.wait("vector", "VE")
            ve(lambda h, ci=ci: h.tensor_scalar(
                out=xln_b[:, ci % 2, :], in0=attn_o[:, ci, :],
                scalar1=st4["m"][:, ci:ci + 1], scalar2=st4["r"][:, ci:ci + 1],
                op0=OP.subtract, op1=OP.mult))
            p.wait("vector", "VE")
            ve(lambda h, ci=ci, gi=gi: h.tensor_tensor(
                out=x_res[:, gi * CPG + ci, :], in0=x_res[:, gi * CPG + ci, :],
                in1=xln_b[:, ci % 2, :], op=OP.add))

    with contextlib.ExitStack() as pre_stack:
        xpre_b = pre_stack.enter_context(nc.sbuf_tensor("xpre_b", [128, 2, 2, D], BF16))
        for gi in range(NG):
            attn_group(gi, own=False, xpre_b=xpre_b)
        p.barrier("PREDONE")
    if PHASE_LIMIT == "prefix":
        p.wait("gpsimd", "VE")
        dma(y_out.rearrange("(c p) f -> p c f", p=128), x_res[:], sem="ST")
        p.wait("gpsimd", "ST", p.val("ST"))
        own_stack.close()
        attn_stack.close()
        p.emit(top)
        top.close()
        return nc
    q_fm = OA("q_fm", [128, H, GA], BF16)
    gscr = OA("gscr", [128, 4, 512], BF16)
    gsilu = OA("gsilu", [128, 16, GA], BF16)
    P_t = OA("P_t", [128, 16, GA], BF16)
    attn_o = v_tm
    _upto = {"ownproj": "proj", "ownchunks": "chunks"}.get(PHASE_LIMIT, "full")
    for gi in range(1 if PHASE_LIMIT in ("own1", "ownproj", "ownchunks") else NG):
        attn_group(gi, own=True, q_fm=q_fm, gscr=gscr, gsilu=gsilu, P_t=P_t, attn_o=attn_o,
                   upto=_upto)
    if PHASE_LIMIT in ("own1", "ownproj", "ownchunks"):
        p.wait("gpsimd", "VE")
        dma(y_out.rearrange("(c p) f -> p c f", p=128), x_res[:], sem="ST")
        p.wait("gpsimd", "ST", p.val("ST"))
        own_stack.close()
        attn_stack.close()
        p.emit(top)
        top.close()
        return nc
    while s_ab[0] < len(s_plan):
        s_step(4)
    own_stack.close()
    attn_stack.close()
    p.barrier("ATTNDONE")
    if "xattn" in DEBUG_TAPS:
        tap_xattn = nc.dram_tensor("tap_xattn", [TOK, D], F32, kind="ExternalOutput").ap()
        dma(tap_xattn.rearrange("(c p) f -> p c f", p=128), x_res[:], sem="ST")
    if PHASE_LIMIT == "attn":
        p.wait("gpsimd", "VE")
        dma(y_out.rearrange("(c p) f -> p c f", p=128), x_res[:], sem="ST")
        p.wait("gpsimd", "ST", p.val("ST"))
        p.emit(top)
        top.close()
        return nc

    ffn_stack = contextlib.ExitStack()
    FA = lambda name, shape, dt: ffn_stack.enter_context(nc.sbuf_tensor(name, shape, dt))
    h128_t = FA("h128_t", [128, 128], BF16)
    idf_t = FA("idf_t", [128, 128], F32)
    dma(h128_t[:], h128)
    dma(idf_t[:], idf)
    p.wait("tensor", "LD")
    p.wait("vector", "LD")
    P2 = FA("P2", [128, 64, 512], FP16)
    bsp = FA("bsp", [128, 8, 512], FP16)
    rows = FA("rows", [1, 2, 512], F32)
    scale_bc = FA("scale_bc", [128, 512], FP16)
    bc_g = FA("bc_g", [128, 512], FP16)
    bc_u = FA("bc_u", [128, 512], FP16)
    wfb = [FA(f"wfb{i}", [128, 16, 128], F32) for i in range(3)]
    tern_s = [FA(f"tern_s{i}", [128, 16, 128], BF16) for i in range(2)]
    aq_s = FA("aq_s", [128, 2048], BF16)
    aq_s2 = FA("aq_s2", [128, 2048], BF16)
    aq2 = [aq_s, aq_s2]
    dfm = [FA(f"dfm{i}", [128, 512], FP16) for i in range(2)]
    tm_buf = FA("tm_buf", [128, CPG, D], FP16)
    y16 = tm_buf[:].rearrange("p c (j f) -> p (c j) f", j=4)
    fst = {n: FA("fst_" + n, [128, CPG], F32) for n in ("s1", "s2", "m", "ms", "v", "r")}
    amax4 = fst["ms"]
    g_scr = [FA(f"g_scr{i}", [128, 512], FP16) for i in range(2)]
    gs_scr = [FA("gs_scr0", [128, 512], FP16)] * 2
    up_scr = [FA(f"up_scr{i}", [128, 512], FP16) for i in range(2)]
    sg_scr = [FA("sg_scr0", [128, 512], FP16)] * 2

    for i in range(3):
        p.sem(f"WFL{i}")
    for i in range(2):
        p.sem(f"STC{i}")
    p.sem("PO")

    def w_quant(ws, ts, aslot, m, wflv, aq_guard, tern_guards):
        p.wait("scalar", f"WFL{ws}", wflv)
        for sm, vv in tern_guards:
            p.wait("scalar", sm, vv)
        if aq_guard is not None:
            p.wait("scalar", "VE", aq_guard)
        ac(lambda h, ws=ws, ts=ts, m=m: h.activation(
            out=tern_s[ts][:].rearrange("p j c -> p (j c)"),
            in_=wfb[ws][:].rearrange("p j c -> p (j c)"),
            func=AF.Sign, bias=nshalf[:, m:m + 1]))
        ac(lambda h, ws=ws, a=aslot, m=m: h.activation(
            out=aq2[a][:, 0:2048],
            in_=wfb[ws][:].rearrange("p j c -> p (j c)"),
            func=AF.Sign, bias=shalf[:, m:m + 1]))
        acv = p.val("AC")
        p.wait("vector", "AC", acv)
        ve(lambda h, ts=ts, a=aslot: h.tensor_tensor(
            out=tern_s[ts][:].rearrange("p j c -> p (j c)"),
            in0=tern_s[ts][:].rearrange("p j c -> p (j c)"),
            in1=aq2[a][:, 0:2048], op=OP.add))
        return acv, p.val("VE")

    def fwht_stage2(tiles, nbits, wave=8):
        p.wait("vector", "VE")
        p.wait("vector", "PE")
        p.wait("vector", "AC")
        p.wait("gpsimd", "VE")
        p.wait("gpsimd", "PE")
        p.wait("gpsimd", "AC")
        n = len(tiles) - 8
        idx = list(range(n))
        spares = list(range(n, n + 8))
        for bit in range(nbits):
            pairs = [(a, a | (1 << bit)) for a in range(n) if not (a & (1 << bit))]
            for w0 in range(0, len(pairs), wave):
                wv = pairs[w0:w0 + wave]
                npool = 2 if len(wv) >= 8 else 0
                for wi, (a, b) in enumerate(wv):
                    fn = lambda h, a=idx[a], b=idx[b], s=spares[wi]: h.tensor_tensor(
                        out=tiles[s], in0=tiles[a], in1=tiles[b], op=OP.add)
                    if wi < npool:
                        p.op("gpsimd", fn, [("PO", 1)])
                    else:
                        ve(fn)
                p.wait("vector", "VE")
                p.wait("gpsimd", "PO")
                for wi, (a, b) in enumerate(wv):
                    fn = lambda h, a=idx[a], b=idx[b]: h.tensor_tensor(
                        out=tiles[b], in0=tiles[a], in1=tiles[b], op=OP.subtract)
                    if wi < npool:
                        p.op("gpsimd", fn, [("PO", 1)])
                    else:
                        ve(fn)
                p.wait("vector", "VE")
                p.wait("vector", "PO")
                p.wait("gpsimd", "PO")
                p.wait("gpsimd", "VE")
                for wi, (a, b) in enumerate(wv):
                    freed = idx[a]
                    idx[a] = spares[wi]
                    spares[wi] = freed
        return idx, spares

    def absmax_scale(tiles, idx, spares, n, cnorm, rowslot):
        p.wait("vector", "PO")
        nm = 8
        for i in range(nm):
            ve(lambda h, a=idx[i], s=spares[i]: h.tensor_scalar(
                out=tiles[s], in0=tiles[a], scalar1=0.0, scalar2=None,
                op0=OP.min, op1=OP.bypass))
        p.wait("vector", "VE")
        for i in range(nm):
            ve(lambda h, a=idx[i], s=spares[i]: h.scalar_tensor_tensor(
                out=tiles[s], in0=tiles[a], scalar=0.0, in1=tiles[s],
                op0=OP.max, op1=OP.subtract))
        p.wait("vector", "VE")
        pos = nm
        while pos < n:
            for i in range(nm):
                if pos + i < n:
                    ve(lambda h, s=spares[i], a=idx[pos + i]: h.scalar_tensor_tensor(
                        out=tiles[s], in0=tiles[a], scalar=-1.0, in1=tiles[s],
                        op0=OP.mult, op1=OP.max))
            p.wait("vector", "VE")
            for i in range(nm):
                if pos + i < n:
                    ve(lambda h, s=spares[i], a=idx[pos + i]: h.scalar_tensor_tensor(
                        out=tiles[s], in0=tiles[a], scalar=1.0, in1=tiles[s],
                        op0=OP.mult, op1=OP.max))
            p.wait("vector", "VE")
            pos += nm
        m = nm
        while m > 1:
            for i in range(m // 2):
                ve(lambda h, s=spares[i], a=spares[i + m // 2]: h.tensor_tensor(
                    out=tiles[s], in0=tiles[s], in1=tiles[a], op=OP.max))
            p.wait("vector", "VE")
            m //= 2
        p.wait("sync", "VE")
        p.wait("sync", "TR", p.val("TR"))
        mt_v = tiles[spares[1]].rearrange("p (j r) -> p j r", j=4)
        dmat(mt_v, tiles[spares[0]])
        p.wait("vector", "TR")
        ve(lambda h, mt_v=mt_v: h.tensor_reduce(out=amax4[:], in_=mt_v, axis=AX.X, op=OP.max))
        for c in range(CPG):
            b = take_bank(6)
            p.wait("tensor", "VE")
            pe(lambda h, c=c, b=b: h.transpose(out=b[0:1, 0:128], in_=amax4[:, c:c + 1],
                                               identity=idf_t[:]))
            p.wait("vector", "PE")
            ve(lambda h, c=c, b=b, rs=rowslot: h.tensor_scalar(
                out=rows[0:1, rs, c * 128:(c + 1) * 128], in0=b[0:1, 0:128],
                scalar1=cnorm, scalar2=EPS, op0=OP.mult, op1=OP.max))
            done_bank(6, ["VE"])
        p.wait("vector", "VE")
        ve(lambda h, rs=rowslot: h.reciprocal(out=srow_rec[0:1, :], in_=rows[0:1, rs, :]))
        p.wait("vector", "VE")
        ve(lambda h, cn=cnorm: h.tensor_scalar(out=srow_rec[0:1, :], in0=srow_rec[0:1, :],
                                               scalar1=127.0 * cn, scalar2=None,
                                               op0=OP.mult, op1=OP.bypass))
        p.wait("tensor", "VE")
        b = take_bank(6)
        pe(lambda h, b=b: h.matmul(out=b[:, 0:512], lhsT=onesf_t[:], rhs=srow_rec[0:1, :],
                                   start=True, stop=True))
        p.wait("vector", "PE")
        ve(lambda h, b=b: h.tensor_copy(out=scale_bc[:], in_=b[:, 0:512]))
        done_bank(6, ["VE"])

    srow_rec = rows[:, 1, :]
    qsc = {
        "t": [FA(f"qt{i}", [128, 512], F32) for i in range(2)],
        "p": [FA(f"qp{i}", [128, 512], BF16) for i in range(2)],
        "a": [FA(f"qa{i}", [128, 512], BF16) for i in range(2)],
        "v": [FA(f"qv{i}", [128, 512], BF16) for i in range(2)],
    }

    def quant_tiles(tiles, idx, n, marks=None):
        for w0 in range(0, n, 2):
            wv = list(range(w0, min(w0 + 2, n)))
            p.wait("vector", "VE")
            p.wait("vector", "AC")
            for wi, i in enumerate(wv):
                ve(lambda h, a=idx[i], wi=wi: h.tensor_tensor(
                    out=qsc["t"][wi][:], in0=tiles[a], in1=scale_bc[:], op=OP.mult))
            p.wait("scalar", "VE")
            for wi, i in enumerate(wv):
                ac(lambda h, wi=wi: h.activation(out=qsc["a"][wi][:], in_=qsc["t"][wi][:],
                                                 func=AF.Copy, scale=S29, bias=C23))
            p.wait("scalar", "AC")
            for wi, i in enumerate(wv):
                ac(lambda h, wi=wi: h.activation(out=qsc["t"][wi][:], in_=qsc["t"][wi][:],
                                                 func=AF.Copy, bias=C23))
            p.wait("vector", "AC")
            for wi, i in enumerate(wv):
                ve(lambda h, wi=wi: h.tensor_scalar(
                    out=qsc["v"][wi][:], in0=qsc["a"][wi][:], scalar1=-C23, scalar2=B29,
                    op0=OP.add, op1=OP.mult))
                ve(lambda h, wi=wi: h.tensor_scalar(
                    out=qsc["p"][wi][:], in0=qsc["t"][wi][:], scalar1=yNY_c[:],
                    scalar2=col2_c[:], op0=OP.mult, op1=OP.subtract))
            p.wait("vector", "VE")
            for wi, i in enumerate(wv):
                ve(lambda h, a=idx[i], wi=wi: h.tensor_tensor(
                    out=tiles[a], in0=qsc["p"][wi][:], in1=qsc["v"][wi][:], op=OP.add))
            if marks is not None and (w0 + 2) % 16 == 0:
                marks[(w0 + 2) // 16 - 1] = p.val("VE")
        p.wait("vector", "VE")

    def ffn_group(gi, first):
        p.barrier(f"FG{gi}")
        cbase = gi * CPG
        src = lambda ci: x_res[:, cbase + ci, :]
        ln_stats(src, fst, CPG)
        p.wait("vector", "AC")
        for ci in range(CPG):
            p.wait("vector", "VE")
            ve(lambda h, ci=ci: h.tensor_scalar(
                out=tm_buf[:, ci, :], in0=src(ci),
                scalar1=fst["m"][:, ci:ci + 1], scalar2=fst["r"][:, ci:ci + 1],
                op0=OP.subtract, op1=OP.mult))
            p.wait("sync", "VE")
            dmat(xlnT_f[:, :, ci * CH:(ci + 1) * CH], tm_buf[:, ci, :])
        p.wait("tensor", "TR")
        p.wait("scalar", "TR")
        for j in range(16):
            bi = j % 4
            bank = take_bank(bi)
            pe(lambda h, j=j, b=bank: h.matmul(out=b[:, 0:512], lhsT=h128_t[:],
                                               rhs=xlnT_f[:, j, :], start=True, stop=True))
            p.wait("scalar", "PE")
            ac(lambda h, j=j, b=bank: h.activation(out=y16[:, j, :], in_=b[:, 0:512],
                                                   func=AF.Copy))
            done_bank(bi, ["AC"])
        tiles2 = [y16[:, j, :] for j in range(16)] + [bsp[:, i, :] for i in range(8)]
        idx2, sp2 = fwht_stage2(tiles2, 4)
        absmax_scale(tiles2, idx2, sp2, 16, CW2, 0)

        p.wait("vector", "VE")
        ve(lambda h: h.tensor_scalar(out=rows[0:1, 1, :], in0=rows[0:1, 0, :],
                                     scalar1=s_sc[0:1, 0:1], scalar2=0.5 / 127.0,
                                     op0=OP.mult, op1=OP.mult))
        p.wait("tensor", "VE")
        b = take_bank(6)
        pe(lambda h, b=b: h.matmul(out=b[:, 0:512], lhsT=onesf_t[:], rhs=rows[0:1, 1, :],
                                   start=True, stop=True))
        p.wait("vector", "PE")
        ve(lambda h, b=b: h.tensor_copy(out=bc_g[:], in_=b[:, 0:512]))
        done_bank(6, ["VE"])
        p.wait("vector", "VE")
        ve(lambda h: h.tensor_scalar(out=rows[0:1, 1, :], in0=rows[0:1, 0, :],
                                     scalar1=s_sc[0:1, 1:2], scalar2=0.5 / 127.0,
                                     op0=OP.mult, op1=OP.mult))
        p.wait("tensor", "VE")
        b = take_bank(6)
        pe(lambda h, b=b: h.matmul(out=b[:, 0:512], lhsT=onesf_t[:], rhs=rows[0:1, 1, :],
                                   start=True, stop=True))
        p.wait("vector", "PE")
        ve(lambda h, b=b: h.tensor_copy(out=bc_u[:], in_=b[:, 0:512]))
        done_bank(6, ["VE"])
        p.wait("vector", "VE")

        NU = 128
        u_src = lambda n: (wgt if n % 2 == 0 else wut)
        u_cache = lambda n: (tern_g if n % 2 == 0 else tern_u)
        q_ve = [None] * NU
        q_ac = [None] * NU
        pe_aft = [None] * NU
        wfl_v = [None] * NU
        stc_v = [None] * NU
        gu_banks = {}

        def gu_load(n):
            if first:
                ws = n % 3
                if n >= 3:
                    p.wait("sync", "AC", q_ac[n - 3])
                src = u_src(n)[(n // 2) * 128:(n // 2 + 1) * 128, :]
                p.op("sync", lambda h, s=src, sl=ws: h.dma_start(
                    out=wfb[sl][:].rearrange("p j c -> p (j c)"), in_=s),
                    [(f"WFL{ws}", 16)])
                wfl_v[n] = p.val(f"WFL{ws}")
            else:
                ts = n % 2
                if n >= 2:
                    p.wait("sync", "PE", pe_aft[n - 2])
                src = u_cache(n)[(n // 2) * 128:(n // 2 + 1) * 128, :]
                p.op("sync", lambda h, s=src, sl=ts: h.dma_start(
                    out=tern_s[sl][:].rearrange("p j c -> p (j c)"), in_=s),
                    [(f"WFL{ts}", 16)])
                wfl_v[n] = p.val(f"WFL{ts}")

        def gu_quant(n):
            ts = n % 2
            guards = ([("PE", pe_aft[n - 2]), (f"STC{ts}", stc_v[n - 2])]
                      if n >= 2 else [])
            q_ac[n], q_ve[n] = w_quant(
                n % 3, ts, n % 2, n % 2, wfl_v[n],
                q_ve[n - 2] if n >= 2 else None, guards)

        def gu_store(n):
            ts = n % 2
            p.wait("gpsimd", "VE", q_ve[n])
            p.op("gpsimd", lambda h, sl=ts, n=n: h.dma_start(
                out=u_cache(n)[(n // 2) * 128:(n // 2 + 1) * 128, :],
                in_=tern_s[sl][:].rearrange("p j c -> p (j c)")), [(f"STC{ts}", 16)])
            stc_v[n] = p.val(f"STC{ts}")

        def gu_mm(n):
            ts = n % 2
            if first:
                p.wait("tensor", "VE", q_ve[n])
            else:
                p.wait("tensor", f"WFL{ts}", wfl_v[n])
            bi = n % 6
            bank = take_bank(bi)
            for j in range(16):
                pe(lambda h, sl=ts, j=j, b=bank, st=(j == 0): h.matmul(
                    out=b[:, 0:512], lhsT=tern_s[sl][:, j, :], rhs=tiles2[idx2[j]],
                    start=st, stop=(j == 15)))
            pe_aft[n] = p.val("PE")
            gu_banks[n] = (bi, bank)

        def gu_evac(oc):
            sl = oc % 2
            big, bg_ = gu_banks.pop(2 * oc)
            biu, bu_ = gu_banks.pop(2 * oc + 1)
            p.wait("vector", "PE", pe_aft[2 * oc])
            ve(lambda h, b=bg_, sl=sl: h.tensor_tensor(out=g_scr[sl][:], in0=b[:, 0:512],
                                                       in1=bc_g[:], op=OP.mult))
            done_bank(big, ["VE"])
            p.wait("scalar", "VE")
            ac(lambda h, sl=sl: h.activation(out=gs_scr[sl][:], in_=g_scr[sl][:],
                                             func=AF.Sigmoid))
            p.wait("vector", "PE", pe_aft[2 * oc + 1])
            ve(lambda h, b=bu_, sl=sl: h.tensor_tensor(out=up_scr[sl][:], in0=b[:, 0:512],
                                                       in1=bc_u[:], op=OP.mult))
            done_bank(biu, ["VE"])
            p.wait("vector", "AC")
            ve(lambda h, sl=sl: h.tensor_tensor(out=sg_scr[sl][:], in0=g_scr[sl][:],
                                                in1=gs_scr[sl][:], op=OP.mult))
            p.wait("vector", "VE")
            ve(lambda h, oc=oc, sl=sl: h.tensor_tensor(out=P2[:, oc, :], in0=sg_scr[sl][:],
                                                       in1=up_scr[sl][:], op=OP.mult))

        look = 3 if first else 2
        for i in range(look):
            gu_load(i)
        quant_tiles(tiles2, idx2, 16)
        p.wait("tensor", "VE")
        if first:
            gu_quant(0)
        for n in range(NU):
            if first and n + 1 < NU:
                gu_quant(n + 1)
            gu_mm(n)
            if first:
                gu_store(n)
            if n + look < NU:
                gu_load(n + look)
            if n % 2 == 1:
                gu_evac(n // 2)

        p.wait("tensor", "VE")
        for j in range(64):
            bi = j % 4
            bank = take_bank(bi)
            pe(lambda h, j=j, b=bank: h.matmul(out=b[:, 0:512], lhsT=h128_t[:],
                                               rhs=P2[:, j, :], start=True, stop=True))
            p.wait("scalar", "PE")
            ac(lambda h, j=j, b=bank: h.activation(out=P2[:, j, :], in_=b[:, 0:512],
                                                   func=AF.Copy))
            done_bank(bi, ["AC"])
        tiles8 = [P2[:, j, :] for j in range(64)] + [bsp[:, i, :] for i in range(8)]
        idx8, sp8 = fwht_stage2(tiles8, 6)
        absmax_scale(tiles8, idx8, sp8, 64, CW8, 0)

        p.wait("vector", "VE")
        ve(lambda h: h.tensor_scalar(out=rows[0:1, 1, :], in0=rows[0:1, 0, :],
                                     scalar1=s_sc[0:1, 2:3], scalar2=0.5 / 127.0,
                                     op0=OP.mult, op1=OP.mult))
        p.wait("tensor", "VE")
        b = take_bank(6)
        pe(lambda h, b=b: h.matmul(out=b[:, 0:512], lhsT=onesf_t[:], rhs=rows[0:1, 1, :],
                                   start=True, stop=True))
        p.wait("vector", "PE")
        ve(lambda h, b=b: h.tensor_copy(out=bc_g[:], in_=b[:, 0:512]))
        done_bank(6, ["VE"])
        p.wait("vector", "VE")
        ND = 64
        q_ve_d = [None] * ND
        q_ac_d = [None] * ND
        pe_d = [None] * ND
        wfl_d = [None] * ND
        stc_d = [None] * ND
        tr_d = [None] * 16
        dn_banks = {}
        last_gu = {0: 126, 1: 127, 2: 125}

        def dn_load(m):
            if first:
                ws = m % 3
                p.wait("scalar", "AC", q_ac_d[m - 3] if m >= 3 else q_ac[last_gu[ws]])
                src = wdt[m * 128:(m + 1) * 128, :]
                p.op("scalar", lambda h, s=src, sl=ws: h.dma_start(
                    out=wfb[sl][:].rearrange("p j c -> p (j c)"), in_=s),
                    [(f"WFL{ws}", 16)])
                wfl_d[m] = p.val(f"WFL{ws}")
            else:
                ts = m % 2
                p.wait("scalar", "PE", pe_d[m - 2] if m >= 2 else pe_aft[NU - 2 + m])
                src = tern_d[m * 128:(m + 1) * 128, :]
                p.op("scalar", lambda h, s=src, sl=ts: h.dma_start(
                    out=tern_s[sl][:].rearrange("p j c -> p (j c)"), in_=s),
                    [(f"WFL{ts}", 16)])
                wfl_d[m] = p.val(f"WFL{ts}")

        def dn_quant(m):
            ts = m % 2
            guards = [("PE", pe_d[m - 2] if m >= 2 else pe_aft[NU - 2 + m]),
                      (f"STC{ts}", stc_d[m - 2] if m >= 2 else stc_v[NU - 2 + m])]
            q_ac_d[m], q_ve_d[m] = w_quant(
                m % 3, ts, m % 2, 2, wfl_d[m],
                q_ve_d[m - 2] if m >= 2 else q_ve[NU - 2 + m], guards)

        def dn_store(m):
            slot = m % 2
            p.wait("gpsimd", "VE", q_ve_d[m])
            p.op("gpsimd", lambda h, sl=slot, m=m: h.dma_start(
                out=tern_d[m * 128:(m + 1) * 128, :],
                in_=tern_s[sl][:].rearrange("p j c -> p (j c)")), [(f"STC{slot}", 16)])
            stc_d[m] = p.val(f"STC{slot}")

        def dn_mm(m):
            slot = m % 2
            oc, qtr = m // 4, m % 4
            if first:
                p.wait("tensor", "VE", q_ve_d[m])
            else:
                p.wait("tensor", f"WFL{m % 2}", wfl_d[m])
                p.wait("tensor", "VE", qmark[qtr])
            bi = oc % 6
            if qtr == 0:
                dn_banks[oc] = (bi, take_bank(bi))
            bank = dn_banks[oc][1]
            for j in range(16):
                pe(lambda h, sl=slot, j=j, jg=qtr * 16 + j, b=bank,
                   st=(qtr == 0 and j == 0), sp=(qtr == 3 and j == 15): h.matmul(
                    out=b[:, 0:512], lhsT=tern_s[sl][:, j, :], rhs=tiles8[idx8[jg]],
                    start=st, stop=sp))
            pe_d[m] = p.val("PE")

        def dn_evac(oc):
            sl = oc % 2
            bi, bank = dn_banks.pop(oc)
            p.wait("vector", "PE", pe_d[4 * oc + 3])
            if oc >= 2:
                p.wait("vector", f"TRD{sl}", tr_d[oc - 2])
            ve(lambda h, b=bank, sl=sl: h.tensor_tensor(out=dfm[sl][:], in0=b[:, 0:512],
                                                        in1=bc_g[:], op=OP.mult))
            done_bank(bi, ["VE"])
            p.wait("sync", "VE")
            p.op("sync", lambda h, oc=oc, sl=sl: h.dma_start(
                out=tm_buf[:, :, oc * 128:(oc + 1) * 128], in_=dfm[sl][:],
                transpose=True), [(f"TRD{sl}", 16)])
            tr_d[oc] = p.val(f"TRD{sl}")

        look_d = 3 if first else 2
        for i in range(look_d):
            dn_load(i)
        qmark = [None] * 4
        quant_tiles(tiles8, idx8, 64, marks=qmark)
        if first:
            dn_quant(0)
        for m in range(ND):
            if first and m + 1 < ND:
                dn_quant(m + 1)
            dn_mm(m)
            if first:
                dn_store(m)
            if m + look_d < ND:
                dn_load(m + look_d)
            if m % 4 == 3:
                dn_evac(m // 4)
        for e in ("vector", "scalar"):
            p.wait(e, "TR")
            p.wait(e, "TRD0")
            p.wait(e, "TRD1")
        ln_stats(lambda ci: tm_buf[:, ci, :], fst, CPG)
        p.wait("vector", "AC")
        for ci in range(CPG):
            p.wait("vector", "VE")
            ve(lambda h, ci=ci: h.tensor_scalar(
                out=aq_s[:, 0:D], in0=tm_buf[:, ci, :],
                scalar1=fst["m"][:, ci:ci + 1], scalar2=fst["r"][:, ci:ci + 1],
                op0=OP.subtract, op1=OP.mult))
            p.wait("vector", "VE")
            ve(lambda h, ci=ci, cb=cbase: h.tensor_tensor(
                out=x_res[:, cb + ci, :], in0=x_res[:, cb + ci, :],
                in1=aq_s[:, 0:D], op=OP.add))
        p.wait("gpsimd", "VE")
        dma(y_out.rearrange("(c p) f -> p c f", p=128)[:, cbase:cbase + CPG, :],
            x_res[:, cbase:cbase + CPG, :], sem="ST")

    xlnT_f = P2[:, 0:16, :]
    for gi in range(TOK // GA):
        ffn_group(gi, first=(gi == 0))

    p.wait("gpsimd", "ST", p.val("ST"))
    ffn_stack.close()

    p.emit(top)
    top.close()
    return nc


_NC_CACHE = {}


def stage_shared(inp):
    f = lambda a: np.ascontiguousarray(np.asarray(a, dtype=np.float32))
    jj, ii = np.meshgrid(np.arange(128), np.arange(128), indexing="ij")
    return dict(
        wq=tile_cols(f(inp["L0_attn_Wq"])),
        wk=tile_cols(f(inp["L0_attn_Wk"])),
        wkg1=np.ascontiguousarray(
            f(inp["L0_attn_Wkg1"]).reshape(16, 128, 16).transpose(1, 0, 2).reshape(128, 256)),
        wkg2=f(inp["L0_attn_Wkg2"]),
        bkg2=np.ascontiguousarray(f(inp["L0_attn_bkg2"]).reshape(8, 128).T),
        wv=tile_cols(f(inp["L0_attn_Wv"])),
        wg=tile_cols(f(inp["L0_attn_Wg"])),
        bg=np.ascontiguousarray(f(inp["L0_attn_bg"]).reshape(16, 128).T),
        wo=np.ascontiguousarray(
            f(inp["L0_attn_Wo"]).reshape(16, 128, 4, 512).transpose(2, 1, 0, 3)
            .reshape(512, 8192)),
        wgt=tile_cols(f(inp["L0_ffn_w_gate"]).T),
        wut=tile_cols(f(inp["L0_ffn_w_up"]).T),
        wdt=np.ascontiguousarray(
            f(inp["L0_ffn_w_down"]).T.reshape(4, 16, 128, 16, 128).transpose(3, 0, 2, 1, 4)
            .reshape(8192, 2048)),
        h128=hadamard(128),
        idf=np.eye(128, dtype=np.float32),
        maskt=(jj <= ii).astype(np.float32),
        onesf=np.ones((1, 128), np.float32),
        onecol=np.ones((128, 1), np.float32),
    )


def kernel(L0_attn_x, L0_attn_Wq, L0_attn_Wk, L0_attn_Wkg1, L0_attn_Wkg2, L0_attn_bkg2,
           L0_attn_Wv, L0_attn_Wg, L0_attn_bg, L0_attn_Wo,
           L0_ffn_w_gate, L0_ffn_w_up, L0_ffn_w_down):
    x = np.ascontiguousarray(np.asarray(L0_attn_x, dtype=np.float32))
    shared = stage_shared(dict(
        L0_attn_Wq=L0_attn_Wq, L0_attn_Wk=L0_attn_Wk, L0_attn_Wkg1=L0_attn_Wkg1,
        L0_attn_Wkg2=L0_attn_Wkg2, L0_attn_bkg2=L0_attn_bkg2, L0_attn_Wv=L0_attn_Wv,
        L0_attn_Wg=L0_attn_Wg, L0_attn_bg=L0_attn_bg, L0_attn_Wo=L0_attn_Wo,
        L0_ffn_w_gate=L0_ffn_w_gate, L0_ffn_w_up=L0_ffn_w_up,
        L0_ffn_w_down=L0_ffn_w_down))
    zeros_pre = np.zeros((TOK, D), np.float32)

    if "nc" not in _NC_CACHE:
        _NC_CACHE["nc"] = build()
    nc = _NC_CACHE["nc"]

    in_maps = []
    for c in range(8):
        seq, half = c // 2, c % 2
        m = dict(shared)
        m["x_own"] = np.ascontiguousarray(x[seq, half * TOK:(half + 1) * TOK, :])
        m["x_pre"] = np.ascontiguousarray(x[seq, 0:TOK, :]) if half == 1 else zeros_pre
        in_maps.append(m)

    res = run_bass_kernel_spmd(nc, in_maps, core_ids=list(range(8)))
    out = np.empty((4, 2048, D), np.float32)
    for c in range(8):
        seq, half = c // 2, c % 2
        out[seq, half * TOK:(half + 1) * TOK, :] = res.results[c]["y"]
    return out


# version 16
# speedup vs baseline: 1.0347x; 1.0273x over previous
import contextlib
import numpy as np
import concourse.bass as bass
import concourse.mybir as mybir
from concourse.bass_utils import run_bass_kernel_spmd

F32 = mybir.dt.float32
BF16 = mybir.dt.bfloat16
FP16 = mybir.dt.float16
AF = mybir.ActivationFunctionType
OP = mybir.AluOpType
AX = mybir.AxisListType

D = 2048
H = 8
DV = 256
CH = 128
TOK = 1024
NCH = TOK // CH
GA = 512
CPG = GA // CH
NG = TOK // GA
I8 = 8192
EPS = 1e-5

C23 = float(np.float32(1.5 * 2 ** 23))
S29 = float(2.0 ** -29)
B29 = float(2.0 ** 29)
LN16 = float(np.log(16.0))
CW2 = float(1.0 / np.sqrt(2048.0))
CW8 = float(1.0 / np.sqrt(8192.0))

ENGS = ("sync", "scalar", "vector", "tensor", "gpsimd")
DEBUG_TAPS = []
PHASE_LIMIT = "full"


def hadamard(n):
    h = np.array([[1.0]], np.float32)
    while h.shape[0] < n:
        h = np.block([[h, h], [h, -h]])
    return np.ascontiguousarray(h, np.float32)


def tile_cols(w):
    di, no = w.shape
    t = w.reshape(di // 128, 128, no // 128, 128).transpose(2, 1, 0, 3).reshape(no, di)
    return np.ascontiguousarray(t)


class P:
    def __init__(self, nc):
        self.nc = nc
        self.q = {e: [] for e in ENGS}
        self.sems = {}
        self.cnt = {}
        self._names = []

    def sem(self, name):
        if name not in self.cnt:
            self._names.append(name)
            self.cnt[name] = 0
        return name

    def op(self, eng, fn, inc=None):
        incs = list(inc or [])
        for s, n in incs:
            self.sem(s)
            self.cnt[s] += n

        def run(h, fn=fn, incs=incs):
            instr = fn(h)
            for s, n in incs:
                instr = instr.then_inc(self.sems[s], n)

        self.q[eng].append(run)

    def wait(self, eng, s, val=None):
        self.sem(s)
        v = self.cnt[s] if val is None else val
        if v <= 0:
            return
        self.q[eng].append(lambda h, s=s, v=v: h.wait_ge(self.sems[s], v))

    def val(self, s):
        return self.cnt[self.sem(s)]

    def barrier(self, name, engines=ENGS):
        s = self.sem(name)
        snap = [(n, v) for n, v in self.cnt.items() if v > 0 and n != s]
        for e in engines:
            for n, v in snap:
                self.wait(e, n, v)
            self.op(e, lambda h: h.nop(), [(s, 1)])
        tot = self.cnt[s]
        for e in engines:
            self.wait(e, s, tot)

    def emit(self, stack):
        for n in self._names:
            self.sems[n] = stack.enter_context(self.nc.semaphore(n))
        block = stack.enter_context(self.nc.Block())
        for e in ENGS:
            ops = self.q[e]

            def body(h, ops=ops):
                for f in ops:
                    f(h)

            getattr(block, e)(body)


def build():
    nc = bass.Bass("TRN2", target_bir_lowering=False, debug=False)
    p = P(nc)
    din = lambda name, shape: nc.dram_tensor(name, shape, F32, kind="ExternalInput").ap()

    x_own = din("x_own", [TOK, D])
    x_pre = din("x_pre", [TOK, D])
    wq = din("wq", [1024, D])
    wk = din("wk", [1024, D])
    wkg1 = din("wkg1", [128, 16 * 16])
    wkg2 = din("wkg2", [16, 1024])
    bkg2 = din("bkg2", [128, 8])
    wv = din("wv", [D, D])
    wg = din("wg", [D, D])
    bg = din("bg", [128, 16])
    wo = din("wo", [4 * 128, 16 * 512])
    wgt = din("wgt", [I8, D])
    wut = din("wut", [I8, D])
    wdt = din("wdt", [I8, D])
    h128 = din("h128", [128, 128])
    idf = din("idf", [128, 128])
    maskt = din("maskt", [128, 128])
    onesf = din("onesf", [1, 128])
    onecol = din("onecol", [128, 1])

    y_out = nc.dram_tensor("y", [TOK, D], F32, kind="ExternalOutput").ap()
    tern_g = nc.dram_tensor("tern_g", [64 * 128, D], BF16, kind="Internal").ap()
    tern_u = nc.dram_tensor("tern_u", [64 * 128, D], BF16, kind="Internal").ap()
    tern_d = nc.dram_tensor("tern_d", [64 * 128, D], BF16, kind="Internal").ap()

    top = contextlib.ExitStack()
    RA = lambda name, shape, dt: top.enter_context(nc.sbuf_tensor(name, shape, dt))

    x_res = RA("x_res", [128, NCH, D], BF16)
    S_f = RA("S_f", [128, H * DV], F32)
    S_bf = RA("S_bf", [128, H * DV], BF16)
    tmpS = RA("tmpS", [128, DV], F32)
    onesf_t = RA("onesf_t", [1, 128], F32)
    onecol_t = RA("onecol_t", [128, 1], F32)
    onecol_b = RA("onecol_b", [128, 1], BF16)
    bkgn_c = RA("bkgn_c", [128, 8], F32)
    bg_c = RA("bg_c", [128, 16], F32)
    yA_c = RA("yA_c", [128, 1], F32)
    yNY_c = RA("yNY_c", [128, 1], F32)
    col2_c = RA("col2_c", [128, 1], F32)
    s_sc = RA("s_sc", [1, 4], F32)
    s_stat = RA("s_stat", [1, 4], F32)
    shalf = RA("shalf", [128, 3], F32)
    nshalf = RA("nshalf", [128, 3], F32)
    trash = RA("trash", [128, D], BF16)
    cst_t = RA("cst_t", [128, 4], F32)

    ps = [top.enter_context(nc.psum_tensor(f"ps{i}", [128, 512], F32)) for i in range(7)]
    psT = top.enter_context(nc.psum_tensor("psT", [128, 1024], BF16))

    for s in ("PE", "VE", "AC", "PO", "TR", "LD", "ST"):
        p.sem(s)

    def dma(out, in_, sem="LD"):
        p.op("gpsimd", lambda h, o=out, i=in_: h.dma_start(out=o, in_=i), [(sem, 16)])

    def dmat(out, in_):
        p.op("sync", lambda h, o=out, i=in_: h.dma_start(out=o, in_=i, transpose=True), [("TR", 16)])

    def ve(fn, inc=1):
        p.op("vector", fn, [("VE", inc)])

    def ac(fn, inc=1):
        p.op("scalar", fn, [("AC", inc)])

    def pe(fn, inc=1):
        p.op("tensor", fn, [("PE", inc)])

    bank_guard = {}

    def take_bank(i):
        g = bank_guard.pop(i, None)
        if g:
            for s, v in g:
                p.wait("tensor", s, v)
        return ps[i]

    def done_bank(i, sems):
        bank_guard[i] = [(s, p.val(s)) for s in sems]

    dma(x_res[:], x_own.rearrange("(c p) f -> p c f", p=128))
    dma(onesf_t[:], onesf)
    dma(onecol_t[:], onecol)
    dma(onecol_b[:], onecol)
    dma(bkgn_c[:], bkg2)
    dma(bg_c[:], bg)
    p.wait("scalar", "LD")
    p.wait("vector", "LD")
    p.wait("tensor", "LD")

    _cvals = [EPS, 1.0, C23, -LN16]
    for _i, _v in enumerate(_cvals):
        ve(lambda h, i=_i, v=_v: h.memset(cst_t[:, i:i + 1], v))
        nc.const_aps.register(F32, float(np.float32(_v)), cst_t[:, _i:_i + 1])
        nc.const_aps.register(F32, float(_v), cst_t[:, _i:_i + 1])
    p.wait("scalar", "VE")
    ve(lambda h: h.tensor_scalar(out=bkgn_c[:], in0=bkgn_c[:], scalar1=-1.0, scalar2=None,
                                 op0=OP.mult, op1=OP.bypass))
    ac(lambda h: h.activation(out=yA_c[:], in_=onecol_t[:], func=AF.Copy, scale=S29, bias=C23))
    p.wait("vector", "AC")
    ve(lambda h: h.tensor_scalar(out=yNY_c[:], in0=yA_c[:], scalar1=-C23, scalar2=B29,
                                 op0=OP.add, op1=OP.mult))
    p.wait("vector", "VE")
    ve(lambda h: h.tensor_scalar(out=yA_c[:], in0=yNY_c[:], scalar1=-1.0, scalar2=1.0,
                                 op0=OP.mult, op1=OP.add))
    p.wait("vector", "VE")
    ve(lambda h: h.tensor_copy(out=yNY_c[:], in_=yA_c[:]))
    p.wait("vector", "VE")
    ve(lambda h: h.tensor_scalar(out=col2_c[:], in0=yNY_c[:], scalar1=C23, scalar2=None,
                                 op0=OP.mult, op1=OP.bypass))
    ve(lambda h: h.memset(S_f[:], 0.0))
    ve(lambda h: h.memset(S_bf[:], 0.0))
    p.wait("tensor", "VE")
    p.wait("gpsimd", "VE")

    def ln_stats(src, stats, n_ch, dmodel=D):
        for e in ("vector", "scalar"):
            for s in ("AC", "VE", "TR"):
                p.wait(e, s)
        for ci in range(n_ch):
            ve(lambda h, ci=ci: h.tensor_reduce(out=stats["s1"][:, ci:ci + 1],
                                                in_=src(ci), axis=AX.X, op=OP.add))
        for ci in range(n_ch):
            if ci > 0:
                p.wait("scalar", "AC")
            ac(lambda h, ci=ci: h.activation(out=trash[:, 0:dmodel], in_=src(ci),
                                             func=AF.Square,
                                             accum_out=stats["s2"][:, ci:ci + 1]))
        p.wait("vector", "AC")
        p.wait("vector", "VE")
        ve(lambda h: h.tensor_scalar(out=stats["m"][:, 0:n_ch], in0=stats["s1"][:, 0:n_ch],
                                     scalar1=1.0 / dmodel, scalar2=None,
                                     op0=OP.mult, op1=OP.bypass))
        p.wait("vector", "VE")
        ve(lambda h: h.tensor_tensor(out=stats["ms"][:, 0:n_ch], in0=stats["m"][:, 0:n_ch],
                                     in1=stats["m"][:, 0:n_ch], op=OP.mult))
        p.wait("vector", "VE")
        ve(lambda h: h.tensor_scalar(out=stats["v"][:, 0:n_ch], in0=stats["s2"][:, 0:n_ch],
                                     scalar1=1.0 / dmodel, scalar2=None,
                                     op0=OP.mult, op1=OP.bypass))
        p.wait("vector", "VE")
        ve(lambda h: h.tensor_tensor(out=stats["v"][:, 0:n_ch], in0=stats["v"][:, 0:n_ch],
                                     in1=stats["ms"][:, 0:n_ch], op=OP.subtract))
        p.wait("scalar", "VE")
        ac(lambda h: h.activation(out=stats["r"][:, 0:n_ch], in_=stats["v"][:, 0:n_ch],
                                  func=AF.Ln, bias=EPS))
        p.wait("scalar", "AC")
        ac(lambda h: h.activation(out=stats["r"][:, 0:n_ch], in_=stats["r"][:, 0:n_ch],
                                  func=AF.Exp, scale=-0.5))

    if PHASE_LIMIT == "init":
        p.wait("gpsimd", "VE")
        dma(y_out.rearrange("(c p) f -> p c f", p=128), x_res[:], sem="ST")
        p.wait("gpsimd", "ST", p.val("ST"))
        p.emit(top)
        top.close()
        return nc

    attn_stack = contextlib.ExitStack()
    AA = lambda name, shape, dt: attn_stack.enter_context(nc.sbuf_tensor(name, shape, dt))
    idb_t = AA("idb_t", [128, 128], BF16)
    mask_t = AA("mask_t", [128, 128], BF16)
    wkg1_t = AA("wkg1_t", [128, 16, 16], BF16)
    wkg2_t = AA("wkg2_t", [16, 1024], BF16)
    dma(idb_t[:], idf)
    dma(mask_t[:], maskt)
    dma(wkg1_t[:], wkg1.rearrange("p (j c) -> p j c", j=16))
    dma(wkg2_t[:], wkg2)
    p.wait("tensor", "LD")
    p.wait("vector", "LD")
    k_fm = AA("k_fm", [128, H, GA], BF16)
    gkln = AA("gkln", [128, H, GA], BF16)
    gkc_c = AA("gkc_c", [128, H, CH], F32)
    pool4 = AA("pool4", [128, 2, 512], BF16)
    e1_c = AA("e1_c", [128, H, CH], BF16)
    e3_c = AA("e3_c", [128, H, CH], BF16)
    dl_c = AA("dl_c", [128, CPG * H], F32)
    v_tm = AA("v_tm", [128, CPG, D], BF16)
    qd_t = AA("qd_t", [128, H, CH], BF16)
    ke3_t = AA("ke3_t", [128, H, CH], BF16)
    ke3m = AA("ke3m", [128, H, CH], BF16)
    amT = AA("amT", [128, H, CH], BF16)
    o_bf = AA("o_bf", [128, D], BF16)
    o_fm = AA("o_fm", [128, D], BF16)
    st4 = {n: AA("st4_" + n, [128, CPG], F32) for n in ("s1", "s2", "m", "ms", "v", "r")}
    st8 = {n: AA("st8_" + n, [128, H], F32) for n in ("s1", "s2", "m", "ms", "v", "r")}
    xlnT = AA("xlnT", [128, 16, GA], BF16)
    xln_b = AA("xln_b", [128, 2, D], BF16)
    wbA = [AA(f"wbA{i}", [128, 16, 128], BF16) for i in range(3)]
    t1_sb = AA("t1_sb", [16, GA], BF16)
    sw_b = [AA(f"sw{i}", [128, 2048], BF16) for i in range(2)]
    o_bfB = AA("o_bfB", [128, D], BF16)
    scol = AA("scol", [128, 192], F32)
    sred = AA("sred", [128, 4], F32)
    own_stack = contextlib.ExitStack()
    OA = lambda name, shape, dt: own_stack.enter_context(nc.sbuf_tensor(name, shape, dt))

    for i in range(2):
        p.sem(f"SWL{i}")
    SUNIT = 128 * 2048
    s_srcs = [wgt.rearrange("a b -> (a b)"), wut.rearrange("a b -> (a b)"),
              wdt.rearrange("a b -> (a b)")]
    NPM = (D * I8) // SUNIT
    s_plan = [(m, off) for m in range(3) for off in range(0, D * I8, SUNIT)]
    s_ld = [0]
    s_ab = [0]
    swl_v = [None] * len(s_plan)
    ac_val = [None] * len(s_plan)

    def s_finalize(m):
        p.wait("vector", "AC")
        ve(lambda h, m=m: h.tensor_reduce(out=sred[:, m:m + 1],
                                          in_=scol[:, m * NPM:(m + 1) * NPM],
                                          axis=AX.X, op=OP.add))
        p.wait("tensor", "VE")
        b = take_bank(6)
        pe(lambda h, m=m, b=b: h.matmul(out=b[0:1, 0:1], lhsT=sred[:, m:m + 1],
                                        rhs=onecol_t[:], start=True, stop=True))
        p.wait("vector", "PE")
        ve(lambda h, m=m, b=b: h.tensor_scalar(out=s_sc[0:1, m:m + 1], in0=b[0:1, 0:1],
                                               scalar1=1.0 / (D * I8), scalar2=EPS,
                                               op0=OP.mult, op1=OP.max))
        done_bank(6, ["VE"])
        p.wait("tensor", "VE")
        b = take_bank(6)
        pe(lambda h, m=m, b=b: h.matmul(out=b[0:128, 0:1], lhsT=onesf_t[:],
                                        rhs=s_sc[0:1, m:m + 1], start=True, stop=True))
        p.wait("vector", "PE")
        ve(lambda h, m=m, b=b: h.tensor_scalar(out=shalf[:, m:m + 1], in0=b[0:128, 0:1],
                                               scalar1=0.5, scalar2=None,
                                               op0=OP.mult, op1=OP.bypass))
        ve(lambda h, m=m, b=b: h.tensor_scalar(out=nshalf[:, m:m + 1], in0=b[0:128, 0:1],
                                               scalar1=-0.5, scalar2=None,
                                               op0=OP.mult, op1=OP.bypass))
        done_bank(6, ["VE"])

    def s_emit_load(i):
        m, off = s_plan[i]
        slot = i % 2
        if i >= 2:
            p.wait("gpsimd", "AC", ac_val[i - 2])
        dma(sw_b[slot][:], s_srcs[m][off:off + SUNIT].rearrange("(p n) -> p n", p=128),
            sem=f"SWL{slot}")
        swl_v[i] = p.val(f"SWL{slot}")

    def s_emit_abs(i):
        m, off = s_plan[i]
        slot = i % 2
        u = i - m * NPM
        p.wait("scalar", f"SWL{slot}", swl_v[i])
        ac(lambda h, sl=slot, c=m * NPM + u: h.activation(
            out=sw_b[sl][:], in_=sw_b[sl][:], func=AF.Abs,
            accum_out=scol[:, c:c + 1]))
        ac_val[i] = p.val("AC")
        if u == NPM - 1:
            s_finalize(m)

    def s_step(k):
        for _ in range(k):
            n = len(s_plan)
            if s_ld[0] < n:
                s_emit_load(s_ld[0])
                s_ld[0] += 1
            lag = 1 if s_ld[0] < n else 0
            if s_ab[0] < s_ld[0] - lag:
                s_emit_abs(s_ab[0])
                s_ab[0] += 1

    for i in range(3):
        p.sem(f"WBL{i}")
    wb_n = [0]
    wb_free = [None, None, None]

    def load_wcol(src_col):
        n = wb_n[0]; wb_n[0] += 1
        slot = n % 3
        if wb_free[slot] is not None:
            p.wait("gpsimd", wb_free[slot][0], wb_free[slot][1])
        dma(wbA[slot][:], src_col.rearrange("p (j c) -> p j c", j=16), sem=f"WBL{slot}")
        return slot, p.val(f"WBL{slot}")

    def project(w_src, n_oc, evac):
        pend = [load_wcol(w_src[i * 128:(i + 1) * 128, :])
                for i in range(min(3, n_oc))]
        for oc in range(n_oc):
            slot, wblv = pend.pop(0)
            p.wait("tensor", f"WBL{slot}", wblv)
            bi = oc % 4
            bank = take_bank(bi)
            for j in range(16):
                pe(lambda h, sl=slot, j=j, b=bank, st=(j == 0): h.matmul(
                    out=b[:, 0:512], lhsT=wbA[sl][:, j, :], rhs=xlnT[:, j, :],
                    start=st, stop=(j == 15)))
            wb_free[slot] = ("PE", p.val("PE"))
            if oc + 3 < n_oc:
                pend.append(load_wcol(w_src[(oc + 3) * 128:(oc + 4) * 128, :]))
            evac(oc, bi, bank)
            s_step(1)

    def attn_group(gi, own, q_fm=None, gscr=None, gsilu=None, P_t=None, attn_o=None,
                   xpre_b=None, upto="full"):
        p.barrier(f"AG{int(own)}_{gi}")
        if own:
            src = lambda ci: x_res[:, gi * CPG + ci, :]
        else:
            for half in range(2):
                dma(xpre_b[:, half, :, :],
                    x_pre.rearrange("(c p) f -> p c f", p=128)
                    [:, gi * CPG + half * 2:gi * CPG + half * 2 + 2, :])
            p.wait("vector", "LD")
            p.wait("scalar", "LD")
            src = lambda ci: xpre_b[:, ci // 2, ci % 2, :]

        ln_stats(src, st4, CPG)
        p.wait("vector", "AC")
        for ci in range(CPG):
            sl = ci % 2
            if ci >= 2:
                p.wait("vector", "TR", p.val("TR"))
            ve(lambda h, ci=ci, sl=sl: h.tensor_scalar(
                out=xln_b[:, sl, :], in0=src(ci),
                scalar1=st4["m"][:, ci:ci + 1], scalar2=st4["r"][:, ci:ci + 1],
                op0=OP.subtract, op1=OP.mult))
            p.wait("sync", "VE")
            dmat(xlnT[:, :, ci * CH:(ci + 1) * CH], xln_b[:, sl, :])
        p.wait("tensor", "TR")

        b4 = take_bank(4)
        for j in range(16):
            pe(lambda h, j=j, b=b4, st=(j == 0): h.matmul(
                out=b[0:16, 0:512], lhsT=wkg1_t[:, j, :], rhs=xlnT[:, j, :],
                start=st, stop=(j == 15)))
        p.wait("vector", "PE")
        ve(lambda h, b=b4: h.tensor_copy(out=t1_sb[:], in_=b[0:16, 0:512]))
        done_bank(4, ["VE"])
        p.wait("tensor", "VE")
        for hh in range(H):
            bi = hh % 2
            bank = take_bank(bi)
            pe(lambda h, hh=hh, b=bank: h.matmul(
                out=b[:, 0:512], lhsT=wkg2_t[0:16, hh * 128:(hh + 1) * 128],
                rhs=t1_sb[:], start=True, stop=True))
            sl = hh % 2
            p.wait("scalar", "PE")
            if hh >= 2:
                p.wait("scalar", "AC", p.val("AC"))
            ac(lambda h, hh=hh, b=bank, sl=sl: h.activation(
                out=pool4[:, sl, :], in_=b[:, 0:512], func=AF.Exp, scale=-1.0,
                bias=bkgn_c[:, hh:hh + 1]))
            p.wait("scalar", "AC")
            ac(lambda h, hh=hh, sl=sl: h.activation(
                out=gkln[:, hh, :], in_=pool4[:, sl, :], func=AF.Ln, bias=1.0))
            done_bank(bi, ["AC"])

        def evac_k(oc, bi, bank):
            p.wait("scalar", "PE")
            ac(lambda h, oc=oc, b=bank: h.activation(out=k_fm[:, oc, :], in_=b[:, 0:512],
                                                     func=AF.Copy))
            done_bank(bi, ["AC"])

        project(wk, 8, evac_k)

        def evac_v(oc, bi, bank):
            sl = oc % 2
            p.wait("vector", "PE")
            if oc < 2:
                p.wait("vector", "AC")
            if oc >= 2:
                p.wait("vector", "TR", p.val("TR"))
            ve(lambda h, b=bank, sl=sl: h.tensor_copy(out=pool4[:, sl, :], in_=b[:, 0:512]))
            done_bank(bi, ["VE"])
            p.wait("sync", "VE")
            dmat(v_tm[:, :, oc * 128:(oc + 1) * 128], pool4[:, sl, :])

        project(wv, 16, evac_v)

        if own:
            def evac_q(oc, bi, bank):
                p.wait("scalar", "PE")
                ac(lambda h, oc=oc, b=bank: h.activation(out=q_fm[:, oc, :], in_=b[:, 0:512],
                                                         func=AF.Copy))
                done_bank(bi, ["AC"])

            project(wq, 8, evac_q)

            def evac_g(oc, bi, bank):
                sl = oc % 2
                p.wait("scalar", "PE")
                if oc >= 2:
                    p.wait("scalar", "VE", p.val("VE"))
                p.wait("vector", "PE")
                ve(lambda h, oc=oc, b=bank, sl=sl: h.tensor_scalar(
                    out=gscr[:, sl, :], in0=b[:, 0:512],
                    scalar1=bg_c[:, oc:oc + 1], scalar2=None, op0=OP.add, op1=OP.bypass))
                p.wait("scalar", "VE")
                ac(lambda h, oc=oc, b=bank, sl=sl: h.activation(
                    out=gscr[:, 2 + sl, :], in_=b[:, 0:512], func=AF.Sigmoid,
                    bias=bg_c[:, oc:oc + 1]))
                done_bank(bi, ["AC", "VE"])
                p.wait("vector", "AC")
                p.wait("vector", "VE")
                ve(lambda h, oc=oc, sl=sl: h.tensor_tensor(
                    out=gsilu[:, oc, :], in0=gscr[:, sl, :], in1=gscr[:, 2 + sl, :],
                    op=OP.mult))

            project(wg, 16, evac_g)
        if upto == "proj":
            return

        p.wait("tensor", "TR")
        o_bf2 = [o_bf, o_bfB]
        prep_ve = [None] * CPG
        prep_ac = [None] * CPG
        body_pe = [None] * CPG
        body_ac = [None] * CPG
        tail_tr = [None] * CPG
        tail_ve = [None] * CPG

        def chunk_prep(ci):
            if ci > 0:
                p.wait("vector", "AC", prep_ac[ci - 1])
            else:
                p.wait("vector", "AC")
                p.wait("vector", "PE")
            for hh in range(H):
                ve(lambda h, hh=hh, ci=ci: h.tensor_tensor_scan(
                    out=gkc_c[:, hh, :], data0=gkln[:, hh, ci * CH:(ci + 1) * CH],
                    data1=gkln[:, hh, ci * CH:(ci + 1) * CH], initial=0.0,
                    op0=OP.add, op1=OP.bypass))
            p.wait("scalar", "VE")
            ac(lambda h: h.activation(out=e1_c[:].rearrange("p h c -> p (h c)"),
                                      in_=gkc_c[:].rearrange("p h c -> p (h c)"),
                                      func=AF.Exp, scale=-1.0 / 16.0))
            ac(lambda h: h.activation(out=e3_c[:].rearrange("p h c -> p (h c)"),
                                      in_=gkc_c[:].rearrange("p h c -> p (h c)"),
                                      func=AF.Exp, scale=1.0 / 16.0, bias=-LN16))
            ac(lambda h, ci=ci: h.activation(
                out=dl_c[:, ci * H:(ci + 1) * H],
                in_=gkc_c[:, :, 127:128].rearrange("p h c -> p (h c)"),
                func=AF.Exp, scale=-1.0 / 16.0))
            prep_ac[ci] = p.val("AC")
            p.wait("vector", "AC", prep_ac[ci])
            if ci > 0:
                p.wait("vector", "PE", body_pe[ci - 1])
            if own:
                ve(lambda h, ci=ci: h.tensor_tensor(
                    out=qd_t[:], in0=q_fm[:, :, ci * CH:(ci + 1) * CH],
                    in1=e1_c[:], op=OP.mult))
            ve(lambda h, ci=ci: h.tensor_tensor(
                out=ke3_t[:], in0=k_fm[:, :, ci * CH:(ci + 1) * CH],
                in1=e3_c[:], op=OP.mult))
            prep_ve[ci] = p.val("VE")

        def chunk_body(ci):
            ob = o_bf2[ci % 2]
            p.wait("tensor", "VE", prep_ve[ci])
            cp_ve = [None] * H
            for hh in range(H):
                half = hh % 2
                if hh >= 1:
                    p.wait("tensor", "VE", cp_ve[hh - 1])
                pe(lambda h, hh=hh, half=half: h.transpose(
                    out=psT[:, half * 128:(half + 1) * 128], in_=ke3_t[:, hh, :],
                    identity=idb_t[:]))
                p.wait("vector", "PE")
                ve(lambda h, hh=hh, half=half: h.tensor_copy(
                    out=ke3m[:, hh, :], in_=psT[:, half * 128:(half + 1) * 128]))
                cp_ve[hh] = p.val("VE")
            for hh in range(H):
                bi = 4 + (hh % 2)
                if own:
                    bank = take_bank(bi)
                    pe(lambda h, hh=hh, b=bank: h.matmul(
                        out=b[:, 0:128], lhsT=ke3_t[:, hh, :], rhs=qd_t[:, hh, :],
                        start=True, stop=True))
                    p.wait("vector", "PE")
                    ve(lambda h, hh=hh, b=bank: h.tensor_tensor(
                        out=amT[:, hh, :], in0=b[:, 0:128], in1=mask_t[:], op=OP.mult))
                    amt_ve = p.val("VE")
                    done_bank(bi, ["VE"])
                    bank = take_bank(bi)
                    p.wait("tensor", "VE", amt_ve)
                    pe(lambda h, hh=hh, ci=ci, b=bank: h.matmul(
                        out=b[:, 0:256], lhsT=amT[:, hh, :],
                        rhs=v_tm[:, ci, hh * 256:(hh + 1) * 256], start=True, stop=False))
                    pe(lambda h, hh=hh, b=bank: h.matmul(
                        out=b[:, 0:256], lhsT=qd_t[:, hh, :],
                        rhs=S_bf[:, hh * 256:(hh + 1) * 256], start=False, stop=True))
                    p.wait("scalar", "PE")
                    if hh == 0 and ci >= 2:
                        p.wait("scalar", "TR", tail_tr[ci - 2])
                        p.wait("scalar", "VE", tail_ve[ci - 2])
                    ac(lambda h, hh=hh, b=bank, ob=ob: h.activation(
                        out=ob[:, hh * 256:(hh + 1) * 256], in_=b[:, 0:256],
                        func=AF.Copy))
                    done_bank(bi, ["AC"])
                bank = take_bank(bi)
                pe(lambda h, hh=hh, ci=ci, b=bank: h.matmul(
                    out=b[:, 0:256], lhsT=ke3m[:, hh, :],
                    rhs=v_tm[:, ci, hh * 256:(hh + 1) * 256], start=True, stop=True))
                p.wait("vector", "PE")
                ve(lambda h, hh=hh, b=bank: h.tensor_tensor(
                    out=tmpS[:], in0=S_f[:, hh * 256:(hh + 1) * 256], in1=b[:, 0:256],
                    op=OP.add))
                done_bank(bi, ["VE"])
                p.wait("vector", "VE")
                ve(lambda h, hh=hh, ci=ci: h.tensor_scalar(
                    out=S_f[:, hh * 256:(hh + 1) * 256], in0=tmpS[:],
                    scalar1=dl_c[:, ci * H + hh:ci * H + hh + 1], scalar2=None,
                    op0=OP.mult, op1=OP.bypass))
                p.wait("vector", "VE")
                ve(lambda h, hh=hh: h.tensor_copy(
                    out=S_bf[:, hh * 256:(hh + 1) * 256],
                    in_=S_f[:, hh * 256:(hh + 1) * 256]))
            body_pe[ci] = p.val("PE")
            body_ac[ci] = p.val("AC")

        def chunk_tail(ci):
            ob = o_bf2[ci % 2]
            p.wait("vector", "AC", body_ac[ci])
            ve(lambda h, ob=ob: h.tensor_reduce(
                out=st8["s1"][:], in_=ob[:].rearrange("p (h w) -> p h w", h=H),
                axis=AX.X, op=OP.add))
            for hh in range(H):
                p.wait("scalar", "AC")
                ac(lambda h, hh=hh, ob=ob: h.activation(
                    out=trash[:, 0:256], in_=ob[:, hh * 256:(hh + 1) * 256],
                    func=AF.Square, accum_out=st8["s2"][:, hh:hh + 1]))
            p.wait("vector", "AC")
            p.wait("vector", "VE")
            ve(lambda h: h.tensor_scalar(out=st8["m"][:], in0=st8["s1"][:],
                                         scalar1=1.0 / DV, scalar2=None,
                                         op0=OP.mult, op1=OP.bypass))
            p.wait("vector", "VE")
            ve(lambda h: h.tensor_tensor(out=st8["ms"][:], in0=st8["m"][:],
                                         in1=st8["m"][:], op=OP.mult))
            p.wait("vector", "VE")
            ve(lambda h: h.tensor_scalar(out=st8["v"][:], in0=st8["s2"][:],
                                         scalar1=1.0 / DV, scalar2=None,
                                         op0=OP.mult, op1=OP.bypass))
            p.wait("vector", "VE")
            ve(lambda h: h.tensor_tensor(out=st8["v"][:], in0=st8["v"][:],
                                         in1=st8["ms"][:], op=OP.subtract))
            p.wait("scalar", "VE")
            ac(lambda h: h.activation(out=st8["r"][:], in_=st8["v"][:],
                                      func=AF.Ln, bias=EPS))
            p.wait("scalar", "AC")
            ac(lambda h: h.activation(out=st8["r"][:], in_=st8["r"][:],
                                      func=AF.Exp, scale=-0.5))
            p.wait("vector", "AC")
            for hh in range(H):
                ve(lambda h, hh=hh, ob=ob: h.tensor_scalar(
                    out=ob[:, hh * 256:(hh + 1) * 256],
                    in0=ob[:, hh * 256:(hh + 1) * 256],
                    scalar1=st8["m"][:, hh:hh + 1], scalar2=st8["r"][:, hh:hh + 1],
                    op0=OP.subtract, op1=OP.mult))
            p.wait("sync", "VE")
            p.wait("sync", "TR", p.val("TR"))
            dmat(o_fm[:].rearrange("p (j r) -> p j r", j=16), ob[:])
            tail_tr[ci] = p.val("TR")
            p.wait("vector", "TR")
            ve(lambda h, ci=ci: h.tensor_tensor(
                out=P_t[:, :, ci * CH:(ci + 1) * CH],
                in0=o_fm[:].rearrange("p (j r) -> p j r", j=16),
                in1=gsilu[:, :, ci * CH:(ci + 1) * CH], op=OP.mult))
            tail_ve[ci] = p.val("VE")

        chunk_prep(0)
        for ci in range(CPG):
            chunk_body(ci)
            if ci + 1 < CPG:
                chunk_prep(ci + 1)
            if own:
                chunk_tail(ci)
            s_step(2)

        if not own or upto == "chunks":
            return

        p.sem("WOL0"); p.sem("WOL1")
        p.wait("tensor", "VE")
        wo_bufs = [xlnT, gsilu]
        wo_pe = [None] * 4
        wov = [None] * 4

        def wo_load(b):
            buf = wo_bufs[b % 2]
            if b < 2:
                p.wait("gpsimd", "PE", p.val("PE"))
                p.wait("gpsimd", "TR", p.val("TR"))
                if b == 1:
                    p.wait("gpsimd", "VE", p.val("VE"))
            else:
                p.wait("gpsimd", "PE", wo_pe[b - 2])
            dma(buf[:], wo[b * 128:(b + 1) * 128, :].rearrange("p (j c) -> p j c", j=16),
                sem=f"WOL{b % 2}")
            wov[b] = p.val(f"WOL{b % 2}")

        wo_load(0)
        wo_load(1)
        for b in range(4):
            buf = wo_bufs[b % 2]
            p.wait("tensor", f"WOL{b % 2}", wov[b])
            for ci in range(CPG):
                bi = ci % 4
                bank = take_bank(bi)
                for j in range(16):
                    pe(lambda h, j=j, ci=ci, b2=bank, bf=buf, st=(j == 0): h.matmul(
                        out=b2[:, 0:512], lhsT=P_t[:, j, ci * CH:(ci + 1) * CH],
                        rhs=bf[:, j, :], start=st, stop=(j == 15)))
                p.wait("scalar", "PE")
                ac(lambda h, ci=ci, b=b, b2=bank: h.activation(
                    out=attn_o[:, ci, b * 512:(b + 1) * 512], in_=b2[:, 0:512],
                    func=AF.Copy))
                done_bank(bi, ["AC"])
            wo_pe[b] = p.val("PE")
            if b + 2 < 4:
                wo_load(b + 2)
            s_step(2)
        p.wait("vector", "AC")
        ln_stats(lambda ci: attn_o[:, ci, :], st4, CPG)
        p.wait("vector", "AC")
        for ci in range(CPG):
            p.wait("vector", "VE")
            ve(lambda h, ci=ci: h.tensor_scalar(
                out=xln_b[:, ci % 2, :], in0=attn_o[:, ci, :],
                scalar1=st4["m"][:, ci:ci + 1], scalar2=st4["r"][:, ci:ci + 1],
                op0=OP.subtract, op1=OP.mult))
            p.wait("vector", "VE")
            ve(lambda h, ci=ci, gi=gi: h.tensor_tensor(
                out=x_res[:, gi * CPG + ci, :], in0=x_res[:, gi * CPG + ci, :],
                in1=xln_b[:, ci % 2, :], op=OP.add))

    with contextlib.ExitStack() as pre_stack:
        xpre_b = pre_stack.enter_context(nc.sbuf_tensor("xpre_b", [128, 2, 2, D], BF16))
        for gi in range(NG):
            attn_group(gi, own=False, xpre_b=xpre_b)
        p.barrier("PREDONE")
    if PHASE_LIMIT == "prefix":
        p.wait("gpsimd", "VE")
        dma(y_out.rearrange("(c p) f -> p c f", p=128), x_res[:], sem="ST")
        p.wait("gpsimd", "ST", p.val("ST"))
        own_stack.close()
        attn_stack.close()
        p.emit(top)
        top.close()
        return nc
    q_fm = OA("q_fm", [128, H, GA], BF16)
    gscr = OA("gscr", [128, 4, 512], BF16)
    gsilu = OA("gsilu", [128, 16, GA], BF16)
    P_t = OA("P_t", [128, 16, GA], BF16)
    attn_o = v_tm
    _upto = {"ownproj": "proj", "ownchunks": "chunks"}.get(PHASE_LIMIT, "full")
    for gi in range(1 if PHASE_LIMIT in ("own1", "ownproj", "ownchunks") else NG):
        attn_group(gi, own=True, q_fm=q_fm, gscr=gscr, gsilu=gsilu, P_t=P_t, attn_o=attn_o,
                   upto=_upto)
    if PHASE_LIMIT in ("own1", "ownproj", "ownchunks"):
        p.wait("gpsimd", "VE")
        dma(y_out.rearrange("(c p) f -> p c f", p=128), x_res[:], sem="ST")
        p.wait("gpsimd", "ST", p.val("ST"))
        own_stack.close()
        attn_stack.close()
        p.emit(top)
        top.close()
        return nc
    while s_ab[0] < len(s_plan):
        s_step(4)
    own_stack.close()
    attn_stack.close()
    p.barrier("ATTNDONE")
    if "xattn" in DEBUG_TAPS:
        tap_xattn = nc.dram_tensor("tap_xattn", [TOK, D], F32, kind="ExternalOutput").ap()
        dma(tap_xattn.rearrange("(c p) f -> p c f", p=128), x_res[:], sem="ST")
    if PHASE_LIMIT == "attn":
        p.wait("gpsimd", "VE")
        dma(y_out.rearrange("(c p) f -> p c f", p=128), x_res[:], sem="ST")
        p.wait("gpsimd", "ST", p.val("ST"))
        p.emit(top)
        top.close()
        return nc

    ffn_stack = contextlib.ExitStack()
    FA = lambda name, shape, dt: ffn_stack.enter_context(nc.sbuf_tensor(name, shape, dt))
    h128_t = FA("h128_t", [128, 128], BF16)
    idf_t = FA("idf_t", [128, 128], F32)
    dma(h128_t[:], h128)
    dma(idf_t[:], idf)
    p.wait("tensor", "LD")
    p.wait("vector", "LD")
    P2 = FA("P2", [128, 64, 512], FP16)
    bsp = FA("bsp", [128, 8, 512], FP16)
    rows = FA("rows", [1, 2, 512], F32)
    scale_bc = FA("scale_bc", [128, 512], FP16)
    bc_g = FA("bc_g", [128, 512], FP16)
    bc_u = FA("bc_u", [128, 512], FP16)
    wfb = [FA(f"wfb{i}", [128, 16, 128], F32) for i in range(3)]
    tern_s = [FA(f"tern_s{i}", [128, 16, 128], BF16) for i in range(3)]
    aq_s = FA("aq_s", [128, 2048], BF16)
    aq_s2 = FA("aq_s2", [128, 2048], BF16)
    aq2 = [aq_s, aq_s2]
    dfm = [FA(f"dfm{i}", [128, 512], FP16) for i in range(2)]
    tm_buf = FA("tm_buf", [128, CPG, D], FP16)
    y16 = tm_buf[:].rearrange("p c (j f) -> p (c j) f", j=4)
    fst = {n: FA("fst_" + n, [128, CPG], F32) for n in ("s1", "s2", "m", "ms", "v", "r")}
    amax4 = fst["ms"]
    g_scr = [FA(f"g_scr{i}", [128, 512], FP16) for i in range(2)]
    gs_scr = [FA("gs_scr0", [128, 512], FP16)] * 2
    up_scr = [FA(f"up_scr{i}", [128, 512], FP16) for i in range(2)]
    sg_scr = [FA("sg_scr0", [128, 512], FP16)] * 2

    for i in range(3):
        p.sem(f"WFL{i}")
    for i in range(2):
        p.sem(f"STC{i}")
    p.sem("PO")

    def w_quant(ws, ts, aslot, m, wflv, aq_guard, tern_guards):
        p.wait("scalar", f"WFL{ws}", wflv)
        for sm, vv in tern_guards:
            p.wait("scalar", sm, vv)
        if aq_guard is not None:
            p.wait("scalar", "VE", aq_guard)
        ac(lambda h, ws=ws, ts=ts, m=m: h.activation(
            out=tern_s[ts][:].rearrange("p j c -> p (j c)"),
            in_=wfb[ws][:].rearrange("p j c -> p (j c)"),
            func=AF.Sign, bias=nshalf[:, m:m + 1]))
        ac(lambda h, ws=ws, a=aslot, m=m: h.activation(
            out=aq2[a][:, 0:2048],
            in_=wfb[ws][:].rearrange("p j c -> p (j c)"),
            func=AF.Sign, bias=shalf[:, m:m + 1]))
        acv = p.val("AC")
        p.wait("vector", "AC", acv)
        ve(lambda h, ts=ts, a=aslot: h.tensor_tensor(
            out=tern_s[ts][:].rearrange("p j c -> p (j c)"),
            in0=tern_s[ts][:].rearrange("p j c -> p (j c)"),
            in1=aq2[a][:, 0:2048], op=OP.add))
        return acv, p.val("VE")

    def fwht_stage2(tiles, nbits, wave=8):
        p.wait("vector", "VE")
        p.wait("vector", "PE")
        p.wait("vector", "AC")
        p.wait("gpsimd", "VE")
        p.wait("gpsimd", "PE")
        p.wait("gpsimd", "AC")
        n = len(tiles) - 8
        idx = list(range(n))
        spares = list(range(n, n + 8))
        for bit in range(nbits):
            pairs = [(a, a | (1 << bit)) for a in range(n) if not (a & (1 << bit))]
            for w0 in range(0, len(pairs), wave):
                wv = pairs[w0:w0 + wave]
                npool = 2 if len(wv) >= 8 else 0
                for wi, (a, b) in enumerate(wv):
                    fn = lambda h, a=idx[a], b=idx[b], s=spares[wi]: h.tensor_tensor(
                        out=tiles[s], in0=tiles[a], in1=tiles[b], op=OP.add)
                    if wi < npool:
                        p.op("gpsimd", fn, [("PO", 1)])
                    else:
                        ve(fn)
                p.wait("vector", "VE")
                p.wait("gpsimd", "PO")
                for wi, (a, b) in enumerate(wv):
                    fn = lambda h, a=idx[a], b=idx[b]: h.tensor_tensor(
                        out=tiles[b], in0=tiles[a], in1=tiles[b], op=OP.subtract)
                    if wi < npool:
                        p.op("gpsimd", fn, [("PO", 1)])
                    else:
                        ve(fn)
                p.wait("vector", "VE")
                p.wait("vector", "PO")
                p.wait("gpsimd", "PO")
                p.wait("gpsimd", "VE")
                for wi, (a, b) in enumerate(wv):
                    freed = idx[a]
                    idx[a] = spares[wi]
                    spares[wi] = freed
        return idx, spares

    def absmax_scale(tiles, idx, spares, n, cnorm, rowslot):
        p.wait("vector", "PO")
        nm = 8
        for i in range(nm):
            ve(lambda h, a=idx[i], s=spares[i]: h.tensor_scalar(
                out=tiles[s], in0=tiles[a], scalar1=0.0, scalar2=None,
                op0=OP.min, op1=OP.bypass))
        p.wait("vector", "VE")
        for i in range(nm):
            ve(lambda h, a=idx[i], s=spares[i]: h.scalar_tensor_tensor(
                out=tiles[s], in0=tiles[a], scalar=0.0, in1=tiles[s],
                op0=OP.max, op1=OP.subtract))
        p.wait("vector", "VE")
        pos = nm
        while pos < n:
            for i in range(nm):
                if pos + i < n:
                    ve(lambda h, s=spares[i], a=idx[pos + i]: h.scalar_tensor_tensor(
                        out=tiles[s], in0=tiles[a], scalar=-1.0, in1=tiles[s],
                        op0=OP.mult, op1=OP.max))
            p.wait("vector", "VE")
            for i in range(nm):
                if pos + i < n:
                    ve(lambda h, s=spares[i], a=idx[pos + i]: h.scalar_tensor_tensor(
                        out=tiles[s], in0=tiles[a], scalar=1.0, in1=tiles[s],
                        op0=OP.mult, op1=OP.max))
            p.wait("vector", "VE")
            pos += nm
        m = nm
        while m > 1:
            for i in range(m // 2):
                ve(lambda h, s=spares[i], a=spares[i + m // 2]: h.tensor_tensor(
                    out=tiles[s], in0=tiles[s], in1=tiles[a], op=OP.max))
            p.wait("vector", "VE")
            m //= 2
        p.wait("sync", "VE")
        p.wait("sync", "TR", p.val("TR"))
        mt_v = tiles[spares[1]].rearrange("p (j r) -> p j r", j=4)
        dmat(mt_v, tiles[spares[0]])
        p.wait("vector", "TR")
        ve(lambda h, mt_v=mt_v: h.tensor_reduce(out=amax4[:], in_=mt_v, axis=AX.X, op=OP.max))
        for c in range(CPG):
            b = take_bank(6)
            p.wait("tensor", "VE")
            pe(lambda h, c=c, b=b: h.transpose(out=b[0:1, 0:128], in_=amax4[:, c:c + 1],
                                               identity=idf_t[:]))
            p.wait("vector", "PE")
            ve(lambda h, c=c, b=b, rs=rowslot: h.tensor_scalar(
                out=rows[0:1, rs, c * 128:(c + 1) * 128], in0=b[0:1, 0:128],
                scalar1=cnorm, scalar2=EPS, op0=OP.mult, op1=OP.max))
            done_bank(6, ["VE"])
        p.wait("vector", "VE")
        ve(lambda h, rs=rowslot: h.reciprocal(out=srow_rec[0:1, :], in_=rows[0:1, rs, :]))
        p.wait("vector", "VE")
        ve(lambda h, cn=cnorm: h.tensor_scalar(out=srow_rec[0:1, :], in0=srow_rec[0:1, :],
                                               scalar1=127.0 * cn, scalar2=None,
                                               op0=OP.mult, op1=OP.bypass))
        p.wait("tensor", "VE")
        b = take_bank(6)
        pe(lambda h, b=b: h.matmul(out=b[:, 0:512], lhsT=onesf_t[:], rhs=srow_rec[0:1, :],
                                   start=True, stop=True))
        p.wait("vector", "PE")
        ve(lambda h, b=b: h.tensor_copy(out=scale_bc[:], in_=b[:, 0:512]))
        done_bank(6, ["VE"])

    srow_rec = rows[:, 1, :]
    qsc = {
        "t": [FA(f"qt{i}", [128, 512], F32) for i in range(2)],
        "p": [FA(f"qp{i}", [128, 512], BF16) for i in range(2)],
        "a": [FA(f"qa{i}", [128, 512], BF16) for i in range(2)],
        "v": [FA(f"qv{i}", [128, 512], BF16) for i in range(2)],
    }

    def quant_tiles(tiles, idx, n, marks=None):
        for w0 in range(0, n, 2):
            wv = list(range(w0, min(w0 + 2, n)))
            p.wait("vector", "VE")
            p.wait("vector", "AC")
            for wi, i in enumerate(wv):
                ve(lambda h, a=idx[i], wi=wi: h.tensor_tensor(
                    out=qsc["t"][wi][:], in0=tiles[a], in1=scale_bc[:], op=OP.mult))
            p.wait("scalar", "VE")
            for wi, i in enumerate(wv):
                ac(lambda h, wi=wi: h.activation(out=qsc["a"][wi][:], in_=qsc["t"][wi][:],
                                                 func=AF.Copy, scale=S29, bias=C23))
            p.wait("scalar", "AC")
            for wi, i in enumerate(wv):
                ac(lambda h, wi=wi: h.activation(out=qsc["t"][wi][:], in_=qsc["t"][wi][:],
                                                 func=AF.Copy, bias=C23))
            p.wait("vector", "AC")
            for wi, i in enumerate(wv):
                ve(lambda h, wi=wi: h.tensor_scalar(
                    out=qsc["v"][wi][:], in0=qsc["a"][wi][:], scalar1=-C23, scalar2=B29,
                    op0=OP.add, op1=OP.mult))
                ve(lambda h, wi=wi: h.tensor_scalar(
                    out=qsc["p"][wi][:], in0=qsc["t"][wi][:], scalar1=yNY_c[:],
                    scalar2=col2_c[:], op0=OP.mult, op1=OP.subtract))
            p.wait("vector", "VE")
            for wi, i in enumerate(wv):
                ve(lambda h, a=idx[i], wi=wi: h.tensor_tensor(
                    out=tiles[a], in0=qsc["p"][wi][:], in1=qsc["v"][wi][:], op=OP.add))
            if marks is not None and (w0 + 2) % 16 == 0:
                marks[(w0 + 2) // 16 - 1] = p.val("VE")
        p.wait("vector", "VE")

    def ffn_group(gi, first):
        p.barrier(f"FG{gi}")
        cbase = gi * CPG
        src = lambda ci: x_res[:, cbase + ci, :]
        ln_stats(src, fst, CPG)
        p.wait("vector", "AC")
        for ci in range(CPG):
            p.wait("vector", "VE")
            ve(lambda h, ci=ci: h.tensor_scalar(
                out=tm_buf[:, ci, :], in0=src(ci),
                scalar1=fst["m"][:, ci:ci + 1], scalar2=fst["r"][:, ci:ci + 1],
                op0=OP.subtract, op1=OP.mult))
            p.wait("sync", "VE")
            dmat(xlnT_f[:, :, ci * CH:(ci + 1) * CH], tm_buf[:, ci, :])
        p.wait("tensor", "TR")
        p.wait("scalar", "TR")
        for j in range(16):
            bi = j % 4
            bank = take_bank(bi)
            pe(lambda h, j=j, b=bank: h.matmul(out=b[:, 0:512], lhsT=h128_t[:],
                                               rhs=xlnT_f[:, j, :], start=True, stop=True))
            p.wait("scalar", "PE")
            ac(lambda h, j=j, b=bank: h.activation(out=y16[:, j, :], in_=b[:, 0:512],
                                                   func=AF.Copy))
            done_bank(bi, ["AC"])
        tiles2 = [y16[:, j, :] for j in range(16)] + [bsp[:, i, :] for i in range(8)]
        idx2, sp2 = fwht_stage2(tiles2, 4)
        absmax_scale(tiles2, idx2, sp2, 16, CW2, 0)

        p.wait("vector", "VE")
        ve(lambda h: h.tensor_scalar(out=rows[0:1, 1, :], in0=rows[0:1, 0, :],
                                     scalar1=s_sc[0:1, 0:1], scalar2=0.5 / 127.0,
                                     op0=OP.mult, op1=OP.mult))
        p.wait("tensor", "VE")
        b = take_bank(6)
        pe(lambda h, b=b: h.matmul(out=b[:, 0:512], lhsT=onesf_t[:], rhs=rows[0:1, 1, :],
                                   start=True, stop=True))
        p.wait("vector", "PE")
        ve(lambda h, b=b: h.tensor_copy(out=bc_g[:], in_=b[:, 0:512]))
        done_bank(6, ["VE"])
        p.wait("vector", "VE")
        ve(lambda h: h.tensor_scalar(out=rows[0:1, 1, :], in0=rows[0:1, 0, :],
                                     scalar1=s_sc[0:1, 1:2], scalar2=0.5 / 127.0,
                                     op0=OP.mult, op1=OP.mult))
        p.wait("tensor", "VE")
        b = take_bank(6)
        pe(lambda h, b=b: h.matmul(out=b[:, 0:512], lhsT=onesf_t[:], rhs=rows[0:1, 1, :],
                                   start=True, stop=True))
        p.wait("vector", "PE")
        ve(lambda h, b=b: h.tensor_copy(out=bc_u[:], in_=b[:, 0:512]))
        done_bank(6, ["VE"])
        p.wait("vector", "VE")

        NU = 128
        u_src = lambda n: (wgt if n % 2 == 0 else wut)
        u_cache = lambda n: (tern_g if n % 2 == 0 else tern_u)
        q_ve = [None] * NU
        q_ac = [None] * NU
        pe_aft = [None] * NU
        wfl_v = [None] * NU
        stc_v = [None] * NU
        gu_banks = {}

        def gu_load(n):
            if first:
                ws = n % 3
                if n >= 3:
                    p.wait("sync", "AC", q_ac[n - 3])
                src = u_src(n)[(n // 2) * 128:(n // 2 + 1) * 128, :]
                p.op("sync", lambda h, s=src, sl=ws: h.dma_start(
                    out=wfb[sl][:].rearrange("p j c -> p (j c)"), in_=s),
                    [(f"WFL{ws}", 16)])
                wfl_v[n] = p.val(f"WFL{ws}")
            else:
                ts = n % 3
                if n >= 3:
                    p.wait("sync", "PE", pe_aft[n - 3])
                src = u_cache(n)[(n // 2) * 128:(n // 2 + 1) * 128, :]
                p.op("sync", lambda h, s=src, sl=ts: h.dma_start(
                    out=tern_s[sl][:].rearrange("p j c -> p (j c)"), in_=s),
                    [(f"WFL{ts}", 16)])
                wfl_v[n] = p.val(f"WFL{ts}")

        def gu_quant(n):
            ts = n % 2
            guards = ([("PE", pe_aft[n - 2]), (f"STC{ts}", stc_v[n - 2])]
                      if n >= 2 else [])
            q_ac[n], q_ve[n] = w_quant(
                n % 3, ts, n % 2, n % 2, wfl_v[n],
                q_ve[n - 2] if n >= 2 else None, guards)

        def gu_store(n):
            ts = n % 2
            p.wait("gpsimd", "VE", q_ve[n])
            p.op("gpsimd", lambda h, sl=ts, n=n: h.dma_start(
                out=u_cache(n)[(n // 2) * 128:(n // 2 + 1) * 128, :],
                in_=tern_s[sl][:].rearrange("p j c -> p (j c)")), [(f"STC{ts}", 16)])
            stc_v[n] = p.val(f"STC{ts}")

        def gu_mm(n):
            ts = n % 2 if first else n % 3
            if first:
                p.wait("tensor", "VE", q_ve[n])
            else:
                p.wait("tensor", f"WFL{ts}", wfl_v[n])
            bi = n % 6
            bank = take_bank(bi)
            for j in range(16):
                pe(lambda h, sl=ts, j=j, b=bank, st=(j == 0): h.matmul(
                    out=b[:, 0:512], lhsT=tern_s[sl][:, j, :], rhs=tiles2[idx2[j]],
                    start=st, stop=(j == 15)))
            pe_aft[n] = p.val("PE")
            gu_banks[n] = (bi, bank)

        def gu_evac(oc):
            sl = oc % 2
            big, bg_ = gu_banks.pop(2 * oc)
            biu, bu_ = gu_banks.pop(2 * oc + 1)
            p.wait("vector", "PE", pe_aft[2 * oc])
            ve(lambda h, b=bg_, sl=sl: h.tensor_tensor(out=g_scr[sl][:], in0=b[:, 0:512],
                                                       in1=bc_g[:], op=OP.mult))
            done_bank(big, ["VE"])
            p.wait("scalar", "VE")
            ac(lambda h, sl=sl: h.activation(out=gs_scr[sl][:], in_=g_scr[sl][:],
                                             func=AF.Sigmoid))
            p.wait("vector", "PE", pe_aft[2 * oc + 1])
            ve(lambda h, b=bu_, sl=sl: h.tensor_tensor(out=up_scr[sl][:], in0=b[:, 0:512],
                                                       in1=bc_u[:], op=OP.mult))
            done_bank(biu, ["VE"])
            p.wait("vector", "AC")
            ve(lambda h, sl=sl: h.tensor_tensor(out=sg_scr[sl][:], in0=g_scr[sl][:],
                                                in1=gs_scr[sl][:], op=OP.mult))
            p.wait("vector", "VE")
            ve(lambda h, oc=oc, sl=sl: h.tensor_tensor(out=P2[:, oc, :], in0=sg_scr[sl][:],
                                                       in1=up_scr[sl][:], op=OP.mult))

        look = 3
        for i in range(look):
            gu_load(i)
        quant_tiles(tiles2, idx2, 16)
        p.wait("tensor", "VE")
        if first:
            gu_quant(0)
        for n in range(NU):
            if first and n + 1 < NU:
                gu_quant(n + 1)
            gu_mm(n)
            if first:
                gu_store(n)
            if n + look < NU:
                gu_load(n + look)
            if n % 2 == 1:
                gu_evac(n // 2)

        p.wait("tensor", "VE")
        for j in range(64):
            bi = j % 4
            bank = take_bank(bi)
            pe(lambda h, j=j, b=bank: h.matmul(out=b[:, 0:512], lhsT=h128_t[:],
                                               rhs=P2[:, j, :], start=True, stop=True))
            p.wait("scalar", "PE")
            ac(lambda h, j=j, b=bank: h.activation(out=P2[:, j, :], in_=b[:, 0:512],
                                                   func=AF.Copy))
            done_bank(bi, ["AC"])
        tiles8 = [P2[:, j, :] for j in range(64)] + [bsp[:, i, :] for i in range(8)]
        idx8, sp8 = fwht_stage2(tiles8, 6)
        absmax_scale(tiles8, idx8, sp8, 64, CW8, 0)

        p.wait("vector", "VE")
        ve(lambda h: h.tensor_scalar(out=rows[0:1, 1, :], in0=rows[0:1, 0, :],
                                     scalar1=s_sc[0:1, 2:3], scalar2=0.5 / 127.0,
                                     op0=OP.mult, op1=OP.mult))
        p.wait("tensor", "VE")
        b = take_bank(6)
        pe(lambda h, b=b: h.matmul(out=b[:, 0:512], lhsT=onesf_t[:], rhs=rows[0:1, 1, :],
                                   start=True, stop=True))
        p.wait("vector", "PE")
        ve(lambda h, b=b: h.tensor_copy(out=bc_g[:], in_=b[:, 0:512]))
        done_bank(6, ["VE"])
        p.wait("vector", "VE")
        ND = 64
        q_ve_d = [None] * ND
        q_ac_d = [None] * ND
        pe_d = [None] * ND
        wfl_d = [None] * ND
        stc_d = [None] * ND
        tr_d = [None] * 16
        dn_banks = {}
        last_gu = {0: 126, 1: 127, 2: 125}

        def dn_load(m):
            if first:
                ws = m % 3
                p.wait("scalar", "AC", q_ac_d[m - 3] if m >= 3 else q_ac[last_gu[ws]])
                src = wdt[m * 128:(m + 1) * 128, :]
                p.op("scalar", lambda h, s=src, sl=ws: h.dma_start(
                    out=wfb[sl][:].rearrange("p j c -> p (j c)"), in_=s),
                    [(f"WFL{ws}", 16)])
                wfl_d[m] = p.val(f"WFL{ws}")
            else:
                ts = m % 3
                p.wait("scalar", "PE", pe_d[m - 3] if m >= 3 else pe_aft[last_gu[ts]])
                src = tern_d[m * 128:(m + 1) * 128, :]
                p.op("scalar", lambda h, s=src, sl=ts: h.dma_start(
                    out=tern_s[sl][:].rearrange("p j c -> p (j c)"), in_=s),
                    [(f"WFL{ts}", 16)])
                wfl_d[m] = p.val(f"WFL{ts}")

        def dn_quant(m):
            ts = m % 2
            guards = [("PE", pe_d[m - 2] if m >= 2 else pe_aft[NU - 2 + m]),
                      (f"STC{ts}", stc_d[m - 2] if m >= 2 else stc_v[NU - 2 + m])]
            q_ac_d[m], q_ve_d[m] = w_quant(
                m % 3, ts, m % 2, 2, wfl_d[m],
                q_ve_d[m - 2] if m >= 2 else q_ve[NU - 2 + m], guards)

        def dn_store(m):
            slot = m % 2
            p.wait("gpsimd", "VE", q_ve_d[m])
            p.op("gpsimd", lambda h, sl=slot, m=m: h.dma_start(
                out=tern_d[m * 128:(m + 1) * 128, :],
                in_=tern_s[sl][:].rearrange("p j c -> p (j c)")), [(f"STC{slot}", 16)])
            stc_d[m] = p.val(f"STC{slot}")

        def dn_mm(m):
            slot = m % 2 if first else m % 3
            oc, qtr = m // 4, m % 4
            if first:
                p.wait("tensor", "VE", q_ve_d[m])
            else:
                p.wait("tensor", f"WFL{slot}", wfl_d[m])
                p.wait("tensor", "VE", qmark[qtr])
            bi = oc % 6
            if qtr == 0:
                dn_banks[oc] = (bi, take_bank(bi))
            bank = dn_banks[oc][1]
            for j in range(16):
                pe(lambda h, sl=slot, j=j, jg=qtr * 16 + j, b=bank,
                   st=(qtr == 0 and j == 0), sp=(qtr == 3 and j == 15): h.matmul(
                    out=b[:, 0:512], lhsT=tern_s[sl][:, j, :], rhs=tiles8[idx8[jg]],
                    start=st, stop=sp))
            pe_d[m] = p.val("PE")

        def dn_evac(oc):
            sl = oc % 2
            bi, bank = dn_banks.pop(oc)
            p.wait("vector", "PE", pe_d[4 * oc + 3])
            if oc >= 2:
                p.wait("vector", f"TRD{sl}", tr_d[oc - 2])
            ve(lambda h, b=bank, sl=sl: h.tensor_tensor(out=dfm[sl][:], in0=b[:, 0:512],
                                                        in1=bc_g[:], op=OP.mult))
            done_bank(bi, ["VE"])
            p.wait("sync", "VE")
            p.op("sync", lambda h, oc=oc, sl=sl: h.dma_start(
                out=tm_buf[:, :, oc * 128:(oc + 1) * 128], in_=dfm[sl][:],
                transpose=True), [(f"TRD{sl}", 16)])
            tr_d[oc] = p.val(f"TRD{sl}")

        look_d = 3
        for i in range(look_d):
            dn_load(i)
        qmark = [None] * 4
        quant_tiles(tiles8, idx8, 64, marks=qmark)
        if first:
            dn_quant(0)
        for m in range(ND):
            if first and m + 1 < ND:
                dn_quant(m + 1)
            dn_mm(m)
            if first:
                dn_store(m)
            if m + look_d < ND:
                dn_load(m + look_d)
            if m % 4 == 3:
                dn_evac(m // 4)
        for e in ("vector", "scalar"):
            p.wait(e, "TR")
            p.wait(e, "TRD0")
            p.wait(e, "TRD1")
        ln_stats(lambda ci: tm_buf[:, ci, :], fst, CPG)
        p.wait("vector", "AC")
        for ci in range(CPG):
            p.wait("vector", "VE")
            ve(lambda h, ci=ci: h.tensor_scalar(
                out=aq_s[:, 0:D], in0=tm_buf[:, ci, :],
                scalar1=fst["m"][:, ci:ci + 1], scalar2=fst["r"][:, ci:ci + 1],
                op0=OP.subtract, op1=OP.mult))
            p.wait("vector", "VE")
            ve(lambda h, ci=ci, cb=cbase: h.tensor_tensor(
                out=x_res[:, cb + ci, :], in0=x_res[:, cb + ci, :],
                in1=aq_s[:, 0:D], op=OP.add))
        p.wait("gpsimd", "VE")
        dma(y_out.rearrange("(c p) f -> p c f", p=128)[:, cbase:cbase + CPG, :],
            x_res[:, cbase:cbase + CPG, :], sem="ST")

    xlnT_f = P2[:, 0:16, :]
    for gi in range(TOK // GA):
        ffn_group(gi, first=(gi == 0))

    p.wait("gpsimd", "ST", p.val("ST"))
    ffn_stack.close()

    p.emit(top)
    top.close()
    return nc


_NC_CACHE = {}


def stage_shared(inp):
    f = lambda a: np.ascontiguousarray(np.asarray(a, dtype=np.float32))
    jj, ii = np.meshgrid(np.arange(128), np.arange(128), indexing="ij")
    return dict(
        wq=tile_cols(f(inp["L0_attn_Wq"])),
        wk=tile_cols(f(inp["L0_attn_Wk"])),
        wkg1=np.ascontiguousarray(
            f(inp["L0_attn_Wkg1"]).reshape(16, 128, 16).transpose(1, 0, 2).reshape(128, 256)),
        wkg2=f(inp["L0_attn_Wkg2"]),
        bkg2=np.ascontiguousarray(f(inp["L0_attn_bkg2"]).reshape(8, 128).T),
        wv=tile_cols(f(inp["L0_attn_Wv"])),
        wg=tile_cols(f(inp["L0_attn_Wg"])),
        bg=np.ascontiguousarray(f(inp["L0_attn_bg"]).reshape(16, 128).T),
        wo=np.ascontiguousarray(
            f(inp["L0_attn_Wo"]).reshape(16, 128, 4, 512).transpose(2, 1, 0, 3)
            .reshape(512, 8192)),
        wgt=tile_cols(f(inp["L0_ffn_w_gate"]).T),
        wut=tile_cols(f(inp["L0_ffn_w_up"]).T),
        wdt=np.ascontiguousarray(
            f(inp["L0_ffn_w_down"]).T.reshape(4, 16, 128, 16, 128).transpose(3, 0, 2, 1, 4)
            .reshape(8192, 2048)),
        h128=hadamard(128),
        idf=np.eye(128, dtype=np.float32),
        maskt=(jj <= ii).astype(np.float32),
        onesf=np.ones((1, 128), np.float32),
        onecol=np.ones((128, 1), np.float32),
    )


def kernel(L0_attn_x, L0_attn_Wq, L0_attn_Wk, L0_attn_Wkg1, L0_attn_Wkg2, L0_attn_bkg2,
           L0_attn_Wv, L0_attn_Wg, L0_attn_bg, L0_attn_Wo,
           L0_ffn_w_gate, L0_ffn_w_up, L0_ffn_w_down):
    x = np.ascontiguousarray(np.asarray(L0_attn_x, dtype=np.float32))
    shared = stage_shared(dict(
        L0_attn_Wq=L0_attn_Wq, L0_attn_Wk=L0_attn_Wk, L0_attn_Wkg1=L0_attn_Wkg1,
        L0_attn_Wkg2=L0_attn_Wkg2, L0_attn_bkg2=L0_attn_bkg2, L0_attn_Wv=L0_attn_Wv,
        L0_attn_Wg=L0_attn_Wg, L0_attn_bg=L0_attn_bg, L0_attn_Wo=L0_attn_Wo,
        L0_ffn_w_gate=L0_ffn_w_gate, L0_ffn_w_up=L0_ffn_w_up,
        L0_ffn_w_down=L0_ffn_w_down))
    zeros_pre = np.zeros((TOK, D), np.float32)

    if "nc" not in _NC_CACHE:
        _NC_CACHE["nc"] = build()
    nc = _NC_CACHE["nc"]

    in_maps = []
    for c in range(8):
        seq, half = c // 2, c % 2
        m = dict(shared)
        m["x_own"] = np.ascontiguousarray(x[seq, half * TOK:(half + 1) * TOK, :])
        m["x_pre"] = np.ascontiguousarray(x[seq, 0:TOK, :]) if half == 1 else zeros_pre
        in_maps.append(m)

    res = run_bass_kernel_spmd(nc, in_maps, core_ids=list(range(8)))
    out = np.empty((4, 2048, D), np.float32)
    for c in range(8):
        seq, half = c // 2, c % 2
        out[seq, half * TOK:(half + 1) * TOK, :] = res.results[c]["y"]
    return out


# version 17 (final)
# speedup vs baseline: 1.0525x; 1.0172x over previous
import contextlib
import numpy as np
import concourse.bass as bass
import concourse.mybir as mybir
from concourse.bass_utils import run_bass_kernel_spmd

F32 = mybir.dt.float32
BF16 = mybir.dt.bfloat16
FP16 = mybir.dt.float16
AF = mybir.ActivationFunctionType
OP = mybir.AluOpType
AX = mybir.AxisListType

D = 2048
H = 8
DV = 256
CH = 128
TOK = 1024
NCH = TOK // CH
GA = 512
CPG = GA // CH
NG = TOK // GA
I8 = 8192
EPS = 1e-5

C23 = float(np.float32(1.5 * 2 ** 23))
S29 = float(2.0 ** -29)
B29 = float(2.0 ** 29)
LN16 = float(np.log(16.0))
CW2 = float(1.0 / np.sqrt(2048.0))
CW8 = float(1.0 / np.sqrt(8192.0))

ENGS = ("sync", "scalar", "vector", "tensor", "gpsimd")
DEBUG_TAPS = []
PHASE_LIMIT = "full"


def hadamard(n):
    h = np.array([[1.0]], np.float32)
    while h.shape[0] < n:
        h = np.block([[h, h], [h, -h]])
    return np.ascontiguousarray(h, np.float32)


def tile_cols(w):
    di, no = w.shape
    t = w.reshape(di // 128, 128, no // 128, 128).transpose(2, 1, 0, 3).reshape(no, di)
    return np.ascontiguousarray(t)


class P:
    def __init__(self, nc):
        self.nc = nc
        self.q = {e: [] for e in ENGS}
        self.sems = {}
        self.cnt = {}
        self._names = []

    def sem(self, name):
        if name not in self.cnt:
            self._names.append(name)
            self.cnt[name] = 0
        return name

    def op(self, eng, fn, inc=None):
        incs = list(inc or [])
        for s, n in incs:
            self.sem(s)
            self.cnt[s] += n

        def run(h, fn=fn, incs=incs):
            instr = fn(h)
            for s, n in incs:
                instr = instr.then_inc(self.sems[s], n)

        self.q[eng].append(run)

    def wait(self, eng, s, val=None):
        self.sem(s)
        v = self.cnt[s] if val is None else val
        if v <= 0:
            return
        self.q[eng].append(lambda h, s=s, v=v: h.wait_ge(self.sems[s], v))

    def val(self, s):
        return self.cnt[self.sem(s)]

    def barrier(self, name, engines=ENGS):
        s = self.sem(name)
        snap = [(n, v) for n, v in self.cnt.items() if v > 0 and n != s]
        for e in engines:
            for n, v in snap:
                self.wait(e, n, v)
            self.op(e, lambda h: h.nop(), [(s, 1)])
        tot = self.cnt[s]
        for e in engines:
            self.wait(e, s, tot)

    def emit(self, stack):
        for n in self._names:
            self.sems[n] = stack.enter_context(self.nc.semaphore(n))
        block = stack.enter_context(self.nc.Block())
        for e in ENGS:
            ops = self.q[e]

            def body(h, ops=ops):
                for f in ops:
                    f(h)

            getattr(block, e)(body)


def build():
    nc = bass.Bass("TRN2", target_bir_lowering=False, debug=False)
    p = P(nc)
    din = lambda name, shape: nc.dram_tensor(name, shape, F32, kind="ExternalInput").ap()

    x_own = din("x_own", [TOK, D])
    x_pre = din("x_pre", [TOK, D])
    wq = din("wq", [1024, D])
    wk = din("wk", [1024, D])
    wkg1 = din("wkg1", [128, 16 * 16])
    wkg2 = din("wkg2", [16, 1024])
    bkg2 = din("bkg2", [128, 8])
    wv = din("wv", [D, D])
    wg = din("wg", [D, D])
    bg = din("bg", [128, 16])
    wo = din("wo", [4 * 128, 16 * 512])
    wgt = din("wgt", [I8, D])
    wut = din("wut", [I8, D])
    wdt = din("wdt", [I8, D])
    h128 = din("h128", [128, 128])
    idf = din("idf", [128, 128])
    maskt = din("maskt", [128, 128])
    onesf = din("onesf", [1, 128])
    onecol = din("onecol", [128, 1])

    y_out = nc.dram_tensor("y", [TOK, D], F32, kind="ExternalOutput").ap()
    tern_g = nc.dram_tensor("tern_g", [64 * 128, D], BF16, kind="Internal").ap()
    tern_u = nc.dram_tensor("tern_u", [64 * 128, D], BF16, kind="Internal").ap()
    tern_d = nc.dram_tensor("tern_d", [64 * 128, D], BF16, kind="Internal").ap()

    top = contextlib.ExitStack()
    RA = lambda name, shape, dt: top.enter_context(nc.sbuf_tensor(name, shape, dt))

    x_res = RA("x_res", [128, NCH, D], BF16)
    S_f = RA("S_f", [128, H * DV], F32)
    S_bf = RA("S_bf", [128, H * DV], BF16)
    tmpS = RA("tmpS", [128, DV], F32)
    onesf_t = RA("onesf_t", [1, 128], F32)
    onecol_t = RA("onecol_t", [128, 1], F32)
    onecol_b = RA("onecol_b", [128, 1], BF16)
    bkgn_c = RA("bkgn_c", [128, 8], F32)
    bg_c = RA("bg_c", [128, 16], F32)
    yA_c = RA("yA_c", [128, 1], F32)
    yNY_c = RA("yNY_c", [128, 1], F32)
    col2_c = RA("col2_c", [128, 1], F32)
    s_sc = RA("s_sc", [1, 4], F32)
    s_stat = RA("s_stat", [1, 4], F32)
    shalf = RA("shalf", [128, 3], F32)
    nshalf = RA("nshalf", [128, 3], F32)
    trash = RA("trash", [128, D], BF16)
    cst_t = RA("cst_t", [128, 4], F32)

    ps = [top.enter_context(nc.psum_tensor(f"ps{i}", [128, 512], F32)) for i in range(7)]
    psT = top.enter_context(nc.psum_tensor("psT", [128, 1024], BF16))

    for s in ("PE", "VE", "AC", "PO", "TR", "LD", "ST"):
        p.sem(s)

    def dma(out, in_, sem="LD"):
        p.op("gpsimd", lambda h, o=out, i=in_: h.dma_start(out=o, in_=i), [(sem, 16)])

    def dmat(out, in_):
        p.op("sync", lambda h, o=out, i=in_: h.dma_start(out=o, in_=i, transpose=True), [("TR", 16)])

    def ve(fn, inc=1):
        p.op("vector", fn, [("VE", inc)])

    def ac(fn, inc=1):
        p.op("scalar", fn, [("AC", inc)])

    def pe(fn, inc=1):
        p.op("tensor", fn, [("PE", inc)])

    bank_guard = {}

    def take_bank(i):
        g = bank_guard.pop(i, None)
        if g:
            for s, v in g:
                p.wait("tensor", s, v)
        return ps[i]

    def done_bank(i, sems):
        bank_guard[i] = [(s, p.val(s)) for s in sems]

    dma(x_res[:], x_own.rearrange("(c p) f -> p c f", p=128))
    dma(onesf_t[:], onesf)
    dma(onecol_t[:], onecol)
    dma(onecol_b[:], onecol)
    dma(bkgn_c[:], bkg2)
    dma(bg_c[:], bg)
    p.wait("scalar", "LD")
    p.wait("vector", "LD")
    p.wait("tensor", "LD")

    _cvals = [EPS, 1.0, C23, -LN16]
    for _i, _v in enumerate(_cvals):
        ve(lambda h, i=_i, v=_v: h.memset(cst_t[:, i:i + 1], v))
        nc.const_aps.register(F32, float(np.float32(_v)), cst_t[:, _i:_i + 1])
        nc.const_aps.register(F32, float(_v), cst_t[:, _i:_i + 1])
    p.wait("scalar", "VE")
    ve(lambda h: h.tensor_scalar(out=bkgn_c[:], in0=bkgn_c[:], scalar1=-1.0, scalar2=None,
                                 op0=OP.mult, op1=OP.bypass))
    ac(lambda h: h.activation(out=yA_c[:], in_=onecol_t[:], func=AF.Copy, scale=S29, bias=C23))
    p.wait("vector", "AC")
    ve(lambda h: h.tensor_scalar(out=yNY_c[:], in0=yA_c[:], scalar1=-C23, scalar2=B29,
                                 op0=OP.add, op1=OP.mult))
    p.wait("vector", "VE")
    ve(lambda h: h.tensor_scalar(out=yA_c[:], in0=yNY_c[:], scalar1=-1.0, scalar2=1.0,
                                 op0=OP.mult, op1=OP.add))
    p.wait("vector", "VE")
    ve(lambda h: h.tensor_copy(out=yNY_c[:], in_=yA_c[:]))
    p.wait("vector", "VE")
    ve(lambda h: h.tensor_scalar(out=col2_c[:], in0=yNY_c[:], scalar1=C23, scalar2=None,
                                 op0=OP.mult, op1=OP.bypass))
    ve(lambda h: h.memset(S_f[:], 0.0))
    ve(lambda h: h.memset(S_bf[:], 0.0))
    p.wait("tensor", "VE")
    p.wait("gpsimd", "VE")

    def ln_stats(src, stats, n_ch, dmodel=D):
        for e in ("vector", "scalar"):
            for s in ("AC", "VE", "TR"):
                p.wait(e, s)
        for ci in range(n_ch):
            ve(lambda h, ci=ci: h.tensor_reduce(out=stats["s1"][:, ci:ci + 1],
                                                in_=src(ci), axis=AX.X, op=OP.add))
        for ci in range(n_ch):
            if ci > 0:
                p.wait("scalar", "AC")
            ac(lambda h, ci=ci: h.activation(out=trash[:, 0:dmodel], in_=src(ci),
                                             func=AF.Square,
                                             accum_out=stats["s2"][:, ci:ci + 1]))
        p.wait("vector", "AC")
        p.wait("vector", "VE")
        ve(lambda h: h.tensor_scalar(out=stats["m"][:, 0:n_ch], in0=stats["s1"][:, 0:n_ch],
                                     scalar1=1.0 / dmodel, scalar2=None,
                                     op0=OP.mult, op1=OP.bypass))
        p.wait("vector", "VE")
        ve(lambda h: h.tensor_tensor(out=stats["ms"][:, 0:n_ch], in0=stats["m"][:, 0:n_ch],
                                     in1=stats["m"][:, 0:n_ch], op=OP.mult))
        p.wait("vector", "VE")
        ve(lambda h: h.tensor_scalar(out=stats["v"][:, 0:n_ch], in0=stats["s2"][:, 0:n_ch],
                                     scalar1=1.0 / dmodel, scalar2=None,
                                     op0=OP.mult, op1=OP.bypass))
        p.wait("vector", "VE")
        ve(lambda h: h.tensor_tensor(out=stats["v"][:, 0:n_ch], in0=stats["v"][:, 0:n_ch],
                                     in1=stats["ms"][:, 0:n_ch], op=OP.subtract))
        p.wait("scalar", "VE")
        ac(lambda h: h.activation(out=stats["r"][:, 0:n_ch], in_=stats["v"][:, 0:n_ch],
                                  func=AF.Ln, bias=EPS))
        p.wait("scalar", "AC")
        ac(lambda h: h.activation(out=stats["r"][:, 0:n_ch], in_=stats["r"][:, 0:n_ch],
                                  func=AF.Exp, scale=-0.5))

    if PHASE_LIMIT == "init":
        p.wait("gpsimd", "VE")
        dma(y_out.rearrange("(c p) f -> p c f", p=128), x_res[:], sem="ST")
        p.wait("gpsimd", "ST", p.val("ST"))
        p.emit(top)
        top.close()
        return nc

    attn_stack = contextlib.ExitStack()
    AA = lambda name, shape, dt: attn_stack.enter_context(nc.sbuf_tensor(name, shape, dt))
    idb_t = AA("idb_t", [128, 128], BF16)
    mask_t = AA("mask_t", [128, 128], BF16)
    wkg1_t = AA("wkg1_t", [128, 16, 16], BF16)
    wkg2_t = AA("wkg2_t", [16, 1024], BF16)
    dma(idb_t[:], idf)
    dma(mask_t[:], maskt)
    dma(wkg1_t[:], wkg1.rearrange("p (j c) -> p j c", j=16))
    dma(wkg2_t[:], wkg2)
    p.wait("tensor", "LD")
    p.wait("vector", "LD")
    k_fm = AA("k_fm", [128, H, GA], BF16)
    gkln = AA("gkln", [128, H, GA], BF16)
    gkc_c = AA("gkc_c", [128, H, CH], F32)
    pool4 = AA("pool4", [128, 2, 512], BF16)
    e1_c = AA("e1_c", [128, H, CH], BF16)
    e3_c = AA("e3_c", [128, H, CH], BF16)
    dl_c = AA("dl_c", [128, CPG * H], F32)
    v_tm = AA("v_tm", [128, CPG, D], BF16)
    qd_t = AA("qd_t", [128, H, CH], BF16)
    ke3_t = AA("ke3_t", [128, H, CH], BF16)
    ke3m = AA("ke3m", [128, H, CH], BF16)
    amT = AA("amT", [128, H, CH], BF16)
    o_bf = AA("o_bf", [128, D], BF16)
    o_fm = AA("o_fm", [128, D], BF16)
    st4 = {n: AA("st4_" + n, [128, CPG], F32) for n in ("s1", "s2", "m", "ms", "v", "r")}
    st8 = {n: AA("st8_" + n, [128, H], F32) for n in ("s1", "s2", "m", "ms", "v", "r")}
    xlnT = AA("xlnT", [128, 16, GA], BF16)
    xln_b = AA("xln_b", [128, 2, D], BF16)
    wbA = [AA(f"wbA{i}", [128, 16, 128], BF16) for i in range(3)]
    t1_sb = AA("t1_sb", [16, GA], BF16)
    sw_b = [AA(f"sw{i}", [128, 2048], BF16) for i in range(2)]
    o_bfB = AA("o_bfB", [128, D], BF16)
    scol = AA("scol", [128, 192], F32)
    sred = AA("sred", [128, 4], F32)
    own_stack = contextlib.ExitStack()
    OA = lambda name, shape, dt: own_stack.enter_context(nc.sbuf_tensor(name, shape, dt))

    for i in range(2):
        p.sem(f"SWL{i}")
    SUNIT = 128 * 2048
    s_srcs = [wgt.rearrange("a b -> (a b)"), wut.rearrange("a b -> (a b)"),
              wdt.rearrange("a b -> (a b)")]
    NPM = (D * I8) // SUNIT
    s_plan = [(m, off) for m in range(3) for off in range(0, D * I8, SUNIT)]
    s_ld = [0]
    s_ab = [0]
    swl_v = [None] * len(s_plan)
    ac_val = [None] * len(s_plan)

    def s_finalize(m):
        p.wait("vector", "AC")
        ve(lambda h, m=m: h.tensor_reduce(out=sred[:, m:m + 1],
                                          in_=scol[:, m * NPM:(m + 1) * NPM],
                                          axis=AX.X, op=OP.add))
        p.wait("tensor", "VE")
        b = take_bank(6)
        pe(lambda h, m=m, b=b: h.matmul(out=b[0:1, 0:1], lhsT=sred[:, m:m + 1],
                                        rhs=onecol_t[:], start=True, stop=True))
        p.wait("vector", "PE")
        ve(lambda h, m=m, b=b: h.tensor_scalar(out=s_sc[0:1, m:m + 1], in0=b[0:1, 0:1],
                                               scalar1=1.0 / (D * I8), scalar2=EPS,
                                               op0=OP.mult, op1=OP.max))
        done_bank(6, ["VE"])
        p.wait("tensor", "VE")
        b = take_bank(6)
        pe(lambda h, m=m, b=b: h.matmul(out=b[0:128, 0:1], lhsT=onesf_t[:],
                                        rhs=s_sc[0:1, m:m + 1], start=True, stop=True))
        p.wait("vector", "PE")
        ve(lambda h, m=m, b=b: h.tensor_scalar(out=shalf[:, m:m + 1], in0=b[0:128, 0:1],
                                               scalar1=0.5, scalar2=None,
                                               op0=OP.mult, op1=OP.bypass))
        ve(lambda h, m=m, b=b: h.tensor_scalar(out=nshalf[:, m:m + 1], in0=b[0:128, 0:1],
                                               scalar1=-0.5, scalar2=None,
                                               op0=OP.mult, op1=OP.bypass))
        done_bank(6, ["VE"])

    def s_emit_load(i):
        m, off = s_plan[i]
        slot = i % 2
        if i >= 2:
            p.wait("gpsimd", "AC", ac_val[i - 2])
        dma(sw_b[slot][:], s_srcs[m][off:off + SUNIT].rearrange("(p n) -> p n", p=128),
            sem=f"SWL{slot}")
        swl_v[i] = p.val(f"SWL{slot}")

    def s_emit_abs(i):
        m, off = s_plan[i]
        slot = i % 2
        u = i - m * NPM
        p.wait("scalar", f"SWL{slot}", swl_v[i])
        ac(lambda h, sl=slot, c=m * NPM + u: h.activation(
            out=sw_b[sl][:], in_=sw_b[sl][:], func=AF.Abs,
            accum_out=scol[:, c:c + 1]))
        ac_val[i] = p.val("AC")
        if u == NPM - 1:
            s_finalize(m)

    def s_step(k):
        for _ in range(k):
            n = len(s_plan)
            if s_ld[0] < n:
                s_emit_load(s_ld[0])
                s_ld[0] += 1
            lag = 1 if s_ld[0] < n else 0
            if s_ab[0] < s_ld[0] - lag:
                s_emit_abs(s_ab[0])
                s_ab[0] += 1

    for i in range(3):
        p.sem(f"WBL{i}")
    wb_n = [0]
    wb_free = [None, None, None]

    def load_wcol(src_col):
        n = wb_n[0]; wb_n[0] += 1
        slot = n % 3
        if wb_free[slot] is not None:
            p.wait("gpsimd", wb_free[slot][0], wb_free[slot][1])
        dma(wbA[slot][:], src_col.rearrange("p (j c) -> p j c", j=16), sem=f"WBL{slot}")
        return slot, p.val(f"WBL{slot}")

    def project(w_src, n_oc, evac):
        pend = [load_wcol(w_src[i * 128:(i + 1) * 128, :])
                for i in range(min(3, n_oc))]
        for oc in range(n_oc):
            slot, wblv = pend.pop(0)
            p.wait("tensor", f"WBL{slot}", wblv)
            bi = oc % 4
            bank = take_bank(bi)
            for j in range(16):
                pe(lambda h, sl=slot, j=j, b=bank, st=(j == 0): h.matmul(
                    out=b[:, 0:512], lhsT=wbA[sl][:, j, :], rhs=xlnT[:, j, :],
                    start=st, stop=(j == 15)))
            wb_free[slot] = ("PE", p.val("PE"))
            if oc + 3 < n_oc:
                pend.append(load_wcol(w_src[(oc + 3) * 128:(oc + 4) * 128, :]))
            evac(oc, bi, bank)
            s_step(1)

    def attn_group(gi, own, q_fm=None, gscr=None, gsilu=None, P_t=None, attn_o=None,
                   xpre_b=None, upto="full"):
        p.barrier(f"AG{int(own)}_{gi}")
        if own:
            src = lambda ci: x_res[:, gi * CPG + ci, :]
        else:
            for half in range(2):
                dma(xpre_b[:, half, :, :],
                    x_pre.rearrange("(c p) f -> p c f", p=128)
                    [:, gi * CPG + half * 2:gi * CPG + half * 2 + 2, :])
            p.wait("vector", "LD")
            p.wait("scalar", "LD")
            src = lambda ci: xpre_b[:, ci // 2, ci % 2, :]

        ln_stats(src, st4, CPG)
        p.wait("vector", "AC")
        for ci in range(CPG):
            sl = ci % 2
            if ci >= 2:
                p.wait("vector", "TR", p.val("TR"))
            ve(lambda h, ci=ci, sl=sl: h.tensor_scalar(
                out=xln_b[:, sl, :], in0=src(ci),
                scalar1=st4["m"][:, ci:ci + 1], scalar2=st4["r"][:, ci:ci + 1],
                op0=OP.subtract, op1=OP.mult))
            p.wait("sync", "VE")
            dmat(xlnT[:, :, ci * CH:(ci + 1) * CH], xln_b[:, sl, :])
        p.wait("tensor", "TR")

        b4 = take_bank(4)
        for j in range(16):
            pe(lambda h, j=j, b=b4, st=(j == 0): h.matmul(
                out=b[0:16, 0:512], lhsT=wkg1_t[:, j, :], rhs=xlnT[:, j, :],
                start=st, stop=(j == 15)))
        p.wait("vector", "PE")
        ve(lambda h, b=b4: h.tensor_copy(out=t1_sb[:], in_=b[0:16, 0:512]))
        done_bank(4, ["VE"])
        p.wait("tensor", "VE")
        for hh in range(H):
            bi = hh % 2
            bank = take_bank(bi)
            pe(lambda h, hh=hh, b=bank: h.matmul(
                out=b[:, 0:512], lhsT=wkg2_t[0:16, hh * 128:(hh + 1) * 128],
                rhs=t1_sb[:], start=True, stop=True))
            sl = hh % 2
            p.wait("scalar", "PE")
            if hh >= 2:
                p.wait("scalar", "AC", p.val("AC"))
            ac(lambda h, hh=hh, b=bank, sl=sl: h.activation(
                out=pool4[:, sl, :], in_=b[:, 0:512], func=AF.Exp, scale=-1.0,
                bias=bkgn_c[:, hh:hh + 1]))
            p.wait("scalar", "AC")
            ac(lambda h, hh=hh, sl=sl: h.activation(
                out=gkln[:, hh, :], in_=pool4[:, sl, :], func=AF.Ln, bias=1.0))
            done_bank(bi, ["AC"])

        def evac_k(oc, bi, bank):
            p.wait("scalar", "PE")
            ac(lambda h, oc=oc, b=bank: h.activation(out=k_fm[:, oc, :], in_=b[:, 0:512],
                                                     func=AF.Copy))
            done_bank(bi, ["AC"])

        project(wk, 8, evac_k)

        def evac_v(oc, bi, bank):
            sl = oc % 2
            p.wait("vector", "PE")
            if oc < 2:
                p.wait("vector", "AC")
            if oc >= 2:
                p.wait("vector", "TR", p.val("TR"))
            ve(lambda h, b=bank, sl=sl: h.tensor_copy(out=pool4[:, sl, :], in_=b[:, 0:512]))
            done_bank(bi, ["VE"])
            p.wait("sync", "VE")
            dmat(v_tm[:, :, oc * 128:(oc + 1) * 128], pool4[:, sl, :])

        project(wv, 16, evac_v)

        if own:
            def evac_q(oc, bi, bank):
                p.wait("scalar", "PE")
                ac(lambda h, oc=oc, b=bank: h.activation(out=q_fm[:, oc, :], in_=b[:, 0:512],
                                                         func=AF.Copy))
                done_bank(bi, ["AC"])

            project(wq, 8, evac_q)

            def evac_g(oc, bi, bank):
                sl = oc % 2
                p.wait("scalar", "PE")
                if oc >= 2:
                    p.wait("scalar", "VE", p.val("VE"))
                p.wait("vector", "PE")
                ve(lambda h, oc=oc, b=bank, sl=sl: h.tensor_scalar(
                    out=gscr[:, sl, :], in0=b[:, 0:512],
                    scalar1=bg_c[:, oc:oc + 1], scalar2=None, op0=OP.add, op1=OP.bypass))
                p.wait("scalar", "VE")
                ac(lambda h, oc=oc, b=bank, sl=sl: h.activation(
                    out=gscr[:, 2 + sl, :], in_=b[:, 0:512], func=AF.Sigmoid,
                    bias=bg_c[:, oc:oc + 1]))
                done_bank(bi, ["AC", "VE"])
                p.wait("vector", "AC")
                p.wait("vector", "VE")
                ve(lambda h, oc=oc, sl=sl: h.tensor_tensor(
                    out=gsilu[:, oc, :], in0=gscr[:, sl, :], in1=gscr[:, 2 + sl, :],
                    op=OP.mult))

            project(wg, 16, evac_g)
        if upto == "proj":
            return

        p.wait("tensor", "TR")
        o_bf2 = [o_bf, o_bfB]
        prep_ve = [None] * CPG
        prep_ac = [None] * CPG
        body_pe = [None] * CPG
        body_ac = [None] * CPG
        tail_tr = [None] * CPG
        tail_ve = [None] * CPG

        def chunk_prep(ci):
            if ci > 0:
                p.wait("vector", "AC", prep_ac[ci - 1])
            else:
                p.wait("vector", "AC")
                p.wait("vector", "PE")
            for hh in range(H):
                ve(lambda h, hh=hh, ci=ci: h.tensor_tensor_scan(
                    out=gkc_c[:, hh, :], data0=gkln[:, hh, ci * CH:(ci + 1) * CH],
                    data1=gkln[:, hh, ci * CH:(ci + 1) * CH], initial=0.0,
                    op0=OP.add, op1=OP.bypass))
            p.wait("scalar", "VE")
            ac(lambda h: h.activation(out=e1_c[:].rearrange("p h c -> p (h c)"),
                                      in_=gkc_c[:].rearrange("p h c -> p (h c)"),
                                      func=AF.Exp, scale=-1.0 / 16.0))
            ac(lambda h: h.activation(out=e3_c[:].rearrange("p h c -> p (h c)"),
                                      in_=gkc_c[:].rearrange("p h c -> p (h c)"),
                                      func=AF.Exp, scale=1.0 / 16.0, bias=-LN16))
            ac(lambda h, ci=ci: h.activation(
                out=dl_c[:, ci * H:(ci + 1) * H],
                in_=gkc_c[:, :, 127:128].rearrange("p h c -> p (h c)"),
                func=AF.Exp, scale=-1.0 / 16.0))
            prep_ac[ci] = p.val("AC")
            p.wait("vector", "AC", prep_ac[ci])
            if ci > 0:
                p.wait("vector", "PE", body_pe[ci - 1])
            if own:
                ve(lambda h, ci=ci: h.tensor_tensor(
                    out=qd_t[:], in0=q_fm[:, :, ci * CH:(ci + 1) * CH],
                    in1=e1_c[:], op=OP.mult))
            ve(lambda h, ci=ci: h.tensor_tensor(
                out=ke3_t[:], in0=k_fm[:, :, ci * CH:(ci + 1) * CH],
                in1=e3_c[:], op=OP.mult))
            prep_ve[ci] = p.val("VE")

        def chunk_body(ci):
            ob = o_bf2[ci % 2]
            p.wait("tensor", "VE", prep_ve[ci])
            cp_ve = [None] * H
            for hh in range(H):
                half = hh % 2
                if hh >= 1:
                    p.wait("tensor", "VE", cp_ve[hh - 1])
                pe(lambda h, hh=hh, half=half: h.transpose(
                    out=psT[:, half * 128:(half + 1) * 128], in_=ke3_t[:, hh, :],
                    identity=idb_t[:]))
                p.wait("vector", "PE")
                ve(lambda h, hh=hh, half=half: h.tensor_copy(
                    out=ke3m[:, hh, :], in_=psT[:, half * 128:(half + 1) * 128]))
                cp_ve[hh] = p.val("VE")
            for hh in range(H):
                bi = 4 + (hh % 2)
                if own:
                    bank = take_bank(bi)
                    pe(lambda h, hh=hh, b=bank: h.matmul(
                        out=b[:, 0:128], lhsT=ke3_t[:, hh, :], rhs=qd_t[:, hh, :],
                        start=True, stop=True))
                    p.wait("vector", "PE")
                    ve(lambda h, hh=hh, b=bank: h.tensor_tensor(
                        out=amT[:, hh, :], in0=b[:, 0:128], in1=mask_t[:], op=OP.mult))
                    amt_ve = p.val("VE")
                    done_bank(bi, ["VE"])
                    bank = take_bank(bi)
                    p.wait("tensor", "VE", amt_ve)
                    pe(lambda h, hh=hh, ci=ci, b=bank: h.matmul(
                        out=b[:, 0:256], lhsT=amT[:, hh, :],
                        rhs=v_tm[:, ci, hh * 256:(hh + 1) * 256], start=True, stop=False))
                    pe(lambda h, hh=hh, b=bank: h.matmul(
                        out=b[:, 0:256], lhsT=qd_t[:, hh, :],
                        rhs=S_bf[:, hh * 256:(hh + 1) * 256], start=False, stop=True))
                    p.wait("scalar", "PE")
                    if hh == 0 and ci >= 2:
                        p.wait("scalar", "TR", tail_tr[ci - 2])
                        p.wait("scalar", "VE", tail_ve[ci - 2])
                    ac(lambda h, hh=hh, b=bank, ob=ob: h.activation(
                        out=ob[:, hh * 256:(hh + 1) * 256], in_=b[:, 0:256],
                        func=AF.Copy))
                    done_bank(bi, ["AC"])
                bank = take_bank(bi)
                pe(lambda h, hh=hh, ci=ci, b=bank: h.matmul(
                    out=b[:, 0:256], lhsT=ke3m[:, hh, :],
                    rhs=v_tm[:, ci, hh * 256:(hh + 1) * 256], start=True, stop=True))
                p.wait("vector", "PE")
                ve(lambda h, hh=hh, b=bank: h.tensor_tensor(
                    out=tmpS[:], in0=S_f[:, hh * 256:(hh + 1) * 256], in1=b[:, 0:256],
                    op=OP.add))
                done_bank(bi, ["VE"])
                p.wait("vector", "VE")
                ve(lambda h, hh=hh, ci=ci: h.tensor_scalar(
                    out=S_f[:, hh * 256:(hh + 1) * 256], in0=tmpS[:],
                    scalar1=dl_c[:, ci * H + hh:ci * H + hh + 1], scalar2=None,
                    op0=OP.mult, op1=OP.bypass))
                p.wait("vector", "VE")
                ve(lambda h, hh=hh: h.tensor_copy(
                    out=S_bf[:, hh * 256:(hh + 1) * 256],
                    in_=S_f[:, hh * 256:(hh + 1) * 256]))
            body_pe[ci] = p.val("PE")
            body_ac[ci] = p.val("AC")

        def chunk_tail(ci):
            ob = o_bf2[ci % 2]
            p.wait("vector", "AC", body_ac[ci])
            ve(lambda h, ob=ob: h.tensor_reduce(
                out=st8["s1"][:], in_=ob[:].rearrange("p (h w) -> p h w", h=H),
                axis=AX.X, op=OP.add))
            for hh in range(H):
                p.wait("scalar", "AC")
                ac(lambda h, hh=hh, ob=ob: h.activation(
                    out=trash[:, 0:256], in_=ob[:, hh * 256:(hh + 1) * 256],
                    func=AF.Square, accum_out=st8["s2"][:, hh:hh + 1]))
            p.wait("vector", "AC")
            p.wait("vector", "VE")
            ve(lambda h: h.tensor_scalar(out=st8["m"][:], in0=st8["s1"][:],
                                         scalar1=1.0 / DV, scalar2=None,
                                         op0=OP.mult, op1=OP.bypass))
            p.wait("vector", "VE")
            ve(lambda h: h.tensor_tensor(out=st8["ms"][:], in0=st8["m"][:],
                                         in1=st8["m"][:], op=OP.mult))
            p.wait("vector", "VE")
            ve(lambda h: h.tensor_scalar(out=st8["v"][:], in0=st8["s2"][:],
                                         scalar1=1.0 / DV, scalar2=None,
                                         op0=OP.mult, op1=OP.bypass))
            p.wait("vector", "VE")
            ve(lambda h: h.tensor_tensor(out=st8["v"][:], in0=st8["v"][:],
                                         in1=st8["ms"][:], op=OP.subtract))
            p.wait("scalar", "VE")
            ac(lambda h: h.activation(out=st8["r"][:], in_=st8["v"][:],
                                      func=AF.Ln, bias=EPS))
            p.wait("scalar", "AC")
            ac(lambda h: h.activation(out=st8["r"][:], in_=st8["r"][:],
                                      func=AF.Exp, scale=-0.5))
            p.wait("vector", "AC")
            for hh in range(H):
                ve(lambda h, hh=hh, ob=ob: h.tensor_scalar(
                    out=ob[:, hh * 256:(hh + 1) * 256],
                    in0=ob[:, hh * 256:(hh + 1) * 256],
                    scalar1=st8["m"][:, hh:hh + 1], scalar2=st8["r"][:, hh:hh + 1],
                    op0=OP.subtract, op1=OP.mult))
            p.wait("sync", "VE")
            p.wait("sync", "TR", p.val("TR"))
            dmat(o_fm[:].rearrange("p (j r) -> p j r", j=16), ob[:])
            tail_tr[ci] = p.val("TR")
            p.wait("vector", "TR")
            ve(lambda h, ci=ci: h.tensor_tensor(
                out=P_t[:, :, ci * CH:(ci + 1) * CH],
                in0=o_fm[:].rearrange("p (j r) -> p j r", j=16),
                in1=gsilu[:, :, ci * CH:(ci + 1) * CH], op=OP.mult))
            tail_ve[ci] = p.val("VE")

        chunk_prep(0)
        for ci in range(CPG):
            chunk_body(ci)
            if ci + 1 < CPG:
                chunk_prep(ci + 1)
            if own:
                chunk_tail(ci)
            s_step(2)

        if not own or upto == "chunks":
            return

        p.sem("WOL0"); p.sem("WOL1")
        p.wait("tensor", "VE")
        wo_bufs = [xlnT, gsilu]
        wo_pe = [None] * 4
        wov = [None] * 4

        def wo_load(b):
            buf = wo_bufs[b % 2]
            if b < 2:
                p.wait("gpsimd", "PE", p.val("PE"))
                p.wait("gpsimd", "TR", p.val("TR"))
                if b == 1:
                    p.wait("gpsimd", "VE", p.val("VE"))
            else:
                p.wait("gpsimd", "PE", wo_pe[b - 2])
            dma(buf[:], wo[b * 128:(b + 1) * 128, :].rearrange("p (j c) -> p j c", j=16),
                sem=f"WOL{b % 2}")
            wov[b] = p.val(f"WOL{b % 2}")

        wo_load(0)
        wo_load(1)
        for b in range(4):
            buf = wo_bufs[b % 2]
            p.wait("tensor", f"WOL{b % 2}", wov[b])
            for ci in range(CPG):
                bi = ci % 4
                bank = take_bank(bi)
                for j in range(16):
                    pe(lambda h, j=j, ci=ci, b2=bank, bf=buf, st=(j == 0): h.matmul(
                        out=b2[:, 0:512], lhsT=P_t[:, j, ci * CH:(ci + 1) * CH],
                        rhs=bf[:, j, :], start=st, stop=(j == 15)))
                p.wait("scalar", "PE")
                ac(lambda h, ci=ci, b=b, b2=bank: h.activation(
                    out=attn_o[:, ci, b * 512:(b + 1) * 512], in_=b2[:, 0:512],
                    func=AF.Copy))
                done_bank(bi, ["AC"])
            wo_pe[b] = p.val("PE")
            if b + 2 < 4:
                wo_load(b + 2)
            s_step(2)
        p.wait("vector", "AC")
        ln_stats(lambda ci: attn_o[:, ci, :], st4, CPG)
        p.wait("vector", "AC")
        for ci in range(CPG):
            p.wait("vector", "VE")
            ve(lambda h, ci=ci: h.tensor_scalar(
                out=xln_b[:, ci % 2, :], in0=attn_o[:, ci, :],
                scalar1=st4["m"][:, ci:ci + 1], scalar2=st4["r"][:, ci:ci + 1],
                op0=OP.subtract, op1=OP.mult))
            p.wait("vector", "VE")
            ve(lambda h, ci=ci, gi=gi: h.tensor_tensor(
                out=x_res[:, gi * CPG + ci, :], in0=x_res[:, gi * CPG + ci, :],
                in1=xln_b[:, ci % 2, :], op=OP.add))

    with contextlib.ExitStack() as pre_stack:
        xpre_b = pre_stack.enter_context(nc.sbuf_tensor("xpre_b", [128, 2, 2, D], BF16))
        for gi in range(NG):
            attn_group(gi, own=False, xpre_b=xpre_b)
        p.barrier("PREDONE")
    if PHASE_LIMIT == "prefix":
        p.wait("gpsimd", "VE")
        dma(y_out.rearrange("(c p) f -> p c f", p=128), x_res[:], sem="ST")
        p.wait("gpsimd", "ST", p.val("ST"))
        own_stack.close()
        attn_stack.close()
        p.emit(top)
        top.close()
        return nc
    q_fm = OA("q_fm", [128, H, GA], BF16)
    gscr = OA("gscr", [128, 4, 512], BF16)
    gsilu = OA("gsilu", [128, 16, GA], BF16)
    P_t = OA("P_t", [128, 16, GA], BF16)
    attn_o = v_tm
    _upto = {"ownproj": "proj", "ownchunks": "chunks"}.get(PHASE_LIMIT, "full")
    for gi in range(1 if PHASE_LIMIT in ("own1", "ownproj", "ownchunks") else NG):
        attn_group(gi, own=True, q_fm=q_fm, gscr=gscr, gsilu=gsilu, P_t=P_t, attn_o=attn_o,
                   upto=_upto)
    if PHASE_LIMIT in ("own1", "ownproj", "ownchunks"):
        p.wait("gpsimd", "VE")
        dma(y_out.rearrange("(c p) f -> p c f", p=128), x_res[:], sem="ST")
        p.wait("gpsimd", "ST", p.val("ST"))
        own_stack.close()
        attn_stack.close()
        p.emit(top)
        top.close()
        return nc
    while s_ab[0] < len(s_plan):
        s_step(4)
    own_stack.close()
    attn_stack.close()
    p.barrier("ATTNDONE")
    if "xattn" in DEBUG_TAPS:
        tap_xattn = nc.dram_tensor("tap_xattn", [TOK, D], F32, kind="ExternalOutput").ap()
        dma(tap_xattn.rearrange("(c p) f -> p c f", p=128), x_res[:], sem="ST")
    if PHASE_LIMIT == "attn":
        p.wait("gpsimd", "VE")
        dma(y_out.rearrange("(c p) f -> p c f", p=128), x_res[:], sem="ST")
        p.wait("gpsimd", "ST", p.val("ST"))
        p.emit(top)
        top.close()
        return nc

    ffn_stack = contextlib.ExitStack()
    FA = lambda name, shape, dt: ffn_stack.enter_context(nc.sbuf_tensor(name, shape, dt))
    h128_t = FA("h128_t", [128, 128], BF16)
    idf_t = FA("idf_t", [128, 128], F32)
    dma(h128_t[:], h128)
    dma(idf_t[:], idf)
    p.wait("tensor", "LD")
    p.wait("vector", "LD")
    P2 = FA("P2", [128, 64, 512], FP16)
    bsp = FA("bsp", [128, 8, 512], FP16)
    rows = FA("rows", [1, 2, 512], F32)
    scale_bc = FA("scale_bc", [128, 512], FP16)
    bc_g = FA("bc_g", [128, 512], FP16)
    bc_u = FA("bc_u", [128, 512], FP16)
    wfb = [FA(f"wfb{i}", [128, 16, 128], F32) for i in range(3)]
    tern_s = [FA(f"tern_s{i}", [128, 16, 128], BF16) for i in range(3)]
    aq_s = FA("aq_s", [128, 2048], BF16)
    aq_s2 = FA("aq_s2", [128, 2048], BF16)
    aq2 = [aq_s, aq_s2]
    dfm = [FA(f"dfm{i}", [128, 512], FP16) for i in range(2)]
    tm_buf = FA("tm_buf", [128, CPG, D], FP16)
    y16 = tm_buf[:].rearrange("p c (j f) -> p (c j) f", j=4)
    fst = {n: FA("fst_" + n, [128, CPG], F32) for n in ("s1", "s2", "m", "ms", "v", "r")}
    amax4 = fst["ms"]
    g_scr = [FA(f"g_scr{i}", [128, 512], FP16) for i in range(2)]
    gs_scr = [FA("gs_scr0", [128, 512], FP16)] * 2
    up_scr = [FA(f"up_scr{i}", [128, 512], FP16) for i in range(2)]
    sg_scr = [FA("sg_scr0", [128, 512], FP16)] * 2

    for i in range(3):
        p.sem(f"WFL{i}")
    for i in range(2):
        p.sem(f"STC{i}")
    p.sem("PO")

    def w_quant(ws, ts, aslot, m, wflv, aq_guard, tern_guards):
        p.wait("scalar", f"WFL{ws}", wflv)
        for sm, vv in tern_guards:
            p.wait("scalar", sm, vv)
        if aq_guard is not None:
            p.wait("scalar", "VE", aq_guard)
        ac(lambda h, ws=ws, ts=ts, m=m: h.activation(
            out=tern_s[ts][:].rearrange("p j c -> p (j c)"),
            in_=wfb[ws][:].rearrange("p j c -> p (j c)"),
            func=AF.Sign, bias=nshalf[:, m:m + 1]))
        ac(lambda h, ws=ws, a=aslot, m=m: h.activation(
            out=aq2[a][:, 0:2048],
            in_=wfb[ws][:].rearrange("p j c -> p (j c)"),
            func=AF.Sign, bias=shalf[:, m:m + 1]))
        acv = p.val("AC")
        p.wait("vector", "AC", acv)
        ve(lambda h, ts=ts, a=aslot: h.tensor_tensor(
            out=tern_s[ts][:].rearrange("p j c -> p (j c)"),
            in0=tern_s[ts][:].rearrange("p j c -> p (j c)"),
            in1=aq2[a][:, 0:2048], op=OP.add))
        return acv, p.val("VE")

    def fwht_stage2(tiles, nbits, wave=8):
        p.wait("vector", "VE")
        p.wait("vector", "PE")
        p.wait("vector", "AC")
        p.wait("gpsimd", "VE")
        p.wait("gpsimd", "PE")
        p.wait("gpsimd", "AC")
        n = len(tiles) - 8
        idx = list(range(n))
        spares = list(range(n, n + 8))
        for bit in range(nbits):
            pairs = [(a, a | (1 << bit)) for a in range(n) if not (a & (1 << bit))]
            for w0 in range(0, len(pairs), wave):
                wv = pairs[w0:w0 + wave]
                npool = 2 if len(wv) >= 8 else 0
                for wi, (a, b) in enumerate(wv):
                    fn = lambda h, a=idx[a], b=idx[b], s=spares[wi]: h.tensor_tensor(
                        out=tiles[s], in0=tiles[a], in1=tiles[b], op=OP.add)
                    if wi < npool:
                        p.op("gpsimd", fn, [("PO", 1)])
                    else:
                        ve(fn)
                p.wait("vector", "VE")
                p.wait("gpsimd", "PO")
                for wi, (a, b) in enumerate(wv):
                    fn = lambda h, a=idx[a], b=idx[b]: h.tensor_tensor(
                        out=tiles[b], in0=tiles[a], in1=tiles[b], op=OP.subtract)
                    if wi < npool:
                        p.op("gpsimd", fn, [("PO", 1)])
                    else:
                        ve(fn)
                p.wait("vector", "VE")
                p.wait("vector", "PO")
                p.wait("gpsimd", "PO")
                p.wait("gpsimd", "VE")
                for wi, (a, b) in enumerate(wv):
                    freed = idx[a]
                    idx[a] = spares[wi]
                    spares[wi] = freed
        return idx, spares

    def absmax_scale(tiles, idx, spares, n, cnorm, rowslot):
        p.wait("vector", "PO")
        nm = 8
        for i in range(nm):
            ve(lambda h, a=idx[i], s=spares[i]: h.tensor_scalar(
                out=tiles[s], in0=tiles[a], scalar1=0.0, scalar2=None,
                op0=OP.min, op1=OP.bypass))
        p.wait("vector", "VE")
        for i in range(nm):
            ve(lambda h, a=idx[i], s=spares[i]: h.scalar_tensor_tensor(
                out=tiles[s], in0=tiles[a], scalar=0.0, in1=tiles[s],
                op0=OP.max, op1=OP.subtract))
        p.wait("vector", "VE")
        pos = nm
        while pos < n:
            for i in range(nm):
                if pos + i < n:
                    ve(lambda h, s=spares[i], a=idx[pos + i]: h.scalar_tensor_tensor(
                        out=tiles[s], in0=tiles[a], scalar=-1.0, in1=tiles[s],
                        op0=OP.mult, op1=OP.max))
            p.wait("vector", "VE")
            for i in range(nm):
                if pos + i < n:
                    ve(lambda h, s=spares[i], a=idx[pos + i]: h.scalar_tensor_tensor(
                        out=tiles[s], in0=tiles[a], scalar=1.0, in1=tiles[s],
                        op0=OP.mult, op1=OP.max))
            p.wait("vector", "VE")
            pos += nm
        m = nm
        while m > 1:
            for i in range(m // 2):
                ve(lambda h, s=spares[i], a=spares[i + m // 2]: h.tensor_tensor(
                    out=tiles[s], in0=tiles[s], in1=tiles[a], op=OP.max))
            p.wait("vector", "VE")
            m //= 2
        p.wait("sync", "VE")
        p.wait("sync", "TR", p.val("TR"))
        mt_v = tiles[spares[1]].rearrange("p (j r) -> p j r", j=4)
        dmat(mt_v, tiles[spares[0]])
        p.wait("vector", "TR")
        ve(lambda h, mt_v=mt_v: h.tensor_reduce(out=amax4[:], in_=mt_v, axis=AX.X, op=OP.max))
        for c in range(CPG):
            b = take_bank(6)
            p.wait("tensor", "VE")
            pe(lambda h, c=c, b=b: h.transpose(out=b[0:1, 0:128], in_=amax4[:, c:c + 1],
                                               identity=idf_t[:]))
            p.wait("vector", "PE")
            ve(lambda h, c=c, b=b, rs=rowslot: h.tensor_scalar(
                out=rows[0:1, rs, c * 128:(c + 1) * 128], in0=b[0:1, 0:128],
                scalar1=cnorm, scalar2=EPS, op0=OP.mult, op1=OP.max))
            done_bank(6, ["VE"])
        p.wait("vector", "VE")
        ve(lambda h, rs=rowslot: h.reciprocal(out=srow_rec[0:1, :], in_=rows[0:1, rs, :]))
        p.wait("vector", "VE")
        ve(lambda h, cn=cnorm: h.tensor_scalar(out=srow_rec[0:1, :], in0=srow_rec[0:1, :],
                                               scalar1=127.0 * cn, scalar2=None,
                                               op0=OP.mult, op1=OP.bypass))
        p.wait("tensor", "VE")
        b = take_bank(6)
        pe(lambda h, b=b: h.matmul(out=b[:, 0:512], lhsT=onesf_t[:], rhs=srow_rec[0:1, :],
                                   start=True, stop=True))
        p.wait("vector", "PE")
        ve(lambda h, b=b: h.tensor_copy(out=scale_bc[:], in_=b[:, 0:512]))
        done_bank(6, ["VE"])

    srow_rec = rows[:, 1, :]
    qsc = {
        "t": [FA(f"qt{i}", [128, 512], F32) for i in range(2)],
        "p": [FA(f"qp{i}", [128, 512], BF16) for i in range(2)],
        "a": [FA(f"qa{i}", [128, 512], BF16) for i in range(2)],
        "v": [FA(f"qv{i}", [128, 512], BF16) for i in range(2)],
    }

    def quant_tiles(tiles, idx, n, marks=None):
        for w0 in range(0, n, 2):
            wv = list(range(w0, min(w0 + 2, n)))
            p.wait("vector", "VE")
            p.wait("vector", "AC")
            for wi, i in enumerate(wv):
                ve(lambda h, a=idx[i], wi=wi: h.tensor_tensor(
                    out=qsc["t"][wi][:], in0=tiles[a], in1=scale_bc[:], op=OP.mult))
            p.wait("scalar", "VE")
            for wi, i in enumerate(wv):
                ac(lambda h, wi=wi: h.activation(out=qsc["a"][wi][:], in_=qsc["t"][wi][:],
                                                 func=AF.Copy, scale=S29, bias=C23))
            p.wait("scalar", "AC")
            for wi, i in enumerate(wv):
                ac(lambda h, wi=wi: h.activation(out=qsc["t"][wi][:], in_=qsc["t"][wi][:],
                                                 func=AF.Copy, bias=C23))
            p.wait("vector", "AC")
            for wi, i in enumerate(wv):
                ve(lambda h, wi=wi: h.tensor_scalar(
                    out=qsc["v"][wi][:], in0=qsc["a"][wi][:], scalar1=-C23, scalar2=B29,
                    op0=OP.add, op1=OP.mult))
                ve(lambda h, wi=wi: h.tensor_scalar(
                    out=qsc["p"][wi][:], in0=qsc["t"][wi][:], scalar1=yNY_c[:],
                    scalar2=col2_c[:], op0=OP.mult, op1=OP.subtract))
            p.wait("vector", "VE")
            for wi, i in enumerate(wv):
                ve(lambda h, a=idx[i], wi=wi: h.tensor_tensor(
                    out=tiles[a], in0=qsc["p"][wi][:], in1=qsc["v"][wi][:], op=OP.add))
            if marks is not None and (w0 + 2) % 16 == 0:
                marks[(w0 + 2) // 16 - 1] = p.val("VE")
        p.wait("vector", "VE")

    def ffn_group(gi, first):
        p.barrier(f"FG{gi}")
        cbase = gi * CPG
        src = lambda ci: x_res[:, cbase + ci, :]
        ln_stats(src, fst, CPG)
        p.wait("vector", "AC")
        for ci in range(CPG):
            p.wait("vector", "VE")
            ve(lambda h, ci=ci: h.tensor_scalar(
                out=tm_buf[:, ci, :], in0=src(ci),
                scalar1=fst["m"][:, ci:ci + 1], scalar2=fst["r"][:, ci:ci + 1],
                op0=OP.subtract, op1=OP.mult))
            p.wait("sync", "VE")
            dmat(xlnT_f[:, :, ci * CH:(ci + 1) * CH], tm_buf[:, ci, :])
        p.wait("tensor", "TR")
        p.wait("scalar", "TR")
        for j in range(16):
            bi = j % 4
            bank = take_bank(bi)
            pe(lambda h, j=j, b=bank: h.matmul(out=b[:, 0:512], lhsT=h128_t[:],
                                               rhs=xlnT_f[:, j, :], start=True, stop=True))
            p.wait("scalar", "PE")
            ac(lambda h, j=j, b=bank: h.activation(out=y16[:, j, :], in_=b[:, 0:512],
                                                   func=AF.Copy))
            done_bank(bi, ["AC"])
        tiles2 = [y16[:, j, :] for j in range(16)] + [bsp[:, i, :] for i in range(8)]
        idx2, sp2 = fwht_stage2(tiles2, 4)
        absmax_scale(tiles2, idx2, sp2, 16, CW2, 0)

        p.wait("vector", "VE")
        ve(lambda h: h.tensor_scalar(out=rows[0:1, 1, :], in0=rows[0:1, 0, :],
                                     scalar1=s_sc[0:1, 0:1], scalar2=0.5 / 127.0,
                                     op0=OP.mult, op1=OP.mult))
        p.wait("tensor", "VE")
        b = take_bank(6)
        pe(lambda h, b=b: h.matmul(out=b[:, 0:512], lhsT=onesf_t[:], rhs=rows[0:1, 1, :],
                                   start=True, stop=True))
        p.wait("vector", "PE")
        ve(lambda h, b=b: h.tensor_copy(out=bc_g[:], in_=b[:, 0:512]))
        done_bank(6, ["VE"])
        p.wait("vector", "VE")
        ve(lambda h: h.tensor_scalar(out=rows[0:1, 1, :], in0=rows[0:1, 0, :],
                                     scalar1=s_sc[0:1, 1:2], scalar2=0.5 / 127.0,
                                     op0=OP.mult, op1=OP.mult))
        p.wait("tensor", "VE")
        b = take_bank(6)
        pe(lambda h, b=b: h.matmul(out=b[:, 0:512], lhsT=onesf_t[:], rhs=rows[0:1, 1, :],
                                   start=True, stop=True))
        p.wait("vector", "PE")
        ve(lambda h, b=b: h.tensor_copy(out=bc_u[:], in_=b[:, 0:512]))
        done_bank(6, ["VE"])
        p.wait("vector", "VE")

        NU = 128
        u_src = lambda n: (wgt if n % 2 == 0 else wut)
        u_cache = lambda n: (tern_g if n % 2 == 0 else tern_u)
        q_ve = [None] * NU
        q_ac = [None] * NU
        pe_aft = [None] * NU
        wfl_v = [None] * NU
        stc_v = [None] * NU
        gu_banks = {}

        def gu_load(n):
            if first:
                ws = n % 3
                if n >= 3:
                    p.wait("sync", "AC", q_ac[n - 3])
                src = u_src(n)[(n // 2) * 128:(n // 2 + 1) * 128, :]
                p.op("sync", lambda h, s=src, sl=ws: h.dma_start(
                    out=wfb[sl][:].rearrange("p j c -> p (j c)"), in_=s),
                    [(f"WFL{ws}", 16)])
                wfl_v[n] = p.val(f"WFL{ws}")
            else:
                ts = n % 3
                if n >= 3:
                    p.wait("sync", "PE", pe_aft[n - 3])
                src = u_cache(n)[(n // 2) * 128:(n // 2 + 1) * 128, :]
                p.op("sync", lambda h, s=src, sl=ts: h.dma_start(
                    out=tern_s[sl][:].rearrange("p j c -> p (j c)"), in_=s),
                    [(f"WFL{ts}", 16)])
                wfl_v[n] = p.val(f"WFL{ts}")

        def gu_quant(n):
            ts = n % 2
            guards = ([("PE", pe_aft[n - 2]), (f"STC{ts}", stc_v[n - 2])]
                      if n >= 2 else [])
            q_ac[n], q_ve[n] = w_quant(
                n % 3, ts, n % 2, n % 2, wfl_v[n],
                q_ve[n - 2] if n >= 2 else None, guards)

        def gu_store(n):
            ts = n % 2
            p.wait("gpsimd", "VE", q_ve[n])
            p.op("gpsimd", lambda h, sl=ts, n=n: h.dma_start(
                out=u_cache(n)[(n // 2) * 128:(n // 2 + 1) * 128, :],
                in_=tern_s[sl][:].rearrange("p j c -> p (j c)")), [(f"STC{ts}", 16)])
            stc_v[n] = p.val(f"STC{ts}")

        def gu_mm(n):
            ts = n % 2 if first else n % 3
            if first:
                p.wait("tensor", "VE", q_ve[n])
            else:
                p.wait("tensor", f"WFL{ts}", wfl_v[n])
            bi = n % 6
            bank = take_bank(bi)
            for j in range(16):
                pe(lambda h, sl=ts, j=j, b=bank, st=(j == 0): h.matmul(
                    out=b[:, 0:512], lhsT=tern_s[sl][:, j, :], rhs=tiles2[idx2[j]],
                    start=st, stop=(j == 15)))
            pe_aft[n] = p.val("PE")
            gu_banks[n] = (bi, bank)

        def gu_evac(oc):
            sl = oc % 2
            big, bg_ = gu_banks.pop(2 * oc)
            biu, bu_ = gu_banks.pop(2 * oc + 1)
            p.wait("vector", "PE", pe_aft[2 * oc])
            ve(lambda h, b=bg_, sl=sl: h.tensor_tensor(out=g_scr[sl][:], in0=b[:, 0:512],
                                                       in1=bc_g[:], op=OP.mult))
            done_bank(big, ["VE"])
            p.wait("scalar", "VE")
            ac(lambda h, sl=sl: h.activation(out=gs_scr[sl][:], in_=g_scr[sl][:],
                                             func=AF.Sigmoid))
            p.wait("vector", "PE", pe_aft[2 * oc + 1])
            ve(lambda h, b=bu_, sl=sl: h.tensor_tensor(out=up_scr[sl][:], in0=b[:, 0:512],
                                                       in1=bc_u[:], op=OP.mult))
            done_bank(biu, ["VE"])
            p.wait("vector", "AC")
            ve(lambda h, sl=sl: h.tensor_tensor(out=sg_scr[sl][:], in0=g_scr[sl][:],
                                                in1=gs_scr[sl][:], op=OP.mult))
            p.wait("vector", "VE")
            ve(lambda h, oc=oc, sl=sl: h.tensor_tensor(out=P2[:, oc, :], in0=sg_scr[sl][:],
                                                       in1=up_scr[sl][:], op=OP.mult))

        look = 3
        for i in range(look):
            gu_load(i)
        quant_tiles(tiles2, idx2, 16)
        p.wait("tensor", "VE")
        if first:
            gu_quant(0)
        for n in range(NU):
            if first and n + 1 < NU:
                gu_quant(n + 1)
            gu_mm(n)
            if first:
                gu_store(n)
            if n + look < NU:
                gu_load(n + look)
            if n % 2 == 1:
                gu_evac(n // 2)

        p.wait("tensor", "VE")
        for j in range(64):
            bi = j % 4
            bank = take_bank(bi)
            pe(lambda h, j=j, b=bank: h.matmul(out=b[:, 0:512], lhsT=h128_t[:],
                                               rhs=P2[:, j, :], start=True, stop=True))
            p.wait("scalar", "PE")
            ac(lambda h, j=j, b=bank: h.activation(out=P2[:, j, :], in_=b[:, 0:512],
                                                   func=AF.Copy))
            done_bank(bi, ["AC"])
        tiles8 = [P2[:, j, :] for j in range(64)] + [bsp[:, i, :] for i in range(8)]
        idx8, sp8 = fwht_stage2(tiles8, 6)
        absmax_scale(tiles8, idx8, sp8, 64, CW8, 0)

        p.wait("vector", "VE")
        ve(lambda h: h.tensor_scalar(out=rows[0:1, 1, :], in0=rows[0:1, 0, :],
                                     scalar1=s_sc[0:1, 2:3], scalar2=0.5 / 127.0,
                                     op0=OP.mult, op1=OP.mult))
        p.wait("tensor", "VE")
        b = take_bank(6)
        pe(lambda h, b=b: h.matmul(out=b[:, 0:512], lhsT=onesf_t[:], rhs=rows[0:1, 1, :],
                                   start=True, stop=True))
        p.wait("vector", "PE")
        ve(lambda h, b=b: h.tensor_copy(out=bc_g[:], in_=b[:, 0:512]))
        done_bank(6, ["VE"])
        p.wait("vector", "VE")
        ND = 64
        q_ve_d = [None] * ND
        q_ac_d = [None] * ND
        pe_d = [None] * ND
        wfl_d = [None] * ND
        stc_d = [None] * ND
        tr_d = [None] * 16
        dn_banks = {}
        last_gu = {0: 126, 1: 127, 2: 125}

        dsched = []
        for blo in range(0, 16, 6):
            blk = list(range(blo, min(blo + 6, 16)))
            for qtr in range(4):
                for oc in blk:
                    dsched.append((oc, qtr))
        u_of = lambda m: dsched[m][0] * 4 + dsched[m][1]
        pos_q3 = {oc: m for m, (oc, qtr) in enumerate(dsched) if qtr == 3}

        def dn_load(m):
            u = u_of(m)
            if first:
                ws = m % 3
                p.wait("scalar", "AC", q_ac_d[m - 3] if m >= 3 else q_ac[last_gu[ws]])
                src = wdt[u * 128:(u + 1) * 128, :]
                p.op("scalar", lambda h, s=src, sl=ws: h.dma_start(
                    out=wfb[sl][:].rearrange("p j c -> p (j c)"), in_=s),
                    [(f"WFL{ws}", 16)])
                wfl_d[m] = p.val(f"WFL{ws}")
            else:
                ts = m % 3
                p.wait("scalar", "PE", pe_d[m - 3] if m >= 3 else pe_aft[last_gu[ts]])
                src = tern_d[u * 128:(u + 1) * 128, :]
                p.op("scalar", lambda h, s=src, sl=ts: h.dma_start(
                    out=tern_s[sl][:].rearrange("p j c -> p (j c)"), in_=s),
                    [(f"WFL{ts}", 16)])
                wfl_d[m] = p.val(f"WFL{ts}")

        def dn_quant(m):
            ts = m % 2
            guards = [("PE", pe_d[m - 2] if m >= 2 else pe_aft[NU - 2 + m]),
                      (f"STC{ts}", stc_d[m - 2] if m >= 2 else stc_v[NU - 2 + m])]
            q_ac_d[m], q_ve_d[m] = w_quant(
                m % 3, ts, m % 2, 2, wfl_d[m],
                q_ve_d[m - 2] if m >= 2 else q_ve[NU - 2 + m], guards)

        def dn_store(m):
            slot = m % 2
            u = u_of(m)
            p.wait("gpsimd", "VE", q_ve_d[m])
            p.op("gpsimd", lambda h, sl=slot, u=u: h.dma_start(
                out=tern_d[u * 128:(u + 1) * 128, :],
                in_=tern_s[sl][:].rearrange("p j c -> p (j c)")), [(f"STC{slot}", 16)])
            stc_d[m] = p.val(f"STC{slot}")

        def dn_mm(m):
            slot = m % 2 if first else m % 3
            oc, qtr = dsched[m]
            if first:
                p.wait("tensor", "VE", q_ve_d[m])
            else:
                p.wait("tensor", f"WFL{slot}", wfl_d[m])
                p.wait("tensor", "VE", qmark[qtr])
            bi = oc % 6
            if qtr == 0:
                dn_banks[oc] = (bi, take_bank(bi))
            bank = dn_banks[oc][1]
            for j in range(16):
                pe(lambda h, sl=slot, j=j, jg=qtr * 16 + j, b=bank,
                   st=(qtr == 0 and j == 0), sp=(qtr == 3 and j == 15): h.matmul(
                    out=b[:, 0:512], lhsT=tern_s[sl][:, j, :], rhs=tiles8[idx8[jg]],
                    start=st, stop=sp))
            pe_d[m] = p.val("PE")

        def dn_evac(oc):
            sl = oc % 2
            bi, bank = dn_banks.pop(oc)
            p.wait("vector", "PE", pe_d[pos_q3[oc]])
            if oc >= 2:
                p.wait("vector", f"TRD{sl}", tr_d[oc - 2])
            ve(lambda h, b=bank, sl=sl: h.tensor_tensor(out=dfm[sl][:], in0=b[:, 0:512],
                                                        in1=bc_g[:], op=OP.mult))
            done_bank(bi, ["VE"])
            p.wait("sync", "VE")
            p.op("sync", lambda h, oc=oc, sl=sl: h.dma_start(
                out=tm_buf[:, :, oc * 128:(oc + 1) * 128], in_=dfm[sl][:],
                transpose=True), [(f"TRD{sl}", 16)])
            tr_d[oc] = p.val(f"TRD{sl}")

        look_d = 3
        for i in range(look_d):
            dn_load(i)
        qmark = [None] * 4
        quant_tiles(tiles8, idx8, 64, marks=qmark)
        if first:
            dn_quant(0)
        for m in range(ND):
            if first and m + 1 < ND:
                dn_quant(m + 1)
            dn_mm(m)
            if first:
                dn_store(m)
            if m + look_d < ND:
                dn_load(m + look_d)
            if dsched[m][1] == 3:
                dn_evac(dsched[m][0])
        for e in ("vector", "scalar"):
            p.wait(e, "TR")
            p.wait(e, "TRD0")
            p.wait(e, "TRD1")
        ln_stats(lambda ci: tm_buf[:, ci, :], fst, CPG)
        p.wait("vector", "AC")
        for ci in range(CPG):
            p.wait("vector", "VE")
            ve(lambda h, ci=ci: h.tensor_scalar(
                out=aq_s[:, 0:D], in0=tm_buf[:, ci, :],
                scalar1=fst["m"][:, ci:ci + 1], scalar2=fst["r"][:, ci:ci + 1],
                op0=OP.subtract, op1=OP.mult))
            p.wait("vector", "VE")
            ve(lambda h, ci=ci, cb=cbase: h.tensor_tensor(
                out=x_res[:, cb + ci, :], in0=x_res[:, cb + ci, :],
                in1=aq_s[:, 0:D], op=OP.add))
        p.wait("gpsimd", "VE")
        dma(y_out.rearrange("(c p) f -> p c f", p=128)[:, cbase:cbase + CPG, :],
            x_res[:, cbase:cbase + CPG, :], sem="ST")

    xlnT_f = P2[:, 0:16, :]
    for gi in range(TOK // GA):
        ffn_group(gi, first=(gi == 0))

    p.wait("gpsimd", "ST", p.val("ST"))
    ffn_stack.close()

    p.emit(top)
    top.close()
    return nc


_NC_CACHE = {}


def stage_shared(inp):
    f = lambda a: np.ascontiguousarray(np.asarray(a, dtype=np.float32))
    jj, ii = np.meshgrid(np.arange(128), np.arange(128), indexing="ij")
    return dict(
        wq=tile_cols(f(inp["L0_attn_Wq"])),
        wk=tile_cols(f(inp["L0_attn_Wk"])),
        wkg1=np.ascontiguousarray(
            f(inp["L0_attn_Wkg1"]).reshape(16, 128, 16).transpose(1, 0, 2).reshape(128, 256)),
        wkg2=f(inp["L0_attn_Wkg2"]),
        bkg2=np.ascontiguousarray(f(inp["L0_attn_bkg2"]).reshape(8, 128).T),
        wv=tile_cols(f(inp["L0_attn_Wv"])),
        wg=tile_cols(f(inp["L0_attn_Wg"])),
        bg=np.ascontiguousarray(f(inp["L0_attn_bg"]).reshape(16, 128).T),
        wo=np.ascontiguousarray(
            f(inp["L0_attn_Wo"]).reshape(16, 128, 4, 512).transpose(2, 1, 0, 3)
            .reshape(512, 8192)),
        wgt=tile_cols(f(inp["L0_ffn_w_gate"]).T),
        wut=tile_cols(f(inp["L0_ffn_w_up"]).T),
        wdt=np.ascontiguousarray(
            f(inp["L0_ffn_w_down"]).T.reshape(4, 16, 128, 16, 128).transpose(3, 0, 2, 1, 4)
            .reshape(8192, 2048)),
        h128=hadamard(128),
        idf=np.eye(128, dtype=np.float32),
        maskt=(jj <= ii).astype(np.float32),
        onesf=np.ones((1, 128), np.float32),
        onecol=np.ones((128, 1), np.float32),
    )


def kernel(L0_attn_x, L0_attn_Wq, L0_attn_Wk, L0_attn_Wkg1, L0_attn_Wkg2, L0_attn_bkg2,
           L0_attn_Wv, L0_attn_Wg, L0_attn_bg, L0_attn_Wo,
           L0_ffn_w_gate, L0_ffn_w_up, L0_ffn_w_down):
    x = np.ascontiguousarray(np.asarray(L0_attn_x, dtype=np.float32))
    shared = stage_shared(dict(
        L0_attn_Wq=L0_attn_Wq, L0_attn_Wk=L0_attn_Wk, L0_attn_Wkg1=L0_attn_Wkg1,
        L0_attn_Wkg2=L0_attn_Wkg2, L0_attn_bkg2=L0_attn_bkg2, L0_attn_Wv=L0_attn_Wv,
        L0_attn_Wg=L0_attn_Wg, L0_attn_bg=L0_attn_bg, L0_attn_Wo=L0_attn_Wo,
        L0_ffn_w_gate=L0_ffn_w_gate, L0_ffn_w_up=L0_ffn_w_up,
        L0_ffn_w_down=L0_ffn_w_down))
    zeros_pre = np.zeros((TOK, D), np.float32)

    if "nc" not in _NC_CACHE:
        _NC_CACHE["nc"] = build()
    nc = _NC_CACHE["nc"]

    in_maps = []
    for c in range(8):
        seq, half = c // 2, c % 2
        m = dict(shared)
        m["x_own"] = np.ascontiguousarray(x[seq, half * TOK:(half + 1) * TOK, :])
        m["x_pre"] = np.ascontiguousarray(x[seq, 0:TOK, :]) if half == 1 else zeros_pre
        in_maps.append(m)

    res = run_bass_kernel_spmd(nc, in_maps, core_ids=list(range(8)))
    out = np.empty((4, 2048, D), np.float32)
    for c in range(8):
        seq, half = c // 2, c % 2
        out[seq, half * TOK:(half + 1) * TOK, :] = res.results[c]["y"]
    return out
